# version 23 (final)
# baseline (speedup 1.0000x reference)
import math
import numpy as np

import concourse.bass as bass
import concourse.mybir as mybir
from concourse.bass_utils import run_bass_kernel_spmd

F32 = mybir.dt.float32
BF16 = mybir.dt.bfloat16
AF = mybir.ActivationFunctionType
OP = mybir.AluOpType
AX = mybir.AxisListType

ENGINES = ("tensor", "vector", "scalar", "gpsimd", "sync")
DMA_QUEUES = ("sync", "gpsimd", "scalar", "vector")


def _key_of(x):
    if isinstance(x, str):
        return (x, 0, 1 << 60)
    ap = x if isinstance(x, bass.AP) else x.ap()
    t = ap.tensor
    if "PSUM" in str(getattr(t, "space", "")):
        esz = 2 if ap.dtype == BF16 else 4
        lo = ap.offset
        span = 1
        for step, count in ap.ap:
            span += abs(step) * (count - 1)
        blo = (lo * esz // 2048) * 2048
        bhi = -((-(lo + span) * esz) // 2048) * 2048
        return (t.name, blo, bhi)
    try:
        esz = 2 if ap.dtype == BF16 else 4
    except Exception:
        esz = 4
    lo = ap.offset
    span = 1
    for step, count in ap.ap:
        span += abs(step) * (count - 1)
    return (t.name, lo * esz, (lo + span) * esz)


class _Op:
    __slots__ = ("eng", "fn", "reads", "writes", "deps", "idx", "inc_sem",
                 "inc_val", "is_dma", "dma_cum", "tag", "emitted", "prod_idx",
                 "wait_on", "needs_inc", "safe", "raw")

    def __init__(self, eng, fn, reads, writes, tag):
        self.eng = eng
        self.fn = fn
        self.reads = reads
        self.writes = writes
        self.deps = []
        self.idx = -1
        self.inc_sem = None
        self.inc_val = 0
        self.is_dma = False
        self.dma_cum = 0
        self.tag = tag
        self.prod_idx = 0
        self.wait_on = []
        self.needs_inc = False
        self.safe = False


class Prog:

    def __init__(self, nc):
        self.nc = nc
        self.ops = []
        self.eng_ops = {e: [] for e in ENGINES}
        self.last_write = {}
        self.readers = {}
        self.dma_count = {q: 0 for q in DMA_QUEUES}
        self._sems = {}
        self._finalized = False

    def op(self, eng, fn, reads=(), writes=(), tag=""):
        o = _Op(eng, fn, [_key_of(r) for r in reads],
                [_key_of(w) for w in writes], tag)
        deps = []
        raw = set()
        for name, lo, hi in o.reads:
            for wlo, whi, w in self.last_write.get(name, ()):
                if wlo < hi and lo < whi:
                    deps.append(w)
                    raw.add(id(w))
        for name, lo, hi in o.writes:
            for rlo, rhi, r in self.readers.get(name, ()):
                if rlo < hi and lo < rhi:
                    deps.append(r)
            for wlo, whi, w in self.last_write.get(name, ()):
                if wlo < hi and lo < whi:
                    deps.append(w)
        seen = set()
        o.deps = [d for d in deps
                  if d is not o and id(d) not in seen and not seen.add(id(d))]
        o.raw = raw
        o.idx = len(self.ops)
        self.ops.append(o)
        self.eng_ops[eng].append(o)
        for name, lo, hi in o.writes:
            wl = self.last_write.setdefault(name, [])
            wl[:] = [(a, b, w) for (a, b, w) in wl if not (lo <= a and b <= hi)]
            wl.append((lo, hi, o))
            rl = self.readers.setdefault(name, [])
            rl[:] = [(a, b, r) for (a, b, r) in rl if not (lo <= a and b <= hi)]
        for name, lo, hi in o.reads:
            self.readers.setdefault(name, []).append((lo, hi, o))
        return o

    def dma(self, queue, out_ap, in_ap, tag=""):
        def fn(eng):
            return eng.dma_start(out=out_ap, in_=in_ap)
        o = self.op(queue, fn, reads=[in_ap], writes=[out_ap], tag=tag)
        o.is_dma = True
        self.dma_count[queue] += 1
        o.dma_cum = self.dma_count[queue]
        return o

    def mm(self, out, lhsT, rhs, start=True, stop=True, tp=None, tag=""):
        def fn(eng):
            return eng.matmul(out, lhsT, rhs, start=start, stop=stop,
                              tile_position=tp)
        return self.op("tensor", fn, reads=[lhsT, rhs], writes=[out], tag=tag)

    def transpose(self, out, in_, ident, tag=""):
        def fn(eng):
            return eng.transpose(out, in_, ident)
        return self.op("tensor", fn, reads=[in_, ident], writes=[out], tag=tag)

    def tt(self, out, a, b, alu, eng="vector", tag=""):
        def fn(e):
            return e.tensor_tensor(out, a, b, alu)
        o = self.op(eng, fn, reads=[a, b], writes=[out], tag=tag)
        o.safe = (eng == "vector" and _key_of(a) != _key_of(b))
        return o

    def ts(self, out, in_, s1, s2, op0, op1=OP.bypass, eng="vector",
           accum_out=None, tag=""):
        rd = [in_] + [s for s in (s1, s2) if isinstance(s, bass.AP)]
        wr = [out] + ([accum_out] if accum_out is not None else [])

        def fn(e):
            return e.tensor_scalar(out, in_, s1, s2, op0, op1,
                                   accum_out=accum_out)
        return self.op(eng, fn, reads=rd, writes=wr, tag=tag)

    def stt(self, out, in0, scalar, in1, op0, op1, tag=""):
        rd = [in0, in1] + ([scalar] if isinstance(scalar, bass.AP) else [])

        def fn(e):
            return e.scalar_tensor_tensor(out, in0, scalar, in1, op0, op1)
        o = self.op("vector", fn, reads=rd, writes=[out], tag=tag)
        o.safe = True
        return o

    def act(self, out, in_, func, bias=None, scale=None, accum_out=None, tag=""):
        kw = {}
        if bias is not None:
            kw["bias"] = bias
        if scale is not None:
            kw["scale"] = scale
        if accum_out is not None:
            kw["accum_out"] = accum_out
        rd = [in_] + ([bias] if isinstance(bias, bass.AP) else [])
        wr = [out] + ([accum_out] if accum_out is not None else [])

        def fn(e):
            return e.activation(out, in_, func, **kw)
        return self.op("scalar", fn, reads=rd, writes=wr, tag=tag)

    def copy(self, out, in_, eng="vector", tag=""):
        if eng == "scalar":
            def fn(e):
                return e.copy(out, in_)
        else:
            def fn(e):
                return e.tensor_copy(out, in_)
        return self.op(eng, fn, reads=[in_], writes=[out], tag=tag)

    def reduce(self, out, in_, alu, axis=AX.X, negate=None, eng="vector", tag=""):
        def fn(e):
            return e.tensor_reduce(out, in_, axis, alu, negate=negate)
        o = self.op(eng, fn, reads=[in_], writes=[out], tag=tag)
        o.safe = (eng == "vector")
        return o

    def memset(self, ap, val, eng="gpsimd", tag=""):
        def fn(e):
            return e.memset(ap, val)
        return self.op(eng, fn, reads=[], writes=[ap], tag=tag)

    def iota(self, out, pattern, base=0, channel_multiplier=0, tag=""):
        def fn(e):
            return e.iota(out, pattern, base=base,
                          channel_multiplier=channel_multiplier,
                          allow_small_or_imprecise_dtypes=True)
        return self.op("gpsimd", fn, reads=[], writes=[out], tag=tag)

    def affine_select(self, out, in_, pattern, cmp, fill, base=0,
                      channel_multiplier=0, tag=""):
        def fn(e):
            return e.affine_select(out, in_, pattern, cmp, fill, base=base,
                                   channel_multiplier=channel_multiplier)
        return self.op("gpsimd", fn, reads=[in_], writes=[out], tag=tag)

    def select(self, out, mask, on_true, on_false, tag=""):
        def fn(e):
            return e.select(out, mask, on_true, on_false)
        return self.op("vector", fn, reads=[mask, on_true, on_false],
                       writes=[out], tag=tag)

    def recip(self, out, in_, tag=""):
        def fn(e):
            return e.reciprocal(out, in_)
        return self.op("vector", fn, reads=[in_], writes=[out], tag=tag)

    def scan(self, out, d0, d1, initial, op0, op1, tag=""):
        rd = [d0, d1] + ([initial] if isinstance(initial, bass.AP) else [])

        def fn(e):
            return e.tensor_tensor_scan(out, d0, d1, initial, op0, op1)
        return self.op("vector", fn, reads=rd, writes=[out], tag=tag)

    def _finalize(self):
        assert not self._finalized
        self._finalized = True
        since_drain = {e: set() for e in ENGINES}
        last_relay = {}
        issued_dma = {q: 0 for q in DMA_QUEUES}
        dwaited = {q: 0 for q in DMA_QUEUES}
        dw = {}
        schedule = {e: [] for e in ENGINES}

        class _Relay:
            __slots__ = ("eng", "needs_inc", "prod_idx", "is_relay")

            def __init__(self, eng):
                self.eng = ("rel", eng)
                self.needs_inc = True
                self.prod_idx = 0
                self.is_relay = True

        pos = {e: 0 for e in ENGINES}
        opos = {}
        kno = {}
        know = {e: {E: 0 for E in ENGINES} for e in ENGINES}
        for o in self.ops:
            o.wait_on = []
            drain_needed = False
            my = know[o.eng]
            sdeps = []
            for d in o.deps:
                if d.is_dma:
                    q = d.eng
                    cover = issued_dma[q]
                    if o.eng == q:
                        if dw.get((o.eng, q), 0) < cover:
                            dw[(o.eng, q)] = cover
                            schedule[o.eng].append(("dwait", q, 16 * cover))
                    else:
                        lr = last_relay.get(q)
                        if lr is None or lr[0] < d.dma_cum:
                            if dw.get((q, q), 0) < cover:
                                dw[(q, q)] = cover
                                schedule[q].append(("dwait", q, 16 * cover))
                            r = _Relay(q)
                            schedule[q].append(("relay", r))
                            since_drain[q] = set()
                            last_relay[q] = (cover, r)
                        o.wait_on.append(last_relay[q][1])
                elif d.eng == o.eng:
                    sdeps.append(d)
                else:
                    d.needs_inc = True
                    o.wait_on.append(d)
                    dk = kno.get(id(d))
                    if dk is not None:
                        for E in ENGINES:
                            if dk[E] > my[E]:
                                my[E] = dk[E]
                    p = opos.get(id(d), 0)
                    if p > my[d.eng]:
                        my[d.eng] = p
            for d in sdeps:
                if o.eng == "tensor" or id(d) not in since_drain[o.eng]:
                    continue
                if id(d) in o.raw:
                    if not (d.safe and o.safe):
                        drain_needed = True
                elif opos.get(id(d), 0) > my[o.eng]:
                    drain_needed = True
            if drain_needed:
                schedule[o.eng].append(("drain",))
                since_drain[o.eng] = set()
                my[o.eng] = pos[o.eng]
            schedule[o.eng].append(("op", o))
            since_drain[o.eng].add(id(o))
            pos[o.eng] += 1
            opos[id(o)] = pos[o.eng]
            kno[id(o)] = dict(my)
            if o.is_dma:
                issued_dma[o.eng] += 1
        for e in ENGINES:
            cnt = 0
            rcnt = 0
            for item in schedule[e]:
                if item[0] == "op" and getattr(item[1], "needs_inc", False):
                    cnt += 1
                    item[1].prod_idx = cnt
                elif item[0] == "relay":
                    rcnt += 1
                    item[1].prod_idx = rcnt
        self._schedule = schedule

    def emit(self):
        self._finalize()
        nc = self.nc
        sems = {}
        ctxs = []
        for e in ENGINES:
            s = nc.semaphore(f"prod_{e}")
            sems[e] = s.__enter__()
            ctxs.append(s)
        for q in DMA_QUEUES:
            s = nc.semaphore(f"dma_{q}")
            sems[("dma", q)] = s.__enter__()
            ctxs.append(s)
            s2 = nc.semaphore(f"rel_{q}")
            sems[("rel", q)] = s2.__enter__()
            ctxs.append(s2)
        self._semctx = ctxs
        self._sems = sems

        block = nc.Block()
        block.__enter__()

        def make_body(ename):
            def body(eng):
                maxw = {}
                for item in self._schedule[ename]:
                    kind = item[0]
                    if kind == "dwait":
                        eng.wait_ge(sems[("dma", item[1])], item[2])
                    elif kind == "drain":
                        eng.drain()
                    elif kind == "relay":
                        eng.sem_inc(sems[("rel", ename)], 1)
                    elif kind == "op":
                        o = item[1]
                        for p in o.wait_on:
                            v = p.prod_idx
                            assert v > 0
                            if maxw.get(p.eng, 0) < v:
                                maxw[p.eng] = v
                                eng.wait_ge(sems[p.eng], v)
                        inst = o.fn(eng)
                        if o.is_dma:
                            inst.then_inc(sems[("dma", o.eng)], 16)
                        elif getattr(o, "needs_inc", False):
                            inst.then_inc(sems[o.eng], 1)
                if ename in DMA_QUEUES and self.dma_count[ename] > 0:
                    eng.wait_ge(sems[("dma", ename)],
                                16 * self.dma_count[ename])
            return body

        for ename in ENGINES:
            if not self._schedule[ename] and not (
                    ename in DMA_QUEUES and self.dma_count[ename]):
                continue
            getattr(block, ename)(make_body(ename))
        block.__exit__(None, None, None)


D = 1024
S = 2048
NTT = S // 128
NC = D // 128
B_CORE = 2
LN_EPS = 1e-5


class Env:

    def __init__(self, nc, prog):
        self.nc = nc
        self.P = prog
        self._stack = []
        self.dram = {}
        self.internal = set()
        self.sb = {}

    def sbuf(self, name, shape, dtype):
        if name in self.sb:
            return self.sb[name]
        cm = self.nc.sbuf_tensor(name, shape, dtype)
        h = cm.__enter__()
        self._stack.append(cm)
        self.sb[name] = h
        return h

    def psum(self, name, shape, dtype):
        if name in self.sb:
            return self.sb[name]
        cm = self.nc.psum_tensor(name, shape, dtype)
        h = cm.__enter__()
        self._stack.append(cm)
        self.sb[name] = h
        return h

    def dram_in(self, name, shape, dtype=F32):
        if name not in self.dram:
            self.dram[name] = self.nc.dram_tensor(
                name, list(shape), dtype, kind="ExternalInput").ap()
        return self.dram[name]

    def dram_out(self, name, shape, dtype=F32):
        if name not in self.dram:
            self.dram[name] = self.nc.dram_tensor(
                name, list(shape), dtype, kind="ExternalOutput").ap()
        return self.dram[name]

    def dram_tmp(self, name, shape, dtype=F32):
        if name not in self.dram:
            self.dram[name] = self.nc.dram_tensor(
                name, list(shape), dtype, kind="Internal").ap()
            self.internal.add(name)
        return self.dram[name]


def setup_common(env):
    P = env.P
    env._constf = {}
    env.cpool = env.sbuf("cpool", [128, 16], F32)

    def constf(val):
        if val not in env._constf:
            i = len(env._constf)
            ap = env.cpool.ap()[:, i:i + 1]
            P.memset(ap, float(val), tag=f"const{i}")
            env._constf[val] = ap
        return env._constf[val]
    env.constf = constf
    ident = env.sbuf("ident", [128, 128], BF16)
    P.memset(ident[:], 1.0)
    P.affine_select(ident[:], ident[:], [[1, 128]], OP.is_equal, 0.0,
                    base=0, channel_multiplier=-1)
    env.ident = ident
    ones_row = env.sbuf("ones_row", [1, 128], BF16)
    P.memset(ones_row[:], 1.0)
    env.ones_row = ones_row
    ones_row_f = env.sbuf("ones_row_f", [1, 128], F32)
    P.memset(ones_row_f[:], 1.0)
    env.ones_row_f = ones_row_f
    identf = env.sbuf("identf", [128, 128], F32)
    P.memset(identf[:], 1.0)
    P.affine_select(identf[:], identf[:], [[1, 128]], OP.is_equal, 0.0,
                    base=0, channel_multiplier=-1)
    env.identf = identf
    env.X = [env.sbuf(f"X{tt}", [128, D], F32) for tt in range(NTT)]
    env.xlnr = env.sbuf("xlnr", [128, 2, D], BF16)
    env.statS = env.sbuf("statS", [128, 2 * NTT], F32)
    env.LACC = env.sbuf("dbLacc", [128, NTT, 16], F32)
    env.PCO = env.sbuf("dbPco", [128, NTT, 16], F32)
    env.stat = env.sbuf("stat", [128, 64], F32)
    env.lnf32 = env.sbuf("lnf32", [128, D], F32)
    env.ps = [env.psum(f"ps{i}", [128, 512], F32) for i in range(6)]
    env.psT = [env.psum(f"psT{i}", [128, 1024], BF16) for i in range(2)]


def stat_col(env, i):
    return env.stat.ap()[:, i:i + 1]


def emit_ln(P, env, x_tiles=None, dst_T=None, gamma_b=None, beta_b=None,
            tag="ln"):
    x_tiles = x_tiles or env.X
    dst_T = dst_T if dst_T is not None else env.T1.ap()
    st = env.stat.ap()
    sS = env.statS.ap()
    def stats(tt):
        xt = (x_tiles[tt].ap() if not isinstance(x_tiles[tt], bass.AP)
              else x_tiles[tt])
        b = 8 * (tt % 2)
        ssum = st[:, b + 0:b + 1]
        ssq = st[:, b + 1:b + 2]
        P.reduce(ssum, xt, OP.add, tag=f"{tag}_sum{tt}")
        P.act(env.xlnr.ap()[:, tt % 2, :], xt, AF.Square, accum_out=ssq,
              tag=f"{tag}_sq{tt}")
        mu = st[:, b + 2:b + 3]
        var = st[:, b + 4:b + 5]
        scol = sS[:, 2 * tt + 1:2 * tt + 2]
        negmu = sS[:, 2 * tt:2 * tt + 1]
        P.ts(mu, ssum, 1.0 / D, None, OP.mult, tag=f"{tag}_mu{tt}")
        P.ts(var, ssq, 1.0 / D, None, OP.mult)
        P.tt(st[:, b + 3:b + 4], mu, mu, OP.mult)
        P.tt(var, var, st[:, b + 3:b + 4], OP.subtract)
        P.act(var, var, AF.Ln, bias=env.constf(LN_EPS))
        P.act(scol, var, AF.Exp, scale=-0.5)
        P.ts(negmu, mu, -1.0, None, OP.mult)

    def apply(tt):
        xt = (x_tiles[tt].ap() if not isinstance(x_tiles[tt], bass.AP)
              else x_tiles[tt])
        xs = env.xlnr.ap()[:, tt % 2, :]
        scol = sS[:, 2 * tt + 1:2 * tt + 2]
        negmu = sS[:, 2 * tt:2 * tt + 1]
        if gamma_b is None:
            P.ts(xs, xt, negmu, scol, OP.add, OP.mult, tag=f"{tag}_app{tt}")
        else:
            tmpf = env.lnf32.ap()
            P.ts(tmpf, xt, negmu, scol, OP.add, OP.mult, tag=f"{tag}_app{tt}")
            P.tt(tmpf, tmpf, gamma_b, OP.mult, tag=f"{tag}_g{tt}")
            P.tt(xs, tmpf, beta_b, OP.add, tag=f"{tag}_b{tt}")
        pst = env.psT[tt % 2]
        for c in range(NC):
            P.transpose(pst.ap()[:, 128 * c:128 * (c + 1)],
                        xs[:, 128 * c:128 * (c + 1)],
                        env.ident.ap(), tag=f"{tag}_T{tt}_{c}")
        P.copy(dst_T[:, :, 128 * tt:128 * (tt + 1)],
               pst.ap().rearrange("p (c t) -> p c t", c=NC),
               tag=f"{tag}_Tc{tt}")
    stats(0)
    for tt in range(NTT):
        if tt + 1 < NTT:
            stats(tt + 1)
        apply(tt)


def load_weight_fm(P, env, name, dram_ap, rows, cols, tag=None):
    nch = rows // 128
    h = env.sbuf(name, [128, nch, cols], BF16)
    P.dma("gpsimd", h.ap(),
          dram_ap.rearrange("(c p) o -> p c o", p=128), tag=tag or name)
    return h


def load_row_bf16(P, env, name, dram_ap, n, tag=None):
    h = env.sbuf(name, [1, n], BF16)
    P.dma("gpsimd", h.ap(), dram_ap.rearrange("... -> 1 (...)")
          if False else dram_ap, tag=tag or name)
    return h


def emit_l0ffn(P, env, pref="f0"):
    gWT = env.sb.get("l0f_gwT")
    if gWT is None:
        gWT = load_weight_fm(P, env, "l0f_gwT",
                             env.dram_in("L0_ffn_gate_wT", [D, 8]), D, 8)
    gb = env.sb.get("l0f_gb")
    if gb is None:
        gb = env.sbuf("l0f_gb", [1, 8], BF16)
        P.dma("gpsimd", gb.ap(),
              env.dram_in("L0_ffn_gate_b", [8]).rearrange("(o e) -> o e", o=1))
    exb_d = env.dram_in("L0_ffn_expert_b", [8, D])
    ewT_dram = env.dram_in("L0_ffn_expert_wT", [8, D, D])
    m_tm = emit_moe_gates(P, env, pref + "g", gWT, gb, renorm=False)
    t2w = env.T2.ap().rearrange("p a b -> p (a b)")[:, 0:8192].rearrange(
        "p (c o) -> p c o", c=NC)
    wbufs = [env.W3.ap(), t2w]
    for e in range(8):
        P.dma("gpsimd", wbufs[e % 2],
              ewT_dram[e].rearrange("(c p) o -> p c o", p=128),
              tag=f"{pref}_wdma{e}")
        P.dma("gpsimd", env.BROW.ap(),
              exb_d[e].rearrange("(o d) -> o d", o=1), tag=f"{pref}_bdma{e}")
        for tt in range(NTT):
            mcol = m_tm[:, tt, e:e + 1]
            for oh in range(2):
                ps = env.ps[(tt * 2 + oh) % 4]
                out = ps.ap()
                for c in range(NC):
                    P.mm(out, env.T1.ap()[:, c, 128 * tt:128 * (tt + 1)],
                         wbufs[e % 2][:, c, 512 * oh:512 * (oh + 1)],
                         start=(c == 0), stop=False, tag=f"{pref}_e{e}t{tt}")
                P.mm(out, env.ones_row.ap(),
                     env.BROW.ap()[:, 512 * oh:512 * (oh + 1)],
                     start=False, stop=True, tag=f"{pref}_eb{e}t{tt}")
                xs = env.X[tt].ap()[:, 512 * oh:512 * (oh + 1)]
                P.stt(xs, out, mcol, xs, OP.mult, OP.add,
                      tag=f"{pref}_acc{e}t{tt}")


SUBLAYERS = ["l0_attn_a", "l0_ffn_1", "l0_attn_b", "l0_ffn_2",
             "l1_attn_a", "l1_ffn_1", "l1_attn_b", "l1_ffn_2"]
LAST_ENV = None

EMITTERS = {}


def _register():
    EMITTERS["l0_ffn_1"] = lambda P, env, seq: emit_l0ffn(P, env, "f0a")
    EMITTERS["l0_ffn_2"] = lambda P, env, seq: emit_l0ffn(P, env, "f0b")


_register()


def build_sub_program(sub_names):
    nc = bass.Bass("TRN2", target_bir_lowering=False)
    P = Prog(nc)
    env = Env(nc, P)
    x_in = env.dram_in("x", [B_CORE, S, D])
    y_out = env.dram_out("y", [B_CORE, S, D])
    setup_common(env)
    setup_big(env)
    for seq in range(B_CORE):
        for tt in range(NTT):
            P.dma("sync", env.X[tt].ap(),
                  x_in[seq, 128 * tt:128 * (tt + 1), :], tag=f"ldx{seq}_{tt}")
        for name in sub_names:
            emit_ln(P, env, tag=f"ln_{name}_s{seq}")
            EMITTERS[name](P, env, seq)
        for tt in range(NTT):
            P.dma("sync", y_out[seq, 128 * tt:128 * (tt + 1), :],
                  env.X[tt].ap(), tag=f"stx{seq}_{tt}")
    P.emit()
    global LAST_ENV
    LAST_ENV = env
    return nc


def prep_weights(inputs):
    w = {}
    w["L0_ffn_gate_wT"] = np.ascontiguousarray(inputs["L0_ffn_gate_w"].T)
    w["L0_ffn_gate_b"] = inputs["L0_ffn_gate_b"]
    w["L0_ffn_expert_wT"] = np.ascontiguousarray(
        np.transpose(inputs["L0_ffn_expert_w"], (0, 2, 1)))
    w["L0_ffn_expert_b"] = inputs["L0_ffn_expert_b"]
    prep_weights2(inputs, w)
    return w


_NC_CACHE = {}


def build_full():
    if "nc" not in _NC_CACHE:
        _NC_CACHE["nc"] = build_sub_program(SUBLAYERS)
    return _NC_CACHE["nc"]


def kernel(**inputs):
    inputs = {k: np.asarray(v) for k, v in inputs.items()}
    x = np.ascontiguousarray(inputs["L0_attn_a_x"], np.float32)
    w = prep_weights(inputs)
    nc = build_full()
    need = [n for n in LAST_ENV.dram
            if n not in LAST_ENV.internal and n != "y"]
    in_maps = []
    for core in range(8):
        m = {}
        for n in need:
            if n == "x":
                m["x"] = x[2 * core:2 * core + 2]
            else:
                src = w.get(n)
                if src is None:
                    src = inputs[n]
                m[n] = np.ascontiguousarray(src, np.float32)
        in_maps.append(m)
    res = run_bass_kernel_spmd(nc, in_maps, core_ids=list(range(8)))
    out = np.concatenate([r["y"] for r in res.results], axis=0)
    return out.astype(np.float32)


def setup_big(env):
    env.T1 = env.sbuf("T1", [128, NC, S], BF16)
    env.T2 = env.sbuf("T2", [128, NTT, 1040], BF16)
    env.QK = env.sbuf("QK", [128, 2, S], BF16)
    env.PR = env.sbuf("PR", [128, 3, 512], BF16)
    env.AFP = env.sbuf("AFP", [128, 1, S], BF16)
    env.AFG = env.sbuf("AFG", [128, 2, NC, 128], BF16)
    env.W1 = env.sbuf("W1", [128, NC, 2, 128], BF16)
    env.W2 = env.sbuf("W2", [128, NC, 2, 128], BF16)
    env.W3 = env.sbuf("W3", [128, NC, D], BF16)
    env.BROW = env.sbuf("BROW", [1, D], BF16)
    env.XR = [env.sbuf("XR0", [128, D], F32)] * 2
    env.GB = env.sbuf("GB", [128, D], BF16)
    env.BB = env.sbuf("BB", [128, D], BF16)
    env.ROWG = env.sbuf("ROWG", [1, 1024], BF16)
    env.ROWB = env.sbuf("ROWB", [1, 1024], BF16)
    env.LND = env.sbuf("LND", [1, 512], F32)
    env.RDB = env.sbuf("RDB", [64, 512], BF16)


def bcast_row(P, env, dst128, src_row, n, tag="bc"):
    ones = env.ones_row_f if src_row.dtype == F32 else env.ones_row
    for off in range(0, n, 512):
        w = min(512, n - off)
        ps = env.ps[5]
        P.mm(ps.ap()[:, 0:w], ones.ap(), src_row[:, off:off + w], tag=tag)
        P.copy(dst128[:, off:off + w], ps.ap()[:, 0:w], tag=tag + "c")


def _wchunk(wbuf, c, sl):
    return wbuf.ap()[:, c, sl]


def emit_attention(P, env, seq, cfg, pref):
    NH = 16
    gb_name = cfg.get("input_ln_gamma")
    if gb_name:
        row = env.ROWG.ap()
        P.dma("gpsimd", row, env.dram_in(gb_name, [D]).rearrange(
            "(o d) -> o d", o=1), tag=f"{pref}_g1")
        bcast_row(P, env, env.GB.ap(), row, D, tag=f"{pref}_gbc")
        row2 = env.ROWB.ap()
        P.dma("gpsimd", row2, env.dram_in(cfg["input_ln_beta"], [D]).rearrange(
            "(o d) -> o d", o=1), tag=f"{pref}_b1")
        bcast_row(P, env, env.BB.ap(), row2, D, tag=f"{pref}_bbc")
        emit_ln(P, env, gamma_b=env.GB.ap(), beta_b=env.BB.ap(),
                tag=f"{pref}_ln")
    else:
        emit_ln(P, env, tag=f"{pref}_ln")
    xsp = env.dram_tmp("xspill", [NTT, 128, D])
    if cfg.get("spill", True):
        for tt in range(NTT):
            P.dma("sync", xsp[tt], env.X[tt].ap(), tag=f"{pref}_sp{tt}")

    wq = env.dram_in(cfg["wq"], [D, D])
    wk = env.dram_in(cfg["wk"], [D, D])
    wv = env.dram_in(cfg["wv"], [D, D])
    wo = env.dram_in(cfg["wo"], [D, D])
    afd = env.dram_tmp("afdram", [NC, 128, S], dtype=BF16)
    T1 = env.T1.ap()
    V65 = env.T2.ap()
    QKa = env.QK.ap()
    PR = env.PR.ap()

    P.dma("gpsimd", env.W3.ap(), wv.rearrange("(c p) o -> p c o", p=128),
          tag=f"{pref}_wv")
    for tt in range(NTT):
        P.memset(V65[:, tt, :].rearrange("p (h e) -> p h e", e=65)[:, :, 64:65],
                 1.0, tag=f"{pref}_vone{tt}")
        for oh in range(2):
            ps = env.ps[oh]
            for c in range(NC):
                P.mm(ps.ap(), T1[:, c, 128 * tt:128 * (tt + 1)],
                     env.W3.ap()[:, c, 512 * oh:512 * (oh + 1)],
                     start=(c == 0), stop=(c == NC - 1), tag=f"{pref}_v{tt}")
            P.copy(V65[:, tt, 520 * oh:520 * oh + 520].rearrange(
                       "p (h e) -> p h e", e=65)[:, :, 0:64],
                   ps.ap().rearrange("p (h e) -> p h e", e=64),
                   eng="scalar", tag=f"{pref}_vc{tt}")
    nqt = S // 512
    for hp in range(NH // 2):
        wsl = hp % 2
        P.dma("gpsimd", env.W1.ap()[:, :, wsl, :],
              wq.rearrange("(c p) o -> p c o", p=128)[:, :, 128 * hp:128 * (hp + 1)],
              tag=f"{pref}_wq{hp}")
        P.dma("gpsimd", env.W2.ap()[:, :, wsl, :],
              wk.rearrange("(c p) o -> p c o", p=128)[:, :, 128 * hp:128 * (hp + 1)],
              tag=f"{pref}_wk{hp}")
        for (wbuf, qki, scale) in ((env.W1, 0, 0.125), (env.W2, 1, None)):
            for tb in range(nqt):
                ps = env.ps[4 + tb % 2]
                for c in range(NC):
                    P.mm(ps.ap(), wbuf.ap()[:, c, wsl, :],
                         T1[:, c, 512 * tb:512 * (tb + 1)],
                         start=(c == 0), stop=(c == NC - 1),
                         tag=f"{pref}_qk{hp}")
                dst = QKa[:, qki, 512 * tb:512 * (tb + 1)]
                if scale is None:
                    P.copy(dst, ps.ap(), tag=f"{pref}_qke{hp}")
                else:
                    P.act(dst, ps.ap(), AF.Copy, scale=scale,
                          tag=f"{pref}_qse{hp}")
        pend = None
        for h2 in range(2):
            h = 2 * hp + h2
            pb = 64 * h2
            for qt in range(nqt):
                nch = NTT if not cfg["causal"] else 4 * qt + 4
                pvb = env.ps[2 + (h * 4 + qt) % 2]

                def emit_s(c, pb=pb, qt=qt):
                    P.mm(env.ps[c % 2].ap(),
                         QKa[pb:pb + 64, 1, 128 * c:128 * (c + 1)],
                         QKa[pb:pb + 64, 0, 512 * qt:512 * (qt + 1)],
                         tag=f"{pref}_s{h}_{qt}")
                if pend is not None:
                    pend()
                emit_s(0)
                for c in range(nch):
                    if c + 1 < nch:
                        emit_s(c + 1)
                    pt = PR[:, c % 3, :]
                    P.act(pt, env.ps[c % 2].ap(), AF.Exp,
                          bias=env.constf(-24.0), tag=f"{pref}_x{h}_{qt}")
                    if cfg["causal"] and c >= 4 * qt:
                        P.affine_select(pt, pt, [[1, 512]], OP.is_ge, 0.0,
                                        base=512 * qt - 128 * c,
                                        channel_multiplier=-1,
                                        tag=f"{pref}_m{h}_{qt}")
                    P.mm(pvb.ap()[0:65, :], V65[:, c, 65 * h:65 * (h + 1)],
                         pt, start=(c == 0), stop=(c == nch - 1),
                         tag=f"{pref}_pv{h}_{qt}")

                def pend(pvb=pvb, h=h, qt=qt, pb=pb):
                    P.act(env.LND.ap(), pvb.ap()[64:65, :], AF.Ln,
                          tag=f"{pref}_ld{h}_{qt}")
                    psf0 = env.psT[0].ap().bitcast(F32)
                    P.mm(psf0[0:64, 0:512], env.ones_row_f.ap()[:, 0:64],
                         env.LND.ap(), tag=f"{pref}_bc{h}_{qt}")
                    P.act(env.RDB.ap(), psf0[0:64, 0:512], AF.Exp,
                          scale=-1.0, tag=f"{pref}_rd{h}_{qt}")
                    P.tt(env.AFP.ap()[pb:pb + 64, 0, 512 * qt:512 * (qt + 1)],
                         pvb.ap()[0:64, :], env.RDB.ap(), OP.mult,
                         tag=f"{pref}_af{h}_{qt}")
        pend()
        P.dma("sync", afd[hp], env.AFP.ap()[:, 0, :],
              tag=f"{pref}_afo{hp}")
    P.dma("gpsimd", env.W3.ap(), wo.rearrange("(c p) o -> p c o", p=128),
          tag=f"{pref}_wo")
    bo_row = None
    if cfg.get("bo"):
        bo_row = env.BROW
        P.dma("gpsimd", bo_row.ap(),
              env.dram_in(cfg["bo"], [D]).rearrange("(o d) -> o d", o=1),
              tag=f"{pref}_bo")
    if cfg["tail"] == "ln2":
        g2 = env.ROWG.ap()
        P.dma("gpsimd", g2, env.dram_in(cfg["ln2_gamma"], [D]).rearrange(
            "(o d) -> o d", o=1), tag=f"{pref}_g2")
        bcast_row(P, env, env.GB.ap(), g2, D, tag=f"{pref}_g2b")
        b2 = env.ROWB.ap()
        P.dma("gpsimd", b2, env.dram_in(cfg["ln2_beta"], [D]).rearrange(
            "(o d) -> o d", o=1), tag=f"{pref}_b2")
        bcast_row(P, env, env.BB.ap(), b2, D, tag=f"{pref}_b2b")
    st = env.stat.ap()
    sS = env.statS.ap()
    for tt in range(NTT):
        afg = env.AFG.ap()[:, tt % 2, :, :]
        P.dma("sync", afg,
              afd.rearrange("h p t -> p h t")[:, :, 128 * tt:128 * (tt + 1)],
              tag=f"{pref}_afg{tt}")
        xr = env.XR[tt % 2].ap()
        if cfg.get("spill", True):
            P.dma("sync", xr, xsp[tt], tag=f"{pref}_rl{tt}")
        for oh in range(2):
            ps = env.ps[4 + oh]
            for c in range(NC):
                P.mm(ps.ap(), afg[:, c, :],
                     env.W3.ap()[:, c, 512 * oh:512 * (oh + 1)],
                     start=(c == 0),
                     stop=(c == NC - 1 and bo_row is None), tag=f"{pref}_o{tt}")
            if bo_row is not None:
                P.mm(ps.ap(), env.ones_row.ap(),
                     bo_row.ap()[:, 512 * oh:512 * (oh + 1)],
                     start=False, stop=True, tag=f"{pref}_ob{tt}")
            xs = env.X[tt].ap()[:, 512 * oh:512 * (oh + 1)]
            if cfg["tail"] == "ln2":
                P.ts(env.lnf32.ap()[:, 512 * oh:512 * (oh + 1)],
                     xr[:, 512 * oh:512 * (oh + 1)],
                     sS[:, 2 * tt:2 * tt + 1], sS[:, 2 * tt + 1:2 * tt + 2],
                     OP.add, OP.mult, tag=f"{pref}_lnr{tt}")
                P.tt(xs, ps.ap(), env.lnf32.ap()[:, 512 * oh:512 * (oh + 1)],
                     OP.add, tag=f"{pref}_t2{tt}")
            elif cfg["tail"] == "acc":
                P.tt(xs, ps.ap(), xs, OP.add, tag=f"{pref}_acc{tt}")
            else:
                P.copy(xs, ps.ap(), tag=f"{pref}_y{tt}")
        xt = env.X[tt].ap()
        if cfg["tail"] == "ln2":
            ssum = st[:, 0:1]
            ssq = st[:, 1:2]
            mu = st[:, 2:3]
            var = st[:, 4:5]
            scol = st[:, 5:6]
            negmu = st[:, 6:7]
            tsc = env.xlnr.ap()[:, tt % 2, :]
            P.reduce(ssum, xt, OP.add, tag=f"{pref}_l2s{tt}")
            P.act(tsc, xt, AF.Square, accum_out=ssq, tag=f"{pref}_l2q{tt}")
            P.ts(mu, ssum, 1.0 / D, None, OP.mult)
            P.ts(var, ssq, 1.0 / D, None, OP.mult)
            P.tt(st[:, 3:4], mu, mu, OP.mult)
            P.tt(var, var, st[:, 3:4], OP.subtract)
            P.act(var, var, AF.Ln, bias=env.constf(LN_EPS))
            P.act(scol, var, AF.Exp, scale=-0.5)
            P.ts(negmu, mu, -1.0, None, OP.mult)
            P.ts(xt, xt, negmu, scol, OP.add, OP.mult, tag=f"{pref}_l2n{tt}")
            P.tt(xt, xt, env.GB.ap(), OP.mult, tag=f"{pref}_l2g{tt}")
            P.tt(xt, xt, env.BB.ap(), OP.add, tag=f"{pref}_l2b{tt}")
        if cfg.get("spill", True):
            P.tt(xt, xt, xr, OP.add, tag=f"{pref}_res{tt}")


def _register2():
    EMITTERS["l0_attn_b"] = lambda P, env, seq: emit_attention(
        P, env, seq, dict(wq="L0_attn_b_WqT", wk="L0_attn_b_WkT",
                          wv="L0_attn_b_WvT", wo="L0_attn_b_WoutT",
                          causal=False, tail="ln2",
                          ln2_gamma="L0_attn_b_gamma",
                          ln2_beta="L0_attn_b_beta"), f"ab{seq}")
    EMITTERS["l1_attn_a"] = lambda P, env, seq: emit_attention(
        P, env, seq, dict(wq="L1_attn_a_wqT", wk="L1_attn_a_wkT",
                          wv="L1_attn_a_wvT", wo="L1_attn_a_w_oT",
                          causal=True, tail="plain"), f"aa{seq}")


_register2()


def prep_weights2(inputs, w):
    w["L0_attn_b_WqT"] = np.ascontiguousarray(inputs["L0_attn_b_Wq"].T)
    w["L0_attn_b_WkT"] = np.ascontiguousarray(inputs["L0_attn_b_Wk"].T)
    w["L0_attn_b_WvT"] = np.ascontiguousarray(inputs["L0_attn_b_Wv"].T)
    w["L0_attn_b_WoutT"] = np.ascontiguousarray(inputs["L0_attn_b_Wout"].T)
    w["L0_attn_b_gamma"] = inputs["L0_attn_b_gamma"]
    w["L0_attn_b_beta"] = inputs["L0_attn_b_beta"]
    qkv = inputs["L1_attn_a_w_qkv"].reshape(3, D, D)
    w["L1_attn_a_wqT"] = np.ascontiguousarray(qkv[0].T)
    w["L1_attn_a_wkT"] = np.ascontiguousarray(qkv[1].T)
    w["L1_attn_a_wvT"] = np.ascontiguousarray(qkv[2].T)
    w["L1_attn_a_w_oT"] = np.ascontiguousarray(inputs["L1_attn_a_w_o"].T)
    M, eo = rwkv_consts()
    w["rwkv_consts"] = M
    w["rwkv_eo"] = eo
    u = np.arange(128)
    m6 = ((u[:, None] > u[None, :]) & (u[:, None] // 32 == u[None, :] // 32))
    w["delta_m6"] = m6.astype(np.float32)
    w["L1_attn_b_norm_w4"] = np.ascontiguousarray(
        np.tile(inputs["L1_attn_b_norm_w"], 4))
    w["L1_attn_b_fir_short_f"] = np.ascontiguousarray(
        inputs["L1_attn_b_fir_short_w"].reshape(1024, 3))
    cdg = np.zeros((3, 8, 4, 128, 128), np.float32)
    ii = np.arange(128)
    for ki_, cn in enumerate(("L1_attn_b_conv_v", "L1_attn_b_conv_q",
                              "L1_attn_b_conv_k")):
        cw = inputs[cn].reshape(1024, 4)
        for oc_ in range(8):
            cdg[ki_, oc_, :, ii, ii] = cw[128 * oc_:128 * (oc_ + 1), :]
    w["conv_diag"] = cdg
    lgf = inputs["L1_attn_b_fir_long_w"].reshape(1024, 63)
    dg = np.zeros((8, 63, 128, 128), np.float32)
    ii = np.arange(128)
    for oc_ in range(8):
        dg[oc_, :, ii, ii] = lgf[128 * oc_:128 * (oc_ + 1), :]
    w["fir_long_diag"] = dg
    return w


def emit_moe_gates(P, env, pref, gwT_buf, gbias_row, renorm):
    ps_g = env.ps[4]
    for tt in range(NTT):
        out = ps_g.ap()[:, 8 * tt:8 * (tt + 1)]
        for c in range(NC):
            P.mm(out, env.T1.ap()[:, c, 128 * tt:128 * (tt + 1)],
                 gwT_buf.ap()[:, c, :], start=(c == 0),
                 stop=(c == NC - 1 and gbias_row is None), tag=f"{pref}_g{tt}")
        if gbias_row is not None:
            P.mm(out, env.ones_row.ap(), gbias_row.ap(), start=False,
                 stop=True, tag=f"{pref}_gb{tt}")
    m_tm = env.PCO.ap()[:, :, 0:8]
    gd = env.PCO.ap()[:, :, 8:16]
    gm = env.LACC.ap()[:, :, 0:8]
    gst = env.LACC.ap()[:, :, 8:12]
    glog = ps_g.ap()[:, 0:8 * NTT].rearrange("p (t e) -> p t e", e=8)
    gmax = gst[:, :, 0:1]
    d2 = gst[:, :, 1:2]
    gsum = gst[:, :, 2:3]
    rgsum = gst[:, :, 3:4]
    P.reduce(gmax, glog, OP.max, tag=f"{pref}_gmax")
    P.tt(gd, glog, gmax.broadcast_to((128, NTT, 8)), OP.subtract,
         tag=f"{pref}_gsub")
    P.act(m_tm, gd, AF.Exp, tag=f"{pref}_gexp")
    P.ts(gm, gd, 0.0, None, OP.is_ge, tag=f"{pref}_ismax")
    P.stt(gm, gm, -1e30, gd, OP.mult, OP.add, tag=f"{pref}_dmask")
    P.reduce(d2, gm, OP.max, tag=f"{pref}_d2")
    P.tt(gm, gd, d2.broadcast_to((128, NTT, 8)), OP.is_ge,
         tag=f"{pref}_keep")
    P.tt(m_tm, m_tm, gm, OP.mult, tag=f"{pref}_mk")
    P.reduce(gsum, m_tm, OP.add, tag=f"{pref}_gsum")
    P.recip(rgsum, gsum, tag=f"{pref}_rg")
    if not renorm:
        P.act(gd, gd, AF.Exp, tag=f"{pref}_gexp2")
        P.reduce(gsum, gd, OP.add, tag=f"{pref}_gsumf")
        P.recip(rgsum, gsum, tag=f"{pref}_rgf")
    P.tt(m_tm, m_tm, rgsum.broadcast_to((128, NTT, 8)), OP.mult,
         tag=f"{pref}_mn")
    return m_tm


def emit_l1ffn(P, env, seq, pref):
    st = env.stat.ap()
    sS = env.statS.ap()
    xsp = env.dram_tmp("xspill", [NTT, 128, D])
    for tt in range(NTT):
        xt = env.X[tt].ap()
        P.dma("sync", xsp[tt], xt, tag=f"{pref}_sp{tt}")
        ssum = st[:, 0:1]
        ssq = st[:, 1:2]
        P.reduce(ssum, xt, OP.add, tag=f"{pref}_s{tt}")
        P.act(env.xlnr.ap()[:, tt % 2, :], xt, AF.Square, accum_out=ssq,
              tag=f"{pref}_q{tt}")
        mu = st[:, 2:3]
        var = st[:, 4:5]
        scol = sS[:, 2 * tt + 1:2 * tt + 2]
        negmu = sS[:, 2 * tt:2 * tt + 1]
        P.ts(mu, ssum, 1.0 / D, None, OP.mult)
        P.ts(var, ssq, 1.0 / D, None, OP.mult)
        P.tt(st[:, 3:4], mu, mu, OP.mult)
        P.tt(var, var, st[:, 3:4], OP.subtract)
        P.act(var, var, AF.Ln, bias=env.constf(LN_EPS))
        P.act(scol, var, AF.Exp, scale=-0.5)
        P.ts(negmu, mu, -1.0, None, OP.mult)
        P.ts(xt, xt, negmu, scol, OP.add, OP.mult, tag=f"{pref}_ln{tt}")
    emit_attention(P, env, seq, dict(
        wq="L1_ffn_wq", wk="L1_ffn_wk", wv="L1_ffn_wv", wo="L1_ffn_wo",
        bo="L1_ffn_bo", causal=True, tail="acc", spill=False,
        input_ln_gamma="L1_ffn_ln1_scale", input_ln_beta="L1_ffn_ln1_shift"),
        pref + "a")
    P.dma("gpsimd", env.ROWG.ap(), env.dram_in("L1_ffn_ln2_scale", [D])
          .rearrange("(o d) -> o d", o=1), tag=f"{pref}_g2")
    bcast_row(P, env, env.GB.ap(), env.ROWG.ap(), D, tag=f"{pref}_g2b")
    P.dma("gpsimd", env.ROWB.ap(), env.dram_in("L1_ffn_ln2_shift", [D])
          .rearrange("(o d) -> o d", o=1), tag=f"{pref}_b2")
    bcast_row(P, env, env.BB.ap(), env.ROWB.ap(), D, tag=f"{pref}_b2b")
    emit_ln(P, env, gamma_b=env.GB.ap(), beta_b=env.BB.ap(),
            tag=f"{pref}_ln2")
    gWT = env.sb.get("l1f_gw")
    if gWT is None:
        gWT = load_weight_fm(P, env, "l1f_gw",
                             env.dram_in("L1_ffn_gate_w", [D, 8]), D, 8)
    m_tm = emit_moe_gates(P, env, pref + "g", gWT, None, renorm=True)
    w1d = env.dram_in("L1_ffn_w1", [8, D, 2048])
    w2d = env.dram_in("L1_ffn_w2", [8, D, 2048])
    w3d = env.dram_in("L1_ffn_w3", [8, 2048, D])
    HT = env.T2.ap().rearrange("p a b -> p (a b)")[:, 0:16384].rearrange(
        "p (c t) -> p c t", c=8)
    AB = env.PR.ap()
    qk4 = env.QK.ap().rearrange("p a b -> p (a b)")
    w1vs = [env.W1.ap().rearrange("p c a b -> p c (a b)"),
            qk4[:, 0:2048].rearrange("p (c o) -> p c o", c=NC)]
    w2vs = [env.W2.ap().rearrange("p c a b -> p c (a b)"),
            qk4[:, 2048:4096].rearrange("p (c o) -> p c o", c=NC)]
    for e in range(8):
        for hq in range(8):
            sl = slice(256 * hq, 256 * (hq + 1))
            w1v = w1vs[hq % 2]
            w2v = w2vs[hq % 2]
            P.dma("gpsimd", w1v,
                  w1d[e].rearrange("(c p) o -> p c o", p=128)[:, :, sl],
                  tag=f"{pref}_w1{e}")
            P.dma("gpsimd", w2v,
                  w2d[e].rearrange("(c p) o -> p c o", p=128)[:, :, sl],
                  tag=f"{pref}_w2{e}")
            for j in range(2):
                hc = 2 * hq + j
                for tb in range(4):
                    psA = env.ps[tb % 2]
                    psB = env.ps[2 + tb % 2]
                    for c in range(NC):
                        P.mm(psA.ap(), w1v[:, c, 128 * j:128 * (j + 1)],
                             env.T1.ap()[:, c, 512 * tb:512 * (tb + 1)],
                             start=(c == 0), stop=(c == NC - 1),
                             tag=f"{pref}_h1{e}")
                        P.mm(psB.ap(), w2v[:, c, 128 * j:128 * (j + 1)],
                             env.T1.ap()[:, c, 512 * tb:512 * (tb + 1)],
                             start=(c == 0), stop=(c == NC - 1),
                             tag=f"{pref}_h2{e}")
                    slab = AB[:, tb % 2, :]
                    P.act(slab, psA.ap(), AF.Silu, tag=f"{pref}_si{e}")
                    P.tt(HT[:, hc % 8, 512 * tb:512 * (tb + 1)], slab,
                         psB.ap(), OP.mult, tag=f"{pref}_hm{e}")
            if hq % 4 == 3:
                hh = hq // 4
                P.dma("gpsimd", env.W3.ap(),
                      w3d[e, 1024 * hh:1024 * (hh + 1), :].rearrange(
                          "(c p) o -> p c o", p=128), tag=f"{pref}_w3{e}")
                for tt in range(NTT):
                    mcol = m_tm[:, tt, e:e + 1]
                    for oh in range(2):
                        ps = env.ps[4 + (tt % 2)]
                        for c in range(NC):
                            P.mm(ps.ap(), HT[:, c, 128 * tt:128 * (tt + 1)],
                                 env.W3.ap()[:, c, 512 * oh:512 * (oh + 1)],
                                 start=(c == 0), stop=(c == NC - 1),
                                 tag=f"{pref}_o{e}t{tt}")
                        xs = env.X[tt].ap()[:, 512 * oh:512 * (oh + 1)]
                        P.stt(xs, ps.ap(), mcol, xs, OP.mult, OP.add,
                              tag=f"{pref}_a{e}t{tt}")
    for tt in range(NTT):
        xr = env.XR[tt % 2].ap()
        P.dma("sync", xr, xsp[tt], tag=f"{pref}_rl{tt}")
        P.tt(env.X[tt].ap(), env.X[tt].ap(), xr, OP.add, tag=f"{pref}_rs{tt}")


def _register3():
    EMITTERS["l1_ffn_1"] = lambda P, env, seq: emit_l1ffn(P, env, seq, f"F1{seq}")
    EMITTERS["l1_ffn_2"] = lambda P, env, seq: emit_l1ffn(P, env, seq, f"F2{seq}")


_register3()


def emit_l0attna(P, env, seq, pref):
    emit_ln(P, env, tag=f"{pref}_ln")
    T1 = env.T1.ap()
    MC = env.sb.get("rwM")
    if MC is None:
        MC = env.sbuf("rwM", [128, 5, 128], F32)
        P.dma("sync", MC.ap(), env.dram_in("rwkv_consts", [5, 128, 128])
              .rearrange("m p j -> p m j"), tag="rwMdma")
        eo = env.sbuf("rwEO", [128, 2], F32)
        P.dma("sync", eo.ap(), env.dram_in("rwkv_eo", [128, 2]), tag="rwEOdma")
        env.rwEO = eo
    eo = env.rwEO
    xrb = env.XR[0].ap().bitcast(BF16)
    tfb = xrb[:, 1024:1536]
    P.dma("gpsimd", env.BROW.ap()[:, 0:512],
          env.dram_in("L0_attn_a_time_first", [8, 64]).rearrange(
              "(o h) k -> o (h k)", o=1), tag=f"{pref}_TFr")
    bcast_row(P, env, tfb, env.BROW.ap()[:, 0:512], 512, tag=f"{pref}_TFb")
    bdb = env.XR[0].ap()[:, 0:512]
    rowf = env.LND.ap()
    P.dma("sync", rowf, env.dram_in("L0_attn_a_base_decay", [8, 64])
          .rearrange("(o h) k -> o (h k)", o=1), tag=f"{pref}_BDr")
    bcast_row(P, env, bdb, rowf, 512, tag=f"{pref}_BDb")
    bS = env.stat.ap()[:, 16:32]
    drF = env.AFP.ap()[0:32, 0, :]
    dtW = env.RDB.ap()[0:32, :]
    dtB = env.BROW.ap()[:, 512:1024]
    P.dma("gpsimd", dtW, env.dram_in("L0_attn_a_dt_W", [32, 512]),
          tag=f"{pref}_dtW")
    P.dma("gpsimd", dtB, env.dram_in("L0_attn_a_dt_b", [512]).rearrange(
        "(o k) -> o k", o=1), tag=f"{pref}_dtB")
    w1f = env.W1.ap().rearrange("p c a b -> p (c a b)")
    w2f = env.W2.ap().rearrange("p c a b -> p (c a b)")
    wsel = w1f[:, 0:1280].rearrange("p (c o) -> p c o", c=NC)
    P.dma("gpsimd", wsel, env.dram_in("L0_attn_a_W_sel", [D, 160]).rearrange(
        "(c p) o -> p c o", p=128), tag=f"{pref}_Wsel")
    for tt in range(NTT):
        ps = env.ps[4 + tt % 2]
        for c in range(NC):
            P.mm(ps.ap()[:, 0:160], T1[:, c, 128 * tt:128 * (tt + 1)],
                 wsel[:, c, :], start=(c == 0), stop=(c == NC - 1),
                 tag=f"{pref}_sel{tt}")
        P.reduce(bS[:, tt:tt + 1], ps.ap()[:, 32:96], OP.add,
                 tag=f"{pref}_bv{tt}")
        P.ts(bS[:, tt:tt + 1], bS[:, tt:tt + 1], 1.0 / 64, None, OP.mult,
             tag=f"{pref}_bvs{tt}")
        pst = env.psT[tt % 2]
        P.copy(env.xlnr.ap()[:, tt % 2, 0:32], ps.ap()[:, 0:32],
               tag=f"{pref}_drc{tt}")
        P.transpose(pst.ap()[0:32, 0:128], env.xlnr.ap()[:, tt % 2, 0:32],
                    env.ident.ap(), tag=f"{pref}_drT{tt}")
        P.copy(drF[0:32, 128 * tt:128 * (tt + 1)], pst.ap()[0:32, 0:128],
               tag=f"{pref}_dre{tt}")
    t2f = env.T2.ap().rearrange("p a b -> p (a b)")
    WkB = t2f[:, 0:4096].rearrange("p (c o) -> p c o", c=NC)
    WvB = t2f[:, 4096:8192].rearrange("p (c o) -> p c o", c=NC)
    WrB = t2f[:, 8192:12288].rearrange("p (c o) -> p c o", c=NC)
    WgB = t2f[:, 12288:16384].rearrange("p (c o) -> p c o", c=NC)
    for nm, dst in (("L0_attn_a_Wk", WkB), ("L0_attn_a_Wv", WvB),
                    ("L0_attn_a_Wr", WrB), ("L0_attn_a_Wg", WgB)):
        P.dma("gpsimd", dst, env.dram_in(nm, [D, 512]).rearrange(
            "(c p) o -> p c o", p=128), tag=f"{pref}_{nm}")
    WoA = w1f[:, 0:2048].rearrange("p (c o) -> p c o", c=4)
    WoB2 = w2f[:, 0:2048].rearrange("p (c o) -> p c o", c=4)
    P.dma("gpsimd", WoA, env.dram_in("L0_attn_a_Wo", [512, D]).rearrange(
        "(c p) o -> p c o", p=128)[:, :, 0:512], tag=f"{pref}_WoA")
    P.dma("gpsimd", WoB2, env.dram_in("L0_attn_a_Wo", [512, D]).rearrange(
        "(c p) o -> p c o", p=128)[:, :, 512:1024], tag=f"{pref}_WoB")
    qkf = env.QK.ap().rearrange("p a b -> p (a b)")
    qe = qkf[:, 0:512]
    qo = qkf[:, 512:1024]
    k1e = qkf[:, 1024:1536]
    k1o = qkf[:, 1536:2048]
    k2e = qkf[:, 2048:2560]
    q2 = qkf[:, 2560:3072]
    kuT = qkf[:, 3072:3584]
    eTmp = qkf[:, 3584:4096]
    lF = env.GB.ap().bitcast(F32)
    sgF = env.BB.ap().bitcast(F32)
    w3f = env.W3.ap().rearrange("p c o -> p (c o)")
    KM = w3f[:, 0:3072].rearrange("p (v c t) -> p v c t", v=6, c=4)
    SC = w3f[:, 3072:4096].rearrange("p (h t) -> p h t", h=8)
    kT = w3f[:, 4096:4608]
    vT = w3f[:, 4608:5120]
    rgT = w3f[:, 5120:5632].rearrange("p (c t) -> p c t", c=4)
    EXS = env.sbuf("rwexs", [128, 4, 4], F32)
    EXB = env.sbuf("rwexb", [128, 4, 4], BF16)
    afgf = env.AFG.ap().rearrange("p a c t -> p (a c t)")
    TFT = afgf[:, 0:512].rearrange("p (c t) -> p c t", c=4)
    WS = afgf[:, 1024:1280].rearrange("p (c t) -> p c t", c=4)
    WT = afgf[:, 1280:1536].rearrange("p (c t) -> p c t", c=4)
    YG = afgf[:, 1536:2048].rearrange("p (c t) -> p c t", c=4)
    P.memset(WS, 0.0, tag=f"{pref}_W0")
    for tt in range(NTT):
        tsl = slice(128 * tt, 128 * (tt + 1))
        for kvi, (wb, dst) in enumerate(((WkB, kT), (WvB, vT))):
            ps = env.ps[4 + kvi]
            for c in range(NC):
                P.mm(ps.ap(), T1[:, c, tsl], wb[:, c, :],
                     start=(c == 0), stop=(c == NC - 1), tag=f"{pref}_kv{tt}")
            P.copy(dst, ps.ap(), tag=f"{pref}_kve{tt}")
        psr = env.ps[4]
        psg = env.ps[5]
        for c in range(NC):
            P.mm(psr.ap(), T1[:, c, tsl], WrB[:, c, :],
                 start=(c == 0), stop=(c == NC - 1), tag=f"{pref}_r{tt}")
            P.mm(psg.ap(), T1[:, c, tsl], WgB[:, c, :],
                 start=(c == 0), stop=(c == NC - 1), tag=f"{pref}_g{tt}")
        P.act(env.PR.ap()[:, 0, :], psr.ap(), AF.Sigmoid, tag=f"{pref}_sr{tt}")
        P.act(env.PR.ap()[:, 1, :], psg.ap(), AF.Sigmoid, tag=f"{pref}_sg{tt}")
        P.tt(env.PR.ap()[:, 2, :], env.PR.ap()[:, 0, :], env.PR.ap()[:, 1, :],
             OP.mult, tag=f"{pref}_rg{tt}")
        pst = env.psT[0]
        for c in range(4):
            P.transpose(pst.ap()[:, 128 * c:128 * (c + 1)],
                        env.PR.ap()[:, 2, 128 * c:128 * (c + 1)],
                        env.ident.ap(), tag=f"{pref}_rgT{tt}")
        P.copy(rgT, pst.ap()[:, 0:512], tag=f"{pref}_rge{tt}")
        psd = env.ps[4]
        P.mm(psd.ap(), drF[0:32, tsl], dtW, start=True, stop=False,
             tag=f"{pref}_dp{tt}")
        P.mm(psd.ap(), env.ones_row.ap(), dtB, start=False, stop=True,
             tag=f"{pref}_dpb{tt}")
        P.act(lF, psd.ap(), AF.Abs, tag=f"{pref}_ab{tt}")
        P.act(lF, lF, AF.Exp, scale=-1.0, tag=f"{pref}_ex{tt}")
        P.act(lF, lF, AF.Ln, bias=env.constf(1.0), tag=f"{pref}_ln1p{tt}")
        P.act(sgF, psd.ap(), AF.Relu, tag=f"{pref}_rl{tt}")
        P.tt(lF, lF, sgF, OP.add, tag=f"{pref}_sp{tt}")
        P.tt(lF, lF, bdb, OP.mult, tag=f"{pref}_l{tt}")
        P.ts(lF, lF, -1.0, None, OP.mult, tag=f"{pref}_ng{tt}")
        specs = [(0, qe, None, eo.ap()[:, 0:1], False),
                 (0, qo, None, eo.ap()[:, 1:2], False),
                 (0, k1e, -1.0, eo.ap()[:, 0:1], True),
                 (0, k1o, -1.0, eo.ap()[:, 1:2], True),
                 (1, k2e, None, eo.ap()[:, 0:1], True),
                 (3, q2, None, None, False)]
        pse = [env.ps[4], env.ps[5]]
        eTmps = [eTmp, env.PR.ap()[:, 0, :]]

        def spec_mm(i, tt=tt):
            mi = specs[i][0]
            P.mm(pse[i % 2].ap()[:, 0:512], MC.ap()[:, mi, :], lF,
                 tag=f"{pref}_f{tt}_{i}")
        spec_mm(0)
        for i, (mi, dst, sc, msk, useb) in enumerate(specs):
            if i + 1 < len(specs):
                spec_mm(i + 1)
            et_ = eTmps[i % 2]
            P.act(et_, pse[i % 2].ap()[:, 0:512], AF.Exp, scale=sc,
                  tag=f"{pref}_fe{tt}_{i}")
            P.tt(dst, et_, kT, OP.mult, tag=f"{pref}_fk{tt}_{i}")
            if msk is not None:
                sc2 = bS[:, tt:tt + 1] if useb else None
                P.ts(dst, dst, msk, sc2, OP.mult,
                     OP.mult if useb else OP.bypass, tag=f"{pref}_fm{tt}_{i}")
            pst = env.psT[i % 2]
            for c in range(4):
                P.transpose(pst.ap()[:, 128 * c:128 * (c + 1)],
                            dst[:, 128 * c:128 * (c + 1)],
                            env.ident.ap(), tag=f"{pref}_fT{tt}_{i}")
            P.copy(KM[:, i, :, :],
                   pst.ap()[:, 0:512].rearrange("p (c t) -> p c t", c=4),
                   tag=f"{pref}_fTe{tt}_{i}")
        ps = env.ps[4]
        P.mm(ps.ap()[:, 0:512], MC.ap()[:, 2, :], lF, tag=f"{pref}_fu{tt}")
        P.act(eTmp, ps.ap()[:, 0:512], AF.Exp, tag=f"{pref}_fue{tt}")
        P.tt(kuT, eTmp, kT, OP.mult, tag=f"{pref}_fuk{tt}")
        P.ts(kuT, kuT, bS[:, tt:tt + 1], None, OP.mult, tag=f"{pref}_fub{tt}")
        ps = env.ps[5]
        P.mm(ps.ap()[:, 0:512], MC.ap()[:, 4, :], lF, tag=f"{pref}_s5{tt}")
        P.copy(sgF, ps.ap()[:, 0:512], tag=f"{pref}_s5c{tt}")
        for c in range(4):
            pst = env.psT[c % 2]
            psf = pst.ap().bitcast(F32)
            P.transpose(psf[:, 0:128], sgF[:, 128 * c:128 * (c + 1)],
                        env.identf.ap(), tag=f"{pref}_s5T{tt}_{c}")
            P.copy(EXS.ap()[:, c, :], psf[:, 0:128].rearrange(
                "p (a b) -> p a b", a=4)[:, :, 0:1].rearrange(
                "p a o -> p (a o)"), tag=f"{pref}_s5e{tt}_{c}")
        P.act(EXB.ap().rearrange("p a b -> p (a b)"),
              EXS.ap().rearrange("p a b -> p (a b)"), AF.Exp,
              tag=f"{pref}_s5x{tt}")
        P.tt(eTmp, kT, vT, OP.mult, tag=f"{pref}_tf{tt}")
        P.tt(eTmp, eTmp, tfb, OP.mult, tag=f"{pref}_tf2{tt}")
        pst = env.psT[0]
        for c in range(4):
            P.transpose(pst.ap()[:, 128 * c:128 * (c + 1)],
                        eTmp[:, 128 * c:128 * (c + 1)],
                        env.ident.ap(), tag=f"{pref}_tfT{tt}")
        P.copy(TFT, pst.ap()[:, 0:512].rearrange("p (c t) -> p c t", c=4),
               tag=f"{pref}_tfe{tt}")
        for h in range(8):
            cc = h // 2
            pb = 64 * (h % 2)
            ps = env.ps[4 + h % 2]
            P.mm(ps.ap()[:, 0:128], KM[pb:pb + 64, 2, cc, :],
                 KM[pb:pb + 64, 0, cc, :], start=True, stop=False,
                 tag=f"{pref}_sa{tt}_{h}")
            P.mm(ps.ap()[:, 0:128], KM[pb:pb + 64, 3, cc, :],
                 KM[pb:pb + 64, 1, cc, :], start=False, stop=False,
                 tag=f"{pref}_sb{tt}_{h}")
            P.mm(ps.ap()[:, 0:128], KM[pb:pb + 64, 4, cc, :],
                 KM[pb:pb + 64, 1, cc, :], start=False, stop=True,
                 tag=f"{pref}_sc{tt}_{h}")
            P.copy(SC[:, h, :], ps.ap()[:, 0:128], tag=f"{pref}_se{tt}_{h}")
            P.affine_select(SC[:, h, :], SC[:, h, :], [[1, 128]],
                            OP.is_ge, 0.0, base=0, channel_multiplier=-1,
                            tag=f"{pref}_sm{tt}_{h}")
        for b in range(4):
            sb = slice(32 * b, 32 * (b + 1))
            for h in range(8):
                p_ = h // 2
                h2 = h % 2
                pb = 64 * h2
                psy = env.ps[p_]
                P.mm(psy.ap()[pb:pb + 64, 32 * b:32 * (b + 1)],
                     vT.rearrange("p (h k) -> p h k", h=8)[sb, h, :],
                     SC[:, h, :][32 * b:32 * (b + 1), 32 * b:32 * (b + 1)],
                     start=True, stop=False, tp=(32 * b, pb),
                     tag=f"{pref}_yi{tt}_{b}_{h}")
            psu = env.ps[4 + b % 2]
            for h in range(8):
                h2 = h % 2
                p_ = h // 2
                P.mm(psu.ap()[64 * h2:64 * h2 + 64, 64 * p_:64 * p_ + 64],
                     kuT.rearrange("p (h k) -> p h k", h=8)[sb, h, :],
                     vT.rearrange("p (h k) -> p h k", h=8)[sb, h, :],
                     start=True, stop=True, tp=(32 * b, 64 * h2),
                     tag=f"{pref}_u{tt}_{b}_{h}")
            for h in range(8):
                p_ = h // 2
                h2 = h % 2
                pb = 64 * h2
                psy = env.ps[p_]
                P.mm(psy.ap()[pb:pb + 64, 32 * b:32 * (b + 1)],
                     WS[pb:pb + 64, p_, :],
                     KM[pb:pb + 64, 5, h // 2, 32 * b:32 * (b + 1)],
                     start=False, stop=True, tp=(pb, pb),
                     tag=f"{pref}_yw{tt}_{b}_{h}")
            P.tt(WT, WS, EXB.ap()[:, :, b:b + 1].broadcast_to((128, 4, 64)),
                 OP.mult, tag=f"{pref}_wd{tt}_{b}")
            P.tt(WS, WT, psu.ap()[:, 0:256].rearrange("p (a k) -> p a k", a=4),
                 OP.add, tag=f"{pref}_wu{tt}_{b}")
        for p_ in range(4):
            P.tt(YG[:, p_, :], env.ps[p_].ap()[:, 0:128], TFT[:, p_, :],
                 OP.add, tag=f"{pref}_ya{tt}_{p_}")
            P.tt(YG[:, p_, :], YG[:, p_, :], rgT[:, p_, :], OP.mult,
                 tag=f"{pref}_yg{tt}_{p_}")
        for oh in range(2):
            ps = env.ps[4 + oh]
            wo = WoA if oh == 0 else WoB2
            for p_ in range(4):
                P.mm(ps.ap(), YG[:, p_, :], wo[:, p_, :],
                     start=(p_ == 0), stop=(p_ == 3), tag=f"{pref}_wo{tt}")
            xs = env.X[tt].ap()[:, 512 * oh:512 * (oh + 1)]
            P.tt(xs, xs, ps.ap(), OP.add, tag=f"{pref}_xo{tt}")


def _register4():
    EMITTERS["l0_attn_a"] = lambda P, env, seq: emit_l0attna(P, env, seq,
                                                            f"R{seq}")


_register4()


def rwkv_consts():
    M = np.zeros((5, 128, 128), np.float32)
    u = np.arange(128)
    bu = u // 16
    sbu = u // 32
    U, T_ = np.meshgrid(u, u, indexing="ij")
    sameb = (bu[:, None] == bu[None, :])
    samesb = (sbu[:, None] == sbu[None, :])
    M[0] = sameb * ((U <= T_) - 0.5)
    even_s = (bu[None, :] % 2 == 0)
    M[1] = (sameb * (U > T_) + 0.5 * ((bu[:, None] == bu[None, :] + 1)
                                      & samesb)) * even_s
    M[2] = samesb * (U > T_)
    M[3] = samesb * (U <= T_)
    M[4] = samesb * 1.0
    eo = np.zeros((128, 2), np.float32)
    eo[:, 0] = (bu % 2 == 0)
    eo[:, 1] = (bu % 2 == 1)
    return M, eo


def emit_l1attnb(P, env, seq, pref):
    emit_ln(P, env, tag=f"{pref}_ln")
    T1 = env.T1.ap()
    xsp = env.dram_tmp("xspill", [NTT, 128, D])
    for tt in range(NTT):
        P.dma("sync", xsp[tt], env.X[tt].ap(), tag=f"{pref}_sp{tt}")
    M6 = env.sb.get("dbM6")
    if M6 is None:
        M6 = env.sbuf("dbM6", [128, 128], F32)
        P.dma("sync", M6.ap(), env.dram_in("delta_m6", [128, 128]),
              tag="dbM6dma")
    lamc = env.sbuf("dbLam", [128, 4], F32)
    FLB = env.sbuf("dbFlb", [128, 16], F32)
    RTB = env.sbuf("dbRt", [128, 4], F32)
    NWB = env.AFG.ap().rearrange("p a c t -> p (a c t)")[:, 0:1024]
    betaS = env.sbuf("dbBeta", [128, NTT, 4], F32)
    statT = env.AFP.ap()[0:64, 0, :]
    LACC = env.LACC
    PCO = env.PCO
    W1S = env.sbuf("dbW1s", [64, 128], BF16)
    W2S = env.sbuf("dbW2s", [128, 16], BF16)
    row = env.LND.ap()
    P.memset(row[:, 0:36], 0.0, tag=f"{pref}_rz")
    P.dma("sync", row[:, 0:4], env.dram_in("L1_attn_b_forget_param", [4])
          .rearrange("(o a) -> o a", o=1), tag=f"{pref}_fg")
    P.act(row[:, 4:8], row[:, 0:4], AF.Sigmoid, tag=f"{pref}_fgs")
    P.ts(row[:, 4:8], row[:, 4:8], 0.45, 0.55, OP.mult, OP.add,
         tag=f"{pref}_fgl")
    P.dma("sync", row[:, 16:32], env.dram_in("L1_attn_b_floor_param", [4, 4])
          .rearrange("(o a) b -> o (a b)", o=1), tag=f"{pref}_fl")
    P.dma("sync", row[:, 32:36], env.dram_in("L1_attn_b_log_temp", [4])
          .rearrange("(o a) -> o a", o=1), tag=f"{pref}_lt")
    ps = env.ps[5]
    P.mm(ps.ap()[:, 0:36], env.ones_row_f.ap(), row[:, 0:36],
         tag=f"{pref}_cb")
    P.copy(lamc.ap(), ps.ap()[:, 4:8], tag=f"{pref}_lc")
    P.act(FLB.ap(), ps.ap()[:, 16:32], AF.Sigmoid, tag=f"{pref}_fls")
    P.ts(FLB.ap(), FLB.ap(), 0.075, None, OP.mult, tag=f"{pref}_flm")
    P.act(RTB.ap(), ps.ap()[:, 32:36], AF.Exp, scale=-1.0, tag=f"{pref}_rt")
    P.dma("gpsimd", env.BROW.ap(), env.dram_in("L1_attn_b_norm_w4", [1024])
          .rearrange("(o a) -> o a", o=1), tag=f"{pref}_nw")
    bcast_row(P, env, NWB, env.BROW.ap(), 1024, tag=f"{pref}_nwb")

    w3f = env.W3.ap().rearrange("p c o -> p (c o)")
    WB = w3f[:, 0:8192].rearrange("p (c o) -> p c o", c=NC)
    Wb4 = env.sbuf("dbWb", [128, NC, 4], BF16)
    vfm = [env.X[8 + j].ap().bitcast(BF16) for j in range(NC)]
    qkd = env.dram_tmp("qkproj", [2, NTT, 128, D], dtype=BF16)
    ext = env.QK.ap().rearrange("p a b -> p (a b)")
    CVD = env.T2.ap().rearrange("p a b -> p (a b)")[:, 0:4096].rearrange(
        "p (z o) -> p z o", o=128)
    cvdd = env.dram_in("conv_diag", [3, NC, 4, 128, 128])
    P.dma("gpsimd", Wb4.ap(), env.dram_in("L1_attn_b_Wb", [D, 4]).rearrange(
        "(c p) o -> p c o", p=128), tag=f"{pref}_Wb")
    for ki, (wname, kind) in enumerate((("L1_attn_b_Wv", "v"),
                                        ("L1_attn_b_Wq", "q"),
                                        ("L1_attn_b_Wk", "k"))):
        P.dma("gpsimd", WB, env.dram_in(wname, [D, D]).rearrange(
            "(c p) o -> p c o", p=128), tag=f"{pref}_{wname}")
        P.dma("gpsimd", CVD,
              cvdd[ki].rearrange("c a p o -> p (c a) o"),
              tag=f"{pref}_cvd{kind}")
        for tt in range(NTT):
            w = 131 if tt > 0 else 128

            def proj(oc, tt=tt, kind=kind, w=w):
                ps = env.ps[oc % 2]
                for c in range(NC):
                    P.mm(ps.ap()[:, 0:w],
                         WB[:, c, 128 * oc:128 * (oc + 1)],
                         T1[:, c, max(0, 128 * tt - 3):128 * (tt + 1)],
                         start=(c == 0), stop=(c == NC - 1),
                         tag=f"{pref}_{kind}p{tt}")

            def convp(oc, tt=tt, kind=kind):
                ps = env.ps[oc % 2]
                et = ext[:, 512 * (oc % 4):512 * (oc % 4) + 131]
                if tt == 0:
                    P.memset(et[:, 0:3], 0.0, tag=f"{pref}_{kind}z{tt}")
                    P.copy(et[:, 3:131], ps.ap()[:, 0:128], eng="scalar",
                           tag=f"{pref}_{kind}e{tt}")
                else:
                    P.copy(et[:, 0:131], ps.ap()[:, 0:131], eng="scalar",
                           tag=f"{pref}_{kind}e{tt}")
                ps2 = env.ps[2 + oc % 2]
                for tau in range(4):
                    P.mm(ps2.ap()[:, 0:128], CVD[:, 4 * oc + tau, :],
                         et[:, tau:tau + 128], start=(tau == 0),
                         stop=(tau == 3), tag=f"{pref}_{kind}c{tau}_{tt}")
                dst = (vfm[oc][:, 128 * tt:128 * (tt + 1)] if kind == "v"
                       else ext[:, 2048 + 128 * oc:2048 + 128 * (oc + 1)])
                P.act(dst, ps2.ap()[:, 0:128], AF.Silu,
                      tag=f"{pref}_{kind}s{tt}")
            proj(0)
            for oc in range(NC):
                if oc + 1 < NC:
                    proj(oc + 1)
                convp(oc)
            if kind == "v":
                psb = env.ps[4]
                for c in range(NC):
                    P.mm(psb.ap()[:, 0:4], T1[:, c, 128 * tt:128 * (tt + 1)],
                         Wb4.ap()[:, c, :], start=(c == 0),
                         stop=(c == NC - 1), tag=f"{pref}_b{tt}")
                P.act(betaS.ap()[:, tt, :], psb.ap()[:, 0:4], AF.Sigmoid,
                      tag=f"{pref}_bs{tt}")
            else:
                qi = 0 if kind == "q" else 1
                P.dma("sync", qkd[qi, tt], ext[:, 2048:3072],
                      tag=f"{pref}_{kind}o{tt}")

    lf = env.lnf32.ap()
    Lm = lf[:, 0:128]
    Tm = lf[:, 128:256]
    Tt = lf[:, 256:384]
    Mm = lf[:, 384:512]
    Mt = lf[:, 512:640]
    Sv = [env.X[0].ap()[:, 0:512], env.X[0].ap()[:, 512:1024],
          env.X[1].ap()[:, 0:512], env.X[1].ap()[:, 512:1024]]
    uT = env.X[2].ap()
    x4b = env.X[4].ap().bitcast(BF16)
    qF = x4b[:, 0:1024].rearrange("p (h c t) -> p h c t", h=4, c=2)
    kF = x4b[:, 1024:2048].rearrange("p (h c t) -> p h c t", h=4, c=2)
    x3b = env.X[3].ap().bitcast(BF16)
    wF = x3b[:, 0:1024].rearrange("p (h c t) -> p h c t", h=4, c=2)
    x5b = env.X[5].ap().bitcast(BF16)
    kTm = x5b[:, 0:1024]
    vTm = x5b[:, 1024:2048]
    x6b = env.X[6].ap().bitcast(BF16)
    kbTm = x6b[:, 0:1024]
    uSn = x6b[:, 1024:2048]
    x7b = env.X[7].ap().bitcast(BF16)
    Ssn = [x7b[:, 0:1024], x7b[:, 1024:2048]]
    atl = env.xlnr.ap().rearrange("p a b -> p (a b)")
    wtmS = atl[:, 512:768]
    P.memset(env.X[0].ap(), 0.0, tag=f"{pref}_S0")
    P.memset(env.X[1].ap(), 0.0, tag=f"{pref}_S1")
    prv = env.PR.ap().rearrange("p a b -> p (a b)")
    DTM = env.T2.ap().rearrange("p a b -> p (a b)")[:, 0:16384].rearrange(
        "p (t d) -> p t d", d=1024)
    for tt in range(NTT):
        P.dma("sync", ext[:, 0:1024], qkd[0, tt], tag=f"{pref}_qr{tt}")
        P.dma("sync", ext[:, 1024:2048], qkd[1, tt], tag=f"{pref}_kr{tt}")
        for src_off, isk in ((0, False), (1024, True)):
            pst = env.psT[0]
            for c in range(NC):
                P.transpose(pst.ap()[:, 128 * c:128 * (c + 1)],
                            ext[:, src_off + 128 * c:src_off + 128 * (c + 1)],
                            env.ident.ap(), tag=f"{pref}_nT{tt}")
            tmv = prv[:, 0:1024]
            P.copy(tmv, pst.ap()[:, 0:1024], tag=f"{pref}_nTe{tt}")
            sq = ext[:, 2048:3072]
            P.act(sq, tmv, AF.Square, tag=f"{pref}_nsq{tt}")
            ssq = env.stat.ap()[:, 32:36]
            P.reduce(ssq, sq.rearrange("p (h d) -> p h d", h=4), OP.add,
                     tag=f"{pref}_nss{tt}")
            P.act(ssq, ssq, AF.Ln, bias=env.constf(1e-6), tag=f"{pref}_nl{tt}")
            P.act(ssq, ssq, AF.Exp, scale=-0.5, tag=f"{pref}_ne{tt}")
            P.tt(tmv.rearrange("p (h d) -> p h d", h=4),
                 tmv.rearrange("p (h d) -> p h d", h=4),
                 ssq[:, :, None].broadcast_to((128, 4, 256)), OP.mult,
                 tag=f"{pref}_nm{tt}")
            if isk:
                P.copy(kTm, tmv, tag=f"{pref}_ktm{tt}")
                P.tt(kbTm.rearrange("p (h d) -> p h d", h=4),
                     tmv.rearrange("p (h d) -> p h d", h=4),
                     betaS.ap()[:, tt, :][:, :, None].broadcast_to(
                         (128, 4, 256)), OP.mult, tag=f"{pref}_kbm{tt}")
            pst = env.psT[1]
            for c in range(NC):
                P.transpose(pst.ap()[:, 128 * c:128 * (c + 1)],
                            tmv[:, 128 * c:128 * (c + 1)],
                            env.ident.ap(), tag=f"{pref}_nB{tt}")
            dstF = kF if isk else qF
            P.copy(dstF.rearrange("p h c t -> p (h c t)"),
                   pst.ap()[:, 0:1024], tag=f"{pref}_nBe{tt}")
        pst = env.psT[0]
        for c in range(NC):
            P.transpose(pst.ap()[:, 128 * c:128 * (c + 1)],
                        vfm[c][:, 128 * tt:128 * (tt + 1)],
                        env.ident.ap(), tag=f"{pref}_vT{tt}")
        P.copy(vTm, pst.ap()[:, 0:1024], tag=f"{pref}_vTe{tt}")
        P.tt(vTm.rearrange("p (h d) -> p h d", h=4),
             vTm.rearrange("p (h d) -> p h d", h=4),
             betaS.ap()[:, tt, :][:, :, None].broadcast_to((128, 4, 256)),
             OP.mult, tag=f"{pref}_vbm{tt}")
        for h in range(4):
            psa = env.ps[h % 2]
            for c in range(2):
                P.mm(psa.ap()[:, 0:128], kF[:, h, c, :], kF[:, h, c, :],
                     start=(c == 0), stop=(c == 1), tag=f"{pref}_kk{tt}_{h}")
            P.ts(Lm, psa.ap()[:, 0:128], betaS.ap()[:, tt, h:h + 1], -1.0,
                 OP.mult, OP.mult, tag=f"{pref}_L{tt}_{h}")
            P.tt(Lm, Lm, M6.ap(), OP.mult, tag=f"{pref}_Lm{tt}_{h}")
            P.tt(Tm, Lm, env.identf.ap(), OP.add, tag=f"{pref}_T0{tt}_{h}")
            psf = env.psT[0].ap().bitcast(F32)
            P.transpose(psf[:, 0:128], Tm, env.identf.ap(),
                        tag=f"{pref}_Tt{tt}_{h}")
            P.copy(Tt, psf[:, 0:128], tag=f"{pref}_Tte{tt}_{h}")
            P.transpose(psf[:, 128:256], Lm, env.identf.ap(),
                        tag=f"{pref}_Lt{tt}_{h}")
            P.copy(Mt, psf[:, 128:256], tag=f"{pref}_Lte{tt}_{h}")
            psb = env.ps[2 + h % 2]
            P.mm(psb.ap()[:, 0:128], Mt, Lm, tag=f"{pref}_M2{tt}_{h}")
            P.copy(Mm, psb.ap()[:, 0:128], tag=f"{pref}_M2e{tt}_{h}")
            for it in range(4):
                P.transpose(psf[:, 128:256], Mm, env.identf.ap(),
                            tag=f"{pref}_Mt{tt}_{h}_{it}")
                P.copy(Mt, psf[:, 128:256], tag=f"{pref}_Mte{tt}_{h}_{it}")
                P.mm(psb.ap()[:, 0:128], Tt, Mm, tag=f"{pref}_TM{tt}_{h}_{it}")
                P.tt(Tm, Tm, psb.ap()[:, 0:128], OP.add,
                     tag=f"{pref}_Ta{tt}_{h}_{it}")
                P.transpose(psf[:, 0:128], Tm, env.identf.ap(),
                            tag=f"{pref}_Tt2{tt}_{h}_{it}")
                P.copy(Tt, psf[:, 0:128], tag=f"{pref}_Tte2{tt}_{h}_{it}")
                if it < 3:
                    P.mm(psb.ap()[:, 0:128], Mt, Mm, tag=f"{pref}_MM{tt}_{h}")
                    P.copy(Mm, psb.ap()[:, 0:128], tag=f"{pref}_MMe{tt}_{h}")
            psu = env.ps[4]
            P.mm(psu.ap()[:, 0:256], Tt.bitcast(BF16)[:, 0:256], None,
                 tag="") if False else None
            TtB = atl[:, 768:896]
            P.copy(TtB, Tt, tag=f"{pref}_TtB{tt}_{h}")
            P.mm(psu.ap()[:, 0:256], TtB, vTm[:, 256 * h:256 * (h + 1)],
                 tag=f"{pref}_u{tt}_{h}")
            P.copy(uT[:, 256 * h:256 * (h + 1)], psu.ap()[:, 0:256],
                   tag=f"{pref}_ue{tt}_{h}")
            psw = env.ps[5]
            P.mm(psw.ap()[:, 0:256], TtB, kbTm[:, 256 * h:256 * (h + 1)],
                 tag=f"{pref}_w{tt}_{h}")
            P.copy(wtmS, psw.ap()[:, 0:256], tag=f"{pref}_we{tt}_{h}")
            pst = env.psT[1]
            for c in range(2):
                P.transpose(pst.ap()[:, 128 * c:128 * (c + 1)],
                            wtmS[:, 128 * c:128 * (c + 1)],
                            env.ident.ap(), tag=f"{pref}_wT{tt}_{h}")
            P.copy(wF[:, h, :, :].rearrange("p c t -> p (c t)"),
                   pst.ap()[:, 0:256], tag=f"{pref}_wTe{tt}_{h}")
            psat = env.ps[h % 2]
            for c in range(2):
                P.mm(psat.ap()[:, 0:128], kF[:, h, c, :], qF[:, h, c, :],
                     start=(c == 0), stop=(c == 1), tag=f"{pref}_at{tt}_{h}")
            ath = atl[:, 128 * h:128 * (h + 1)]
            P.copy(ath, psat.ap()[:, 0:128], tag=f"{pref}_ate{tt}_{h}")
            P.affine_select(ath, ath, [[1, 128]], OP.is_ge, 0.0, base=0,
                            channel_multiplier=-1, tag=f"{pref}_atm{tt}_{h}")
        for i in range(4):
            rows = slice(32 * i, 32 * (i + 1))
            for c in range(2):
                P.copy(Ssn[c].rearrange("p (h d) -> p h d", h=4),
                       bass.AP.__new__(bass.AP) if False else
                       None, tag="") if False else None
            for h in range(4):
                for c in range(2):
                    P.copy(Ssn[c][:, 256 * h:256 * (h + 1)],
                           Sv[h][:, 256 * c:256 * (c + 1)],
                           tag=f"{pref}_ss{tt}_{i}_{h}")
            for h in range(4):
                psu = env.ps[h % 2]
                for c in range(2):
                    P.mm(psu.ap()[32 * i:32 * (i + 1), 0:256],
                         wF[:, h, c, 32 * i:32 * (i + 1)], Ssn[c][:, 256 * h:256 * (h + 1)],
                         start=(c == 0), stop=(c == 1), tp=(0, 32 * i),
                         tag=f"{pref}_up{tt}_{i}_{h}")
                usl = uT[rows, 256 * h:256 * (h + 1)]
                P.tt(usl, usl, psu.ap()[32 * i:32 * (i + 1), 0:256],
                     OP.subtract, tag=f"{pref}_ui{tt}_{i}_{h}")
                P.copy(uSn[rows, 256 * h:256 * (h + 1)], usl,
                       tag=f"{pref}_us{tt}_{i}_{h}")
                pso = env.ps[2 + h % 2]
                for c in range(2):
                    P.mm(pso.ap()[32 * i:32 * (i + 1), 0:256],
                         qF[:, h, c, 32 * i:32 * (i + 1)],
                         Ssn[c][:, 256 * h:256 * (h + 1)],
                         start=(c == 0), stop=False, tp=(0, 32 * i),
                         tag=f"{pref}_oq{tt}_{i}_{h}")
                P.mm(pso.ap()[32 * i:32 * (i + 1), 0:256],
                     atl[rows, 128 * h + 32 * i:128 * h + 32 * (i + 1)],
                     uSn[rows, 256 * h:256 * (h + 1)],
                     start=False, stop=True, tp=(32 * i, 32 * i),
                     tag=f"{pref}_oa{tt}_{i}_{h}")
                P.copy(DTM[rows, tt, 256 * h:256 * (h + 1)],
                       pso.ap()[32 * i:32 * (i + 1), 0:256],
                       tag=f"{pref}_oe{tt}_{i}_{h}")
                pss = env.ps[4 + h % 2]
                for c in range(2):
                    P.mm(pss.ap()[:, 0:256],
                         kTm[rows, 256 * h + 128 * c:256 * h + 128 * (c + 1)],
                         uSn[rows, 256 * h:256 * (h + 1)],
                         start=True, stop=True, tp=(32 * i, 0),
                         tag=f"{pref}_su{tt}_{i}_{h}_{c}")
                    P.stt(Sv[h][:, 256 * c:256 * (c + 1)],
                          Sv[h][:, 256 * c:256 * (c + 1)],
                          lamc.ap()[:, h:h + 1], pss.ap()[:, 0:256],
                          OP.mult, OP.add, tag=f"{pref}_sup{tt}_{i}_{h}_{c}")

    sh_w = env.sbuf("dbFs", [128, NC, 3], F32)
    P.dma("sync", sh_w.ap(), env.dram_in("L1_attn_b_fir_short_f", [D, 3])
          .rearrange("(c p) o -> p c o", p=128), tag=f"{pref}_fsw")
    ldram = env.dram_tmp("longtm", [NTT, 128, 1024], dtype=BF16)
    DTM2 = env.T2.ap().rearrange("p a b -> p (a b)")[:, 0:16384].rearrange(
        "p (t d) -> p t d", d=1024)
    shTM = [env.X[j].ap().bitcast(BF16) for j in range(8)]

    def sh_tile(tt):
        return shTM[tt // 2][:, 1024 * (tt % 2):1024 * (tt % 2) + 1024]
    SH = env.GB.ap().bitcast(F32)
    sqs = env.xlnr.ap().rearrange("p a b -> p (a b)")[:, 0:1024]
    stt_tm = env.sbuf("dbStm", [128, 64], F32)
    DG = env.W2.ap().rearrange("p c a b -> p (c a b)").rearrange(
        "p (u t o) -> p u t o", u=2, t=8)
    ZB = env.sbuf("dbZB", [128, 128], BF16)
    P.memset(ZB.ap(), 0.0, tag=f"{pref}_zb")
    dgd = env.dram_in("fir_long_diag", [NC, 63, 128, 128])
    extb = env.QK.ap().bitcast(BF16).rearrange("p a b -> p (a b)")
    lgfm = extb[:, 0:2048]
    strips = prv[:, 1024:1536]

    def fir(acc, wcol, taps, oc, tb):
        base = 512 * tb
        for tau in range(taps):
            sh = taps - 1 - tau
            lo = max(0, sh - base)
            src = vfm[oc][:, base - sh + lo:base - sh + 512]
            if tau == 0:
                P.memset(acc[:, 0:512], 0.0, tag=f"{pref}_fz{oc}_{tb}")
            P.stt(acc[:, lo:512], src, wcol[:, oc, tau:tau + 1],
                  acc[:, lo:512], OP.mult, OP.add, tag=f"{pref}_ft{oc}_{tb}")

    def dostats(P, env, src_tm, ts_idx, tt):
        P.act(sqs, src_tm, AF.Square, tag=f"{pref}_sq{ts_idx}_{tt}")
        g = src_tm.rearrange("p (h d) -> p h d", h=4)
        gq = sqs.rearrange("p (h d) -> p h d", h=4)
        st = env.stat.ap()
        msum = st[:, 40:44]
        mssq = st[:, 44:48]
        mmax = st[:, 48:52]
        P.reduce(msum, g, OP.add, tag=f"{pref}_st1{ts_idx}_{tt}")
        P.reduce(mssq, gq, OP.add, tag=f"{pref}_st2{ts_idx}_{tt}")
        P.reduce(mmax, g, OP.max, tag=f"{pref}_st3{ts_idx}_{tt}")
        base = 16 * ts_idx
        o = stt_tm.ap().rearrange("p (ts h st) -> p ts h st", ts=4, h=4)
        P.ts(o[:, ts_idx, :, 0], msum, 1.0 / 256, None, OP.mult,
             tag=f"{pref}_sm{ts_idx}_{tt}")
        P.ts(st[:, 52:56], mssq, 1.0 / 256, None, OP.mult,
             tag=f"{pref}_sv1{ts_idx}_{tt}")
        P.tt(st[:, 56:60], o[:, ts_idx, :, 0], o[:, ts_idx, :, 0], OP.mult,
             tag=f"{pref}_sv2{ts_idx}_{tt}")
        P.tt(o[:, ts_idx, :, 1], st[:, 52:56], st[:, 56:60], OP.subtract,
             tag=f"{pref}_sv3{ts_idx}_{tt}")
        P.copy(o[:, ts_idx, :, 2], mmax, tag=f"{pref}_sx{ts_idx}_{tt}")
        P.act(o[:, ts_idx, :, 3], mssq, AF.Ln, tag=f"{pref}_sl{ts_idx}_{tt}")
        P.act(o[:, ts_idx, :, 3], o[:, ts_idx, :, 3], AF.Exp, scale=0.5,
              tag=f"{pref}_se{ts_idx}_{tt}")

    groups = [(8 * i, min(8 * i + 8, 63)) for i in range(8)]
    for oc in range(NC):
        for gi, (g0, g1) in enumerate(groups):
            buf = DG[:, (oc * 8 + gi) % 2]
            P.dma("gpsimd", buf[:, 0:g1 - g0, :],
                  dgd[oc, g0:g1].rearrange("t p o -> p t o"),
                  tag=f"{pref}_dg{oc}_{gi}")
            for tb in range(4):
                for j, tau in enumerate(range(g0, g1)):
                    sh = 62 - tau
                    base = 512 * tb - sh
                    if tb == 0:
                        if tau == 0:
                            P.mm(env.ps[0].ap(), ZB.ap(),
                                 vfm[oc][:, 0:512], start=True, stop=False,
                                 tag=f"{pref}_fl{oc}_z")
                        lo = max(0, -base)
                        P.mm(env.ps[0].ap()[:, lo:512], buf[:, j, :],
                             vfm[oc][:, base + lo:base + 512],
                             start=False, stop=(tau == 62),
                             tag=f"{pref}_fl{oc}_{tb}")
                    else:
                        P.mm(env.ps[tb].ap(), buf[:, j, :],
                             vfm[oc][:, base:base + 512],
                             start=(tau == 0), stop=(tau == 62),
                             tag=f"{pref}_fl{oc}_{tb}")
        for tb in range(4):
            P.copy(lgfm[:, 512 * tb:512 * (tb + 1)], env.ps[tb].ap(),
                   eng="scalar", tag=f"{pref}_fle{oc}_{tb}")
            fir(SH, sh_w.ap(), 3, oc, tb)
            shb = sqs[:, 0:512]
            P.copy(shb, SH[:, 0:512], tag=f"{pref}_fse{oc}_{tb}")
            for j in range(4):
                pst = env.psT[1]
                P.transpose(pst.ap()[:, 0:128], shb[:, 128 * j:128 * (j + 1)],
                            env.ident.ap(), tag=f"{pref}_fTs{oc}_{tb}")
                P.copy(sh_tile(4 * tb + j)[:, 128 * oc:128 * (oc + 1)],
                       pst.ap()[:, 0:128], tag=f"{pref}_fTse{oc}_{tb}")
        for tt in range(NTT):
            pst = env.psT[0]
            P.transpose(pst.ap()[:, 128 * (tt % 4):128 * (tt % 4) + 128],
                        lgfm[:, 128 * tt:128 * (tt + 1)],
                        env.ident.ap(), tag=f"{pref}_fTl{oc}_{tt}")
            stp = strips[:, 128 * (tt % 4):128 * (tt % 4) + 128]
            P.copy(stp, pst.ap()[:, 128 * (tt % 4):128 * (tt % 4) + 128],
                   tag=f"{pref}_fTle{oc}_{tt}")
            P.dma("scalar", ldram[tt][:, 128 * oc:128 * (oc + 1)], stp,
                  tag=f"{pref}_ldo{oc}_{tt}")
    for tt in range(NTT):
        lrb = extb[:, 2048 + 1024 * (tt % 2):3072 + 1024 * (tt % 2)]
        P.dma("sync", lrb, ldram[tt], tag=f"{pref}_lst{tt}")
        dostats(P, env, sh_tile(tt), 0, tt)
        dostats(P, env, lrb, 1, tt)
        dostats(P, env, DTM2[:, tt, :], 2, tt)
        pst = env.psT[0]
        for c in range(NC):
            P.transpose(pst.ap()[:, 128 * c:128 * (c + 1)],
                        vfm[c][:, 128 * tt:128 * (tt + 1)],
                        env.ident.ap(), tag=f"{pref}_vT2{tt}")
        vtm2 = prv[:, 0:1024]
        P.copy(vtm2, pst.ap()[:, 0:1024], tag=f"{pref}_vT2e{tt}")
        dostats(P, env, vtm2, 3, tt)
        stb = prv[:, 1024:1088]
        P.copy(stb, stt_tm.ap(), tag=f"{pref}_stb{tt}")
        pst = env.psT[1]
        P.transpose(pst.ap()[0:64, 0:128], stb, env.ident.ap(),
                    tag=f"{pref}_stT{tt}")
        P.copy(statT[:, 128 * tt:128 * (tt + 1)],
               pst.ap()[0:64, 0:128], tag=f"{pref}_stTe{tt}")

    w1d = env.dram_in("L1_attn_b_W1", [1088, 2048])
    w2d = env.dram_in("L1_attn_b_W2", [2048, 16])
    b1d = env.dram_in("L1_attn_b_b1", [2048])
    b2d = env.dram_in("L1_attn_b_b2", [16])
    W1Ms = [w3f[:, 0:1024].rearrange("p (c o) -> p c o", c=NC),
            w3f[:, 1024:2048].rearrange("p (c o) -> p c o", c=NC)]
    P.dma("gpsimd", env.BROW.ap()[:, 128:144],
          b2d.rearrange("(o a) -> o a", o=1), tag=f"{pref}_b2")
    w1sb = env.W1.ap().rearrange("p c a b -> p (c a b)")
    xlf = env.xlnr.ap().rearrange("p a b -> p (a b)")
    w2bufs = [W2S.ap(), xlf[:, 1536:1552]]
    ones512 = env.RDB.ap()[0:1, :]
    P.memset(ones512, 1.0, tag=f"{pref}_o512")
    for hs in range(16):
        sl = slice(128 * hs, 128 * (hs + 1))
        W1M = W1Ms[hs % 2]
        P.dma("gpsimd", W1M, w1d[0:1024, sl].rearrange(
            "(c p) o -> p c o", p=128), tag=f"{pref}_w1{hs}")
        w1s = w1sb[0:64, 1024 * (hs % 2):1024 * (hs % 2) + 128]
        P.dma("gpsimd", w1s, w1d[1024:1088, sl], tag=f"{pref}_w1s{hs}")
        w2s = w2bufs[hs % 2]
        P.dma("gpsimd", w2s, w2d[sl, :], tag=f"{pref}_w2{hs}")
        b1row = env.BROW.ap()[:, 256 + 128 * (hs % 2):384 + 128 * (hs % 2)]
        P.dma("gpsimd", b1row, b1d[sl].rearrange("(o a) -> o a", o=1),
              tag=f"{pref}_b1{hs}")
        for tb in range(4):
            ps = env.ps[tb % 2]
            for c in range(NC):
                P.mm(ps.ap(), W1M[:, c, :],
                     T1[:, c, 512 * tb:512 * (tb + 1)],
                     start=(c == 0), stop=False, tag=f"{pref}_h1{hs}_{tb}")
            P.mm(ps.ap(), w1s, statT[:, 512 * tb:512 * (tb + 1)],
                 start=False, stop=False, tag=f"{pref}_h2{hs}_{tb}")
            P.mm(ps.ap(), b1row, ones512,
                 start=False, stop=True, tag=f"{pref}_h3{hs}_{tb}")
            hm = xlf[:, 512 * (tb % 2):512 * (tb % 2) + 512]
            P.act(hm, ps.ap(), AF.Gelu, tag=f"{pref}_ge{hs}_{tb}")
            for q4 in range(4):
                tt = 4 * tb + q4
                ps2 = env.ps[2 + tt % 2]
                P.mm(ps2.ap()[:, 0:16], hm[:, 128 * q4:128 * (q4 + 1)],
                     w2s, start=True, stop=(hs > 0),
                     tag=f"{pref}_l2m{hs}_{tt}")
                if hs == 0:
                    P.mm(ps2.ap()[:, 0:16], env.ones_row.ap(),
                         env.BROW.ap()[:, 128:144], start=False, stop=True,
                         tag=f"{pref}_l2b{hs}_{tt}")
                if hs == 0:
                    P.copy(LACC.ap()[:, tt, :], ps2.ap()[:, 0:16],
                           tag=f"{pref}_la{hs}_{tt}")
                else:
                    P.tt(LACC.ap()[:, tt, :], LACC.ap()[:, tt, :],
                         ps2.ap()[:, 0:16], OP.add, tag=f"{pref}_la{hs}_{tt}")
    for tt in range(NTT):
        z = env.stat.ap()[:, 32:48].rearrange("p (h s) -> p h s", h=4)
        la = LACC.ap()[:, tt, :].rearrange("p (h s) -> p h s", h=4)
        P.tt(z, la, RTB.ap()[:, :, None].broadcast_to((128, 4, 4)), OP.mult,
             tag=f"{pref}_gz{tt}")
        mx = env.stat.ap()[:, 48:52]
        P.reduce(mx, z, OP.max, tag=f"{pref}_gm{tt}")
        P.tt(z, z, mx[:, :, None].broadcast_to((128, 4, 4)), OP.subtract,
             tag=f"{pref}_gs{tt}")
        P.act(z.rearrange("p h s -> p (h s)"), z.rearrange("p h s -> p (h s)"),
              AF.Exp, tag=f"{pref}_gx{tt}")
        sm = env.stat.ap()[:, 52:56]
        P.reduce(sm, z, OP.add, tag=f"{pref}_gsum{tt}")
        P.recip(sm, sm, tag=f"{pref}_gr{tt}")
        P.tt(z, z, sm[:, :, None].broadcast_to((128, 4, 4)), OP.mult,
             tag=f"{pref}_gp{tt}")
        P.tt(z.rearrange("p h s -> p (h s)"),
             z.rearrange("p h s -> p (h s)"), FLB.ap(), OP.max,
             tag=f"{pref}_gf{tt}")
        P.reduce(sm, z, OP.add, tag=f"{pref}_gs2{tt}")
        P.recip(sm, sm, tag=f"{pref}_gr2{tt}")
        P.tt(z, z, sm[:, :, None].broadcast_to((128, 4, 4)), OP.mult,
             tag=f"{pref}_gp2{tt}")
        P.copy(PCO.ap()[:, tt, :], z.rearrange("p h s -> p (h s)"),
               tag=f"{pref}_gpc{tt}")

    P.dma("gpsimd", WB, env.dram_in("L1_attn_b_Wo", [D, D]).rearrange(
        "(c p) o -> p c o", p=128), tag=f"{pref}_Wo")
    youtd = env.dram_tmp("youtd", [NTT, 128, D])
    DTM3 = env.T2.ap().rearrange("p a b -> p (a b)")[:, 0:16384].rearrange(
        "p (t d) -> p t d", d=1024)
    oacc = env.GB.ap().bitcast(F32)
    otmp = env.BB.ap().bitcast(F32)
    obf = env.xlnr.ap().rearrange("p a b -> p (a b)")
    stage = env.lnf32.ap()
    pend = None
    for tt in range(NTT):
        llt = ext[:, 1024 * (tt % 2):1024 * (tt % 2) + 1024]
        P.dma("sync", llt, ldram[tt], tag=f"{pref}_lri{tt}")
        pst = env.psT[0]
        for c in range(NC):
            P.transpose(pst.ap()[:, 128 * c:128 * (c + 1)],
                        vfm[c][:, 128 * tt:128 * (tt + 1)],
                        env.ident.ap(), tag=f"{pref}_vT3{tt}")
        vtm3 = prv[:, 0:1024]
        P.copy(vtm3, pst.ap()[:, 0:1024], tag=f"{pref}_vT3e{tt}")
        if pend is not None:
            pend()
        xr = env.XR[0].ap()
        P.dma("sync", xr, xsp[tt], tag=f"{pref}_rl{tt}")
        pc = PCO.ap()[:, tt, :].rearrange("p (h s) -> p h s", h=4)
        ob = obf[:, 0:1024]
        ssq = env.stat.ap()[:, 32:36]
        for hf in range(2):
            csl = slice(512 * hf, 512 * (hf + 1))
            hsl = slice(2 * hf, 2 * hf + 2)
            srcs = [(sh_tile(tt), 0), (llt, 1), (DTM3[:, tt, :], 2),
                    (vtm3, 3)]
            for src, si in srcs:
                g = src[:, csl].rearrange("p (h d) -> p h d", h=2)
                pcb = pc[:, hsl, si:si + 1].broadcast_to((128, 2, 256))
                if si == 0:
                    P.tt(oacc.rearrange("p (h d) -> p h d", h=2), g, pcb,
                         OP.mult, tag=f"{pref}_cb{si}_{tt}")
                else:
                    P.tt(otmp.rearrange("p (h d) -> p h d", h=2), g, pcb,
                         OP.mult, tag=f"{pref}_cm{si}_{tt}")
                    P.tt(oacc, oacc, otmp, OP.add, tag=f"{pref}_ca{si}_{tt}")
            P.act(otmp, oacc, AF.Square, tag=f"{pref}_rsq{tt}")
            P.reduce(ssq[:, 0:2], otmp.rearrange("p (h d) -> p h d", h=2),
                     OP.add, tag=f"{pref}_rss{tt}")
            P.ts(ssq[:, 0:2], ssq[:, 0:2], 1.0 / 256, None, OP.mult,
                 tag=f"{pref}_rsm{tt}")
            P.act(ssq[:, 0:2], ssq[:, 0:2], AF.Ln, bias=env.constf(LN_EPS),
                  tag=f"{pref}_rsl{tt}")
            P.act(ssq[:, 0:2], ssq[:, 0:2], AF.Exp, scale=-0.5,
                  tag=f"{pref}_rse{tt}")
            P.tt(oacc.rearrange("p (h d) -> p h d", h=2),
                 oacc.rearrange("p (h d) -> p h d", h=2),
                 ssq[:, 0:2][:, :, None].broadcast_to((128, 2, 256)), OP.mult,
                 tag=f"{pref}_rsn{tt}")
            P.tt(ob[:, csl], oacc, NWB[:, csl], OP.mult,
                 tag=f"{pref}_rnw{tt}")
        def pend(tt=tt, ob=ob, xr=xr):
            pst = env.psT[1]
            for c in range(NC):
                P.transpose(pst.ap()[:, 128 * c:128 * (c + 1)],
                            ob[:, 128 * c:128 * (c + 1)], env.ident.ap(),
                            tag=f"{pref}_oT{tt}")
            ofm = ext[:, 2048:3072]
            P.copy(ofm, pst.ap()[:, 0:1024], tag=f"{pref}_oTe{tt}")
            for oh in range(2):
                ps = env.ps[4 + oh]
                for c in range(NC):
                    P.mm(ps.ap(),
                         ofm.rearrange("p (c t) -> p c t", c=NC)[:, c, :],
                         WB[:, c, 512 * oh:512 * (oh + 1)],
                         start=(c == 0), stop=(c == NC - 1),
                         tag=f"{pref}_om{tt}")
                P.tt(stage[:, 512 * oh:512 * (oh + 1)], ps.ap(),
                     xr[:, 512 * oh:512 * (oh + 1)],
                     OP.add, tag=f"{pref}_oadd{tt}")
            P.dma("sync", youtd[tt], stage, tag=f"{pref}_yo{tt}")
    pend()
    for tt in range(NTT):
        P.dma("sync", env.X[tt].ap(), youtd[tt], tag=f"{pref}_xr{tt}")


def _register5():
    EMITTERS["l1_attn_b"] = lambda P, env, seq: emit_l1attnb(P, env, seq,
                                                             f"B{seq}")


_register5()


# version 25
# speedup vs baseline: 1.0185x; 1.0185x over previous
import math
import numpy as np

import concourse.bass as bass
import concourse.mybir as mybir
from concourse.bass_utils import run_bass_kernel_spmd

F32 = mybir.dt.float32
BF16 = mybir.dt.bfloat16
AF = mybir.ActivationFunctionType
OP = mybir.AluOpType
AX = mybir.AxisListType

ENGINES = ("tensor", "vector", "scalar", "gpsimd", "sync")
DMA_QUEUES = ("sync", "gpsimd", "scalar", "vector")


def _key_of(x):
    if isinstance(x, str):
        return (x, 0, 1 << 60)
    ap = x if isinstance(x, bass.AP) else x.ap()
    t = ap.tensor
    if "PSUM" in str(getattr(t, "space", "")):
        esz = 2 if ap.dtype == BF16 else 4
        lo = ap.offset
        span = 1
        for step, count in ap.ap:
            span += abs(step) * (count - 1)
        blo = (lo * esz // 2048) * 2048
        bhi = -((-(lo + span) * esz) // 2048) * 2048
        return (t.name, blo, bhi)
    try:
        esz = 2 if ap.dtype == BF16 else 4
    except Exception:
        esz = 4
    lo = ap.offset
    span = 1
    for step, count in ap.ap:
        span += abs(step) * (count - 1)
    return (t.name, lo * esz, (lo + span) * esz)


class _Op:
    __slots__ = ("eng", "fn", "reads", "writes", "deps", "idx", "inc_sem",
                 "inc_val", "is_dma", "dma_cum", "tag", "emitted", "prod_idx",
                 "wait_on", "needs_inc", "safe", "raw")

    def __init__(self, eng, fn, reads, writes, tag):
        self.eng = eng
        self.fn = fn
        self.reads = reads
        self.writes = writes
        self.deps = []
        self.idx = -1
        self.inc_sem = None
        self.inc_val = 0
        self.is_dma = False
        self.dma_cum = 0
        self.tag = tag
        self.prod_idx = 0
        self.wait_on = []
        self.needs_inc = False
        self.safe = False


class Prog:

    def __init__(self, nc):
        self.nc = nc
        self.ops = []
        self.eng_ops = {e: [] for e in ENGINES}
        self.last_write = {}
        self.readers = {}
        self.dma_count = {q: 0 for q in DMA_QUEUES}
        self._sems = {}
        self._finalized = False

    def op(self, eng, fn, reads=(), writes=(), tag=""):
        o = _Op(eng, fn, [_key_of(r) for r in reads],
                [_key_of(w) for w in writes], tag)
        deps = []
        raw = set()
        for name, lo, hi in o.reads:
            for wlo, whi, w in self.last_write.get(name, ()):
                if wlo < hi and lo < whi:
                    deps.append(w)
                    raw.add(id(w))
        for name, lo, hi in o.writes:
            for rlo, rhi, r in self.readers.get(name, ()):
                if rlo < hi and lo < rhi:
                    deps.append(r)
            for wlo, whi, w in self.last_write.get(name, ()):
                if wlo < hi and lo < whi:
                    deps.append(w)
        seen = set()
        o.deps = [d for d in deps
                  if d is not o and id(d) not in seen and not seen.add(id(d))]
        o.raw = raw
        o.idx = len(self.ops)
        self.ops.append(o)
        self.eng_ops[eng].append(o)
        for name, lo, hi in o.writes:
            wl = self.last_write.setdefault(name, [])
            wl[:] = [(a, b, w) for (a, b, w) in wl if not (lo <= a and b <= hi)]
            wl.append((lo, hi, o))
            rl = self.readers.setdefault(name, [])
            rl[:] = [(a, b, r) for (a, b, r) in rl if not (lo <= a and b <= hi)]
        for name, lo, hi in o.reads:
            self.readers.setdefault(name, []).append((lo, hi, o))
        return o

    def dma(self, queue, out_ap, in_ap, tag=""):
        def fn(eng):
            return eng.dma_start(out=out_ap, in_=in_ap)
        o = self.op(queue, fn, reads=[in_ap], writes=[out_ap], tag=tag)
        o.is_dma = True
        self.dma_count[queue] += 1
        o.dma_cum = self.dma_count[queue]
        return o

    def mm(self, out, lhsT, rhs, start=True, stop=True, tp=None, tag=""):
        def fn(eng):
            return eng.matmul(out, lhsT, rhs, start=start, stop=stop,
                              tile_position=tp)
        return self.op("tensor", fn, reads=[lhsT, rhs], writes=[out], tag=tag)

    def transpose(self, out, in_, ident, tag=""):
        def fn(eng):
            return eng.transpose(out, in_, ident)
        return self.op("tensor", fn, reads=[in_, ident], writes=[out], tag=tag)

    def tt(self, out, a, b, alu, eng="vector", tag=""):
        def fn(e):
            return e.tensor_tensor(out, a, b, alu)
        o = self.op(eng, fn, reads=[a, b], writes=[out], tag=tag)
        o.safe = (eng == "vector" and _key_of(a) != _key_of(b))
        return o

    def ts(self, out, in_, s1, s2, op0, op1=OP.bypass, eng="vector",
           accum_out=None, tag=""):
        rd = [in_] + [s for s in (s1, s2) if isinstance(s, bass.AP)]
        wr = [out] + ([accum_out] if accum_out is not None else [])

        def fn(e):
            return e.tensor_scalar(out, in_, s1, s2, op0, op1,
                                   accum_out=accum_out)
        return self.op(eng, fn, reads=rd, writes=wr, tag=tag)

    def stt(self, out, in0, scalar, in1, op0, op1, tag=""):
        rd = [in0, in1] + ([scalar] if isinstance(scalar, bass.AP) else [])

        def fn(e):
            return e.scalar_tensor_tensor(out, in0, scalar, in1, op0, op1)
        o = self.op("vector", fn, reads=rd, writes=[out], tag=tag)
        o.safe = True
        return o

    def act(self, out, in_, func, bias=None, scale=None, accum_out=None, tag=""):
        kw = {}
        if bias is not None:
            kw["bias"] = bias
        if scale is not None:
            kw["scale"] = scale
        if accum_out is not None:
            kw["accum_out"] = accum_out
        rd = [in_] + ([bias] if isinstance(bias, bass.AP) else [])
        wr = [out] + ([accum_out] if accum_out is not None else [])

        def fn(e):
            return e.activation(out, in_, func, **kw)
        return self.op("scalar", fn, reads=rd, writes=wr, tag=tag)

    def copy(self, out, in_, eng="vector", tag=""):
        if eng == "scalar":
            def fn(e):
                return e.copy(out, in_)
        else:
            def fn(e):
                return e.tensor_copy(out, in_)
        return self.op(eng, fn, reads=[in_], writes=[out], tag=tag)

    def reduce(self, out, in_, alu, axis=AX.X, negate=None, eng="vector", tag=""):
        def fn(e):
            return e.tensor_reduce(out, in_, axis, alu, negate=negate)
        o = self.op(eng, fn, reads=[in_], writes=[out], tag=tag)
        o.safe = (eng == "vector")
        return o

    def memset(self, ap, val, eng="gpsimd", tag=""):
        def fn(e):
            return e.memset(ap, val)
        return self.op(eng, fn, reads=[], writes=[ap], tag=tag)

    def iota(self, out, pattern, base=0, channel_multiplier=0, tag=""):
        def fn(e):
            return e.iota(out, pattern, base=base,
                          channel_multiplier=channel_multiplier,
                          allow_small_or_imprecise_dtypes=True)
        return self.op("gpsimd", fn, reads=[], writes=[out], tag=tag)

    def affine_select(self, out, in_, pattern, cmp, fill, base=0,
                      channel_multiplier=0, tag=""):
        def fn(e):
            return e.affine_select(out, in_, pattern, cmp, fill, base=base,
                                   channel_multiplier=channel_multiplier)
        return self.op("gpsimd", fn, reads=[in_], writes=[out], tag=tag)

    def select(self, out, mask, on_true, on_false, tag=""):
        def fn(e):
            return e.select(out, mask, on_true, on_false)
        return self.op("vector", fn, reads=[mask, on_true, on_false],
                       writes=[out], tag=tag)

    def recip(self, out, in_, tag=""):
        def fn(e):
            return e.reciprocal(out, in_)
        return self.op("vector", fn, reads=[in_], writes=[out], tag=tag)

    def scan(self, out, d0, d1, initial, op0, op1, tag=""):
        rd = [d0, d1] + ([initial] if isinstance(initial, bass.AP) else [])

        def fn(e):
            return e.tensor_tensor_scan(out, d0, d1, initial, op0, op1)
        return self.op("vector", fn, reads=rd, writes=[out], tag=tag)

    def _finalize(self):
        assert not self._finalized
        self._finalized = True
        since_drain = {e: set() for e in ENGINES}
        last_relay = {}
        issued_dma = {q: 0 for q in DMA_QUEUES}
        dwaited = {q: 0 for q in DMA_QUEUES}
        dw = {}
        schedule = {e: [] for e in ENGINES}

        class _Relay:
            __slots__ = ("eng", "needs_inc", "prod_idx", "is_relay")

            def __init__(self, eng):
                self.eng = ("rel", eng)
                self.needs_inc = True
                self.prod_idx = 0
                self.is_relay = True

        pos = {e: 0 for e in ENGINES}
        opos = {}
        kno = {}
        know = {e: {E: 0 for E in ENGINES} for e in ENGINES}
        for o in self.ops:
            o.wait_on = []
            drain_needed = False
            my = know[o.eng]
            sdeps = []
            for d in o.deps:
                if d.is_dma:
                    q = d.eng
                    cover = issued_dma[q]
                    if o.eng == q:
                        if dw.get((o.eng, q), 0) < cover:
                            dw[(o.eng, q)] = cover
                            schedule[o.eng].append(("dwait", q, 16 * cover))
                    else:
                        lr = last_relay.get(q)
                        if lr is None or lr[0] < d.dma_cum:
                            if dw.get((q, q), 0) < cover:
                                dw[(q, q)] = cover
                                schedule[q].append(("dwait", q, 16 * cover))
                            r = _Relay(q)
                            schedule[q].append(("relay", r))
                            since_drain[q] = set()
                            last_relay[q] = (cover, r)
                        o.wait_on.append(last_relay[q][1])
                elif d.eng == o.eng:
                    sdeps.append(d)
                else:
                    d.needs_inc = True
                    o.wait_on.append(d)
                    dk = kno.get(id(d))
                    if dk is not None:
                        for E in ENGINES:
                            if dk[E] > my[E]:
                                my[E] = dk[E]
                    p = opos.get(id(d), 0)
                    if p > my[d.eng]:
                        my[d.eng] = p
            for d in sdeps:
                if o.eng == "tensor" or id(d) not in since_drain[o.eng]:
                    continue
                if id(d) in o.raw:
                    if not (d.safe and o.safe):
                        drain_needed = True
                elif opos.get(id(d), 0) > my[o.eng]:
                    drain_needed = True
            if drain_needed:
                schedule[o.eng].append(("drain",))
                since_drain[o.eng] = set()
                my[o.eng] = pos[o.eng]
            schedule[o.eng].append(("op", o))
            since_drain[o.eng].add(id(o))
            pos[o.eng] += 1
            opos[id(o)] = pos[o.eng]
            kno[id(o)] = dict(my)
            if o.is_dma:
                issued_dma[o.eng] += 1
        for e in ENGINES:
            cnt = 0
            rcnt = 0
            for item in schedule[e]:
                if item[0] == "op" and getattr(item[1], "needs_inc", False):
                    cnt += 1
                    item[1].prod_idx = cnt
                elif item[0] == "relay":
                    rcnt += 1
                    item[1].prod_idx = rcnt
        self._schedule = schedule

    def emit(self):
        self._finalize()
        nc = self.nc
        sems = {}
        ctxs = []
        for e in ENGINES:
            s = nc.semaphore(f"prod_{e}")
            sems[e] = s.__enter__()
            ctxs.append(s)
        for q in DMA_QUEUES:
            s = nc.semaphore(f"dma_{q}")
            sems[("dma", q)] = s.__enter__()
            ctxs.append(s)
            s2 = nc.semaphore(f"rel_{q}")
            sems[("rel", q)] = s2.__enter__()
            ctxs.append(s2)
        self._semctx = ctxs
        self._sems = sems

        block = nc.Block()
        block.__enter__()

        def make_body(ename):
            def body(eng):
                maxw = {}
                for item in self._schedule[ename]:
                    kind = item[0]
                    if kind == "dwait":
                        eng.wait_ge(sems[("dma", item[1])], item[2])
                    elif kind == "drain":
                        eng.drain()
                    elif kind == "relay":
                        eng.sem_inc(sems[("rel", ename)], 1)
                    elif kind == "op":
                        o = item[1]
                        for p in o.wait_on:
                            v = p.prod_idx
                            assert v > 0
                            if maxw.get(p.eng, 0) < v:
                                maxw[p.eng] = v
                                eng.wait_ge(sems[p.eng], v)
                        inst = o.fn(eng)
                        if o.is_dma:
                            inst.then_inc(sems[("dma", o.eng)], 16)
                        elif getattr(o, "needs_inc", False):
                            inst.then_inc(sems[o.eng], 1)
                if ename in DMA_QUEUES and self.dma_count[ename] > 0:
                    eng.wait_ge(sems[("dma", ename)],
                                16 * self.dma_count[ename])
            return body

        for ename in ENGINES:
            if not self._schedule[ename] and not (
                    ename in DMA_QUEUES and self.dma_count[ename]):
                continue
            getattr(block, ename)(make_body(ename))
        block.__exit__(None, None, None)


D = 1024
S = 2048
NTT = S // 128
NC = D // 128
B_CORE = 2
LN_EPS = 1e-5


class Env:

    def __init__(self, nc, prog):
        self.nc = nc
        self.P = prog
        self._stack = []
        self.dram = {}
        self.internal = set()
        self.sb = {}

    def sbuf(self, name, shape, dtype):
        if name in self.sb:
            return self.sb[name]
        cm = self.nc.sbuf_tensor(name, shape, dtype)
        h = cm.__enter__()
        self._stack.append(cm)
        self.sb[name] = h
        return h

    def psum(self, name, shape, dtype):
        if name in self.sb:
            return self.sb[name]
        cm = self.nc.psum_tensor(name, shape, dtype)
        h = cm.__enter__()
        self._stack.append(cm)
        self.sb[name] = h
        return h

    def dram_in(self, name, shape, dtype=F32):
        if name not in self.dram:
            self.dram[name] = self.nc.dram_tensor(
                name, list(shape), dtype, kind="ExternalInput").ap()
        return self.dram[name]

    def dram_out(self, name, shape, dtype=F32):
        if name not in self.dram:
            self.dram[name] = self.nc.dram_tensor(
                name, list(shape), dtype, kind="ExternalOutput").ap()
        return self.dram[name]

    def dram_tmp(self, name, shape, dtype=F32):
        if name not in self.dram:
            self.dram[name] = self.nc.dram_tensor(
                name, list(shape), dtype, kind="Internal").ap()
            self.internal.add(name)
        return self.dram[name]


def setup_common(env):
    P = env.P
    env._constf = {}
    env.cpool = env.sbuf("cpool", [128, 16], F32)

    def constf(val):
        if val not in env._constf:
            i = len(env._constf)
            ap = env.cpool.ap()[:, i:i + 1]
            P.memset(ap, float(val), tag=f"const{i}")
            env._constf[val] = ap
        return env._constf[val]
    env.constf = constf
    ident = env.sbuf("ident", [128, 128], BF16)
    P.memset(ident[:], 1.0)
    P.affine_select(ident[:], ident[:], [[1, 128]], OP.is_equal, 0.0,
                    base=0, channel_multiplier=-1)
    env.ident = ident
    ones_row = env.sbuf("ones_row", [1, 128], BF16)
    P.memset(ones_row[:], 1.0)
    env.ones_row = ones_row
    ones_row_f = env.sbuf("ones_row_f", [1, 128], F32)
    P.memset(ones_row_f[:], 1.0)
    env.ones_row_f = ones_row_f
    identf = env.sbuf("identf", [128, 128], F32)
    P.memset(identf[:], 1.0)
    P.affine_select(identf[:], identf[:], [[1, 128]], OP.is_equal, 0.0,
                    base=0, channel_multiplier=-1)
    env.identf = identf
    env.X = [env.sbuf(f"X{tt}", [128, D], F32) for tt in range(NTT)]
    env.xlnr = env.sbuf("xlnr", [128, 2, D], BF16)
    env.statS = env.sbuf("statS", [128, 2 * NTT], F32)
    env.LACC = env.sbuf("dbLacc", [128, NTT, 16], F32)
    env.PCO = env.sbuf("dbPco", [128, NTT, 16], F32)
    env.stat = env.sbuf("stat", [128, 64], F32)
    env.lnf32 = env.sbuf("lnf32", [128, D], F32)
    env.ps = [env.psum(f"ps{i}", [128, 512], F32) for i in range(6)]
    env.psT = [env.psum(f"psT{i}", [128, 1024], BF16) for i in range(2)]


def stat_col(env, i):
    return env.stat.ap()[:, i:i + 1]


def emit_ln(P, env, x_tiles=None, dst_T=None, gamma_b=None, beta_b=None,
            tag="ln"):
    x_tiles = x_tiles or env.X
    dst_T = dst_T if dst_T is not None else env.T1.ap()
    st = env.stat.ap()
    sS = env.statS.ap()
    def stats(tt):
        xt = (x_tiles[tt].ap() if not isinstance(x_tiles[tt], bass.AP)
              else x_tiles[tt])
        b = 8 * (tt % 2)
        ssum = st[:, b + 0:b + 1]
        ssq = st[:, b + 1:b + 2]
        P.reduce(ssum, xt, OP.add, tag=f"{tag}_sum{tt}")
        P.act(env.xlnr.ap()[:, tt % 2, :], xt, AF.Square, accum_out=ssq,
              tag=f"{tag}_sq{tt}")
        mu = st[:, b + 2:b + 3]
        var = st[:, b + 4:b + 5]
        scol = sS[:, 2 * tt + 1:2 * tt + 2]
        negmu = sS[:, 2 * tt:2 * tt + 1]
        P.ts(mu, ssum, 1.0 / D, None, OP.mult, tag=f"{tag}_mu{tt}")
        P.ts(var, ssq, 1.0 / D, None, OP.mult)
        P.tt(st[:, b + 3:b + 4], mu, mu, OP.mult)
        P.tt(var, var, st[:, b + 3:b + 4], OP.subtract)
        P.act(var, var, AF.Ln, bias=env.constf(LN_EPS))
        P.act(scol, var, AF.Exp, scale=-0.5)
        P.ts(negmu, mu, -1.0, None, OP.mult)

    def apply(tt):
        xt = (x_tiles[tt].ap() if not isinstance(x_tiles[tt], bass.AP)
              else x_tiles[tt])
        xs = env.xlnr.ap()[:, tt % 2, :]
        scol = sS[:, 2 * tt + 1:2 * tt + 2]
        negmu = sS[:, 2 * tt:2 * tt + 1]
        if gamma_b is None:
            P.ts(xs, xt, negmu, scol, OP.add, OP.mult, tag=f"{tag}_app{tt}")
        else:
            tmpf = env.lnf32.ap()
            P.ts(tmpf, xt, negmu, scol, OP.add, OP.mult, tag=f"{tag}_app{tt}")
            P.tt(tmpf, tmpf, gamma_b, OP.mult, tag=f"{tag}_g{tt}")
            P.tt(xs, tmpf, beta_b, OP.add, tag=f"{tag}_b{tt}")
        pst = env.psT[tt % 2]
        for c in range(NC):
            P.transpose(pst.ap()[:, 128 * c:128 * (c + 1)],
                        xs[:, 128 * c:128 * (c + 1)],
                        env.ident.ap(), tag=f"{tag}_T{tt}_{c}")
        P.copy(dst_T[:, :, 128 * tt:128 * (tt + 1)],
               pst.ap().rearrange("p (c t) -> p c t", c=NC),
               tag=f"{tag}_Tc{tt}")
    stats(0)
    for tt in range(NTT):
        if tt + 1 < NTT:
            stats(tt + 1)
        apply(tt)


def load_weight_fm(P, env, name, dram_ap, rows, cols, tag=None):
    nch = rows // 128
    h = env.sbuf(name, [128, nch, cols], BF16)
    P.dma("gpsimd", h.ap(),
          dram_ap.rearrange("(c p) o -> p c o", p=128), tag=tag or name)
    return h


def load_row_bf16(P, env, name, dram_ap, n, tag=None):
    h = env.sbuf(name, [1, n], BF16)
    P.dma("gpsimd", h.ap(), dram_ap.rearrange("... -> 1 (...)")
          if False else dram_ap, tag=tag or name)
    return h


def emit_l0ffn(P, env, pref="f0"):
    gWT = env.sb.get("l0f_gwT")
    if gWT is None:
        gWT = load_weight_fm(P, env, "l0f_gwT",
                             env.dram_in("L0_ffn_gate_wT", [D, 8]), D, 8)
    gb = env.sb.get("l0f_gb")
    if gb is None:
        gb = env.sbuf("l0f_gb", [1, 8], BF16)
        P.dma("gpsimd", gb.ap(),
              env.dram_in("L0_ffn_gate_b", [8]).rearrange("(o e) -> o e", o=1))
    exb_d = env.dram_in("L0_ffn_expert_b", [8, D])
    ewT_dram = env.dram_in("L0_ffn_expert_wT", [8, D, D])
    m_tm = emit_moe_gates(P, env, pref + "g", gWT, gb, renorm=False)
    t2w = env.T2.ap().rearrange("p a b -> p (a b)")[:, 0:8192].rearrange(
        "p (c o) -> p c o", c=NC)
    wbufs = [env.W3.ap(), t2w]
    for e in range(8):
        P.dma("gpsimd", wbufs[e % 2],
              ewT_dram[e].rearrange("(c p) o -> p c o", p=128),
              tag=f"{pref}_wdma{e}")
        P.dma("gpsimd", env.BROW.ap(),
              exb_d[e].rearrange("(o d) -> o d", o=1), tag=f"{pref}_bdma{e}")
        for tt in range(NTT):
            mcol = m_tm[:, tt, e:e + 1]
            for oh in range(2):
                ps = env.ps[(tt * 2 + oh) % 4]
                out = ps.ap()
                for c in range(NC):
                    P.mm(out, env.T1.ap()[:, c, 128 * tt:128 * (tt + 1)],
                         wbufs[e % 2][:, c, 512 * oh:512 * (oh + 1)],
                         start=(c == 0), stop=False, tag=f"{pref}_e{e}t{tt}")
                P.mm(out, env.ones_row.ap(),
                     env.BROW.ap()[:, 512 * oh:512 * (oh + 1)],
                     start=False, stop=True, tag=f"{pref}_eb{e}t{tt}")
                xs = env.X[tt].ap()[:, 512 * oh:512 * (oh + 1)]
                P.stt(xs, out, mcol, xs, OP.mult, OP.add,
                      tag=f"{pref}_acc{e}t{tt}")


SUBLAYERS = ["l0_attn_a", "l0_ffn_1", "l0_attn_b", "l0_ffn_2",
             "l1_attn_a", "l1_ffn_1", "l1_attn_b", "l1_ffn_2"]
LAST_ENV = None

EMITTERS = {}


def _register():
    EMITTERS["l0_ffn_1"] = lambda P, env, seq: emit_l0ffn(P, env, "f0a")
    EMITTERS["l0_ffn_2"] = lambda P, env, seq: emit_l0ffn(P, env, "f0b")


_register()


def build_sub_program(sub_names):
    nc = bass.Bass("TRN2", target_bir_lowering=False)
    P = Prog(nc)
    env = Env(nc, P)
    x_in = env.dram_in("x", [B_CORE, S, D])
    y_out = env.dram_out("y", [B_CORE, S, D])
    setup_common(env)
    setup_big(env)
    for seq in range(B_CORE):
        for tt in range(NTT):
            P.dma("sync", env.X[tt].ap(),
                  x_in[seq, 128 * tt:128 * (tt + 1), :], tag=f"ldx{seq}_{tt}")
        for name in sub_names:
            emit_ln(P, env, tag=f"ln_{name}_s{seq}")
            EMITTERS[name](P, env, seq)
        for tt in range(NTT):
            P.dma("sync", y_out[seq, 128 * tt:128 * (tt + 1), :],
                  env.X[tt].ap(), tag=f"stx{seq}_{tt}")
    P.emit()
    global LAST_ENV
    LAST_ENV = env
    return nc


def prep_weights(inputs):
    w = {}
    w["L0_ffn_gate_wT"] = np.ascontiguousarray(inputs["L0_ffn_gate_w"].T)
    w["L0_ffn_gate_b"] = inputs["L0_ffn_gate_b"]
    w["L0_ffn_expert_wT"] = np.ascontiguousarray(
        np.transpose(inputs["L0_ffn_expert_w"], (0, 2, 1)))
    w["L0_ffn_expert_b"] = inputs["L0_ffn_expert_b"]
    prep_weights2(inputs, w)
    return w


_NC_CACHE = {}


def build_full():
    if "nc" not in _NC_CACHE:
        _NC_CACHE["nc"] = build_sub_program(SUBLAYERS)
    return _NC_CACHE["nc"]


def kernel(**inputs):
    inputs = {k: np.asarray(v) for k, v in inputs.items()}
    x = np.ascontiguousarray(inputs["L0_attn_a_x"], np.float32)
    w = prep_weights(inputs)
    nc = build_full()
    need = [n for n in LAST_ENV.dram
            if n not in LAST_ENV.internal and n != "y"]
    in_maps = []
    for core in range(8):
        m = {}
        for n in need:
            if n == "x":
                m["x"] = x[2 * core:2 * core + 2]
            else:
                src = w.get(n)
                if src is None:
                    src = inputs[n]
                m[n] = np.ascontiguousarray(src, np.float32)
        in_maps.append(m)
    res = run_bass_kernel_spmd(nc, in_maps, core_ids=list(range(8)))
    out = np.concatenate([r["y"] for r in res.results], axis=0)
    return out.astype(np.float32)


def setup_big(env):
    env.T1 = env.sbuf("T1", [128, NC, S], BF16)
    env.T2 = env.sbuf("T2", [128, NTT, 1040], BF16)
    env.QK = env.sbuf("QK", [128, 2, S], BF16)
    env.PR = env.sbuf("PR", [128, 3, 512], BF16)
    env.AFP = env.sbuf("AFP", [128, 1, S], BF16)
    env.AFG = env.sbuf("AFG", [128, 2, NC, 128], BF16)
    env.W1 = env.sbuf("W1", [128, NC, 2, 128], BF16)
    env.W2 = env.sbuf("W2", [128, NC, 2, 128], BF16)
    env.W3 = env.sbuf("W3", [128, NC, D], BF16)
    env.BROW = env.sbuf("BROW", [1, D], BF16)
    env.XR = [env.sbuf("XR0", [128, D], F32)] * 2
    env.GB = env.sbuf("GB", [128, D], BF16)
    env.BB = env.sbuf("BB", [128, D], BF16)
    env.ROWG = env.sbuf("ROWG", [1, 1024], BF16)
    env.ROWB = env.sbuf("ROWB", [1, 1024], BF16)
    env.LND = env.sbuf("LND", [1, 512], F32)
    env.RDB = env.sbuf("RDB", [64, 512], BF16)


def bcast_row(P, env, dst128, src_row, n, tag="bc"):
    ones = env.ones_row_f if src_row.dtype == F32 else env.ones_row
    for off in range(0, n, 512):
        w = min(512, n - off)
        ps = env.ps[5]
        P.mm(ps.ap()[:, 0:w], ones.ap(), src_row[:, off:off + w], tag=tag)
        P.copy(dst128[:, off:off + w], ps.ap()[:, 0:w], tag=tag + "c")


def _wchunk(wbuf, c, sl):
    return wbuf.ap()[:, c, sl]


def emit_attention(P, env, seq, cfg, pref):
    NH = 16
    gb_name = cfg.get("input_ln_gamma")
    if gb_name:
        row = env.ROWG.ap()
        P.dma("gpsimd", row, env.dram_in(gb_name, [D]).rearrange(
            "(o d) -> o d", o=1), tag=f"{pref}_g1")
        bcast_row(P, env, env.GB.ap(), row, D, tag=f"{pref}_gbc")
        row2 = env.ROWB.ap()
        P.dma("gpsimd", row2, env.dram_in(cfg["input_ln_beta"], [D]).rearrange(
            "(o d) -> o d", o=1), tag=f"{pref}_b1")
        bcast_row(P, env, env.BB.ap(), row2, D, tag=f"{pref}_bbc")
        emit_ln(P, env, gamma_b=env.GB.ap(), beta_b=env.BB.ap(),
                tag=f"{pref}_ln")
    else:
        emit_ln(P, env, tag=f"{pref}_ln")
    xsp = env.dram_tmp("xspill", [NTT, 128, D])
    if cfg.get("spill", True):
        for tt in range(NTT):
            P.dma("sync", xsp[tt], env.X[tt].ap(), tag=f"{pref}_sp{tt}")

    wq = env.dram_in(cfg["wq"], [D, D])
    wk = env.dram_in(cfg["wk"], [D, D])
    wv = env.dram_in(cfg["wv"], [D, D])
    wo = env.dram_in(cfg["wo"], [D, D])
    afd = env.dram_tmp("afdram", [NC, 128, S], dtype=BF16)
    T1 = env.T1.ap()
    V65 = env.T2.ap()
    QKa = env.QK.ap()
    PR = env.PR.ap()

    P.dma("gpsimd", env.W3.ap(), wv.rearrange("(c p) o -> p c o", p=128),
          tag=f"{pref}_wv")
    for tt in range(NTT):
        P.memset(V65[:, tt, :].rearrange("p (h e) -> p h e", e=65)[:, :, 64:65],
                 1.0, tag=f"{pref}_vone{tt}")
        for oh in range(2):
            ps = env.ps[oh]
            for c in range(NC):
                P.mm(ps.ap(), T1[:, c, 128 * tt:128 * (tt + 1)],
                     env.W3.ap()[:, c, 512 * oh:512 * (oh + 1)],
                     start=(c == 0), stop=(c == NC - 1), tag=f"{pref}_v{tt}")
            P.copy(V65[:, tt, 520 * oh:520 * oh + 520].rearrange(
                       "p (h e) -> p h e", e=65)[:, :, 0:64],
                   ps.ap().rearrange("p (h e) -> p h e", e=64),
                   eng="scalar", tag=f"{pref}_vc{tt}")
    nqt = S // 512
    for hp in range(NH // 2):
        wsl = hp % 2
        P.dma("gpsimd", env.W1.ap()[:, :, wsl, :],
              wq.rearrange("(c p) o -> p c o", p=128)[:, :, 128 * hp:128 * (hp + 1)],
              tag=f"{pref}_wq{hp}")
        P.dma("gpsimd", env.W2.ap()[:, :, wsl, :],
              wk.rearrange("(c p) o -> p c o", p=128)[:, :, 128 * hp:128 * (hp + 1)],
              tag=f"{pref}_wk{hp}")
        for (wbuf, qki, scale) in ((env.W1, 0, 0.125), (env.W2, 1, None)):
            for tb in range(nqt):
                ps = env.ps[4 + tb % 2]
                for c in range(NC):
                    P.mm(ps.ap(), wbuf.ap()[:, c, wsl, :],
                         T1[:, c, 512 * tb:512 * (tb + 1)],
                         start=(c == 0), stop=(c == NC - 1),
                         tag=f"{pref}_qk{hp}")
                dst = QKa[:, qki, 512 * tb:512 * (tb + 1)]
                if scale is None:
                    P.copy(dst, ps.ap(), tag=f"{pref}_qke{hp}")
                else:
                    P.act(dst, ps.ap(), AF.Copy, scale=scale,
                          tag=f"{pref}_qse{hp}")
        pend = None
        for h2 in range(2):
            h = 2 * hp + h2
            pb = 64 * h2
            for qt in range(nqt):
                nch = NTT if not cfg["causal"] else 4 * qt + 4
                pvb = env.ps[2 + (h * 4 + qt) % 2]

                def emit_s(c, pb=pb, qt=qt):
                    P.mm(env.ps[c % 2].ap(),
                         QKa[pb:pb + 64, 1, 128 * c:128 * (c + 1)],
                         QKa[pb:pb + 64, 0, 512 * qt:512 * (qt + 1)],
                         tag=f"{pref}_s{h}_{qt}")
                if pend is not None:
                    pend()
                emit_s(0)
                for c in range(nch):
                    if c + 1 < nch:
                        emit_s(c + 1)
                    pt = PR[:, c % 3, :]
                    P.act(pt, env.ps[c % 2].ap(), AF.Exp,
                          bias=env.constf(-24.0), tag=f"{pref}_x{h}_{qt}")
                    if cfg["causal"] and c >= 4 * qt:
                        P.affine_select(pt, pt, [[1, 512]], OP.is_ge, 0.0,
                                        base=512 * qt - 128 * c,
                                        channel_multiplier=-1,
                                        tag=f"{pref}_m{h}_{qt}")
                    P.mm(pvb.ap()[0:65, :], V65[:, c, 65 * h:65 * (h + 1)],
                         pt, start=(c == 0), stop=(c == nch - 1),
                         tag=f"{pref}_pv{h}_{qt}")

                def pend(pvb=pvb, h=h, qt=qt, pb=pb):
                    P.act(env.LND.ap(), pvb.ap()[64:65, :], AF.Ln,
                          tag=f"{pref}_ld{h}_{qt}")
                    psf0 = env.psT[0].ap().bitcast(F32)
                    P.mm(psf0[0:64, 0:512], env.ones_row_f.ap()[:, 0:64],
                         env.LND.ap(), tag=f"{pref}_bc{h}_{qt}")
                    P.act(env.RDB.ap(), psf0[0:64, 0:512], AF.Exp,
                          scale=-1.0, tag=f"{pref}_rd{h}_{qt}")
                    P.tt(env.AFP.ap()[pb:pb + 64, 0, 512 * qt:512 * (qt + 1)],
                         pvb.ap()[0:64, :], env.RDB.ap(), OP.mult,
                         tag=f"{pref}_af{h}_{qt}")
        pend()
        P.dma("sync", afd[hp], env.AFP.ap()[:, 0, :],
              tag=f"{pref}_afo{hp}")
    P.dma("gpsimd", env.W3.ap(), wo.rearrange("(c p) o -> p c o", p=128),
          tag=f"{pref}_wo")
    bo_row = None
    if cfg.get("bo"):
        bo_row = env.BROW
        P.dma("gpsimd", bo_row.ap(),
              env.dram_in(cfg["bo"], [D]).rearrange("(o d) -> o d", o=1),
              tag=f"{pref}_bo")
    if cfg["tail"] == "ln2":
        g2 = env.ROWG.ap()
        P.dma("gpsimd", g2, env.dram_in(cfg["ln2_gamma"], [D]).rearrange(
            "(o d) -> o d", o=1), tag=f"{pref}_g2")
        bcast_row(P, env, env.GB.ap(), g2, D, tag=f"{pref}_g2b")
        b2 = env.ROWB.ap()
        P.dma("gpsimd", b2, env.dram_in(cfg["ln2_beta"], [D]).rearrange(
            "(o d) -> o d", o=1), tag=f"{pref}_b2")
        bcast_row(P, env, env.BB.ap(), b2, D, tag=f"{pref}_b2b")
    st = env.stat.ap()
    sS = env.statS.ap()
    for tt in range(NTT):
        afg = env.AFG.ap()[:, tt % 2, :, :]
        P.dma("scalar", afg,
              afd.rearrange("h p t -> p h t")[:, :, 128 * tt:128 * (tt + 1)],
              tag=f"{pref}_afg{tt}")
        xr = env.XR[tt % 2].ap()
        if cfg.get("spill", True):
            P.dma("scalar", xr, xsp[tt], tag=f"{pref}_rl{tt}")
        for oh in range(2):
            ps = env.ps[4 + oh]
            for c in range(NC):
                P.mm(ps.ap(), afg[:, c, :],
                     env.W3.ap()[:, c, 512 * oh:512 * (oh + 1)],
                     start=(c == 0),
                     stop=(c == NC - 1 and bo_row is None), tag=f"{pref}_o{tt}")
            if bo_row is not None:
                P.mm(ps.ap(), env.ones_row.ap(),
                     bo_row.ap()[:, 512 * oh:512 * (oh + 1)],
                     start=False, stop=True, tag=f"{pref}_ob{tt}")
            xs = env.X[tt].ap()[:, 512 * oh:512 * (oh + 1)]
            if cfg["tail"] == "ln2":
                P.ts(env.lnf32.ap()[:, 512 * oh:512 * (oh + 1)],
                     xr[:, 512 * oh:512 * (oh + 1)],
                     sS[:, 2 * tt:2 * tt + 1], sS[:, 2 * tt + 1:2 * tt + 2],
                     OP.add, OP.mult, tag=f"{pref}_lnr{tt}")
                P.tt(xs, ps.ap(), env.lnf32.ap()[:, 512 * oh:512 * (oh + 1)],
                     OP.add, tag=f"{pref}_t2{tt}")
            elif cfg["tail"] == "acc":
                P.tt(xs, ps.ap(), xs, OP.add, tag=f"{pref}_acc{tt}")
            else:
                P.copy(xs, ps.ap(), tag=f"{pref}_y{tt}")
        xt = env.X[tt].ap()
        if cfg["tail"] == "ln2":
            ssum = st[:, 0:1]
            ssq = st[:, 1:2]
            mu = st[:, 2:3]
            var = st[:, 4:5]
            scol = st[:, 5:6]
            negmu = st[:, 6:7]
            tsc = env.xlnr.ap()[:, tt % 2, :]
            P.reduce(ssum, xt, OP.add, tag=f"{pref}_l2s{tt}")
            P.act(tsc, xt, AF.Square, accum_out=ssq, tag=f"{pref}_l2q{tt}")
            P.ts(mu, ssum, 1.0 / D, None, OP.mult)
            P.ts(var, ssq, 1.0 / D, None, OP.mult)
            P.tt(st[:, 3:4], mu, mu, OP.mult)
            P.tt(var, var, st[:, 3:4], OP.subtract)
            P.act(var, var, AF.Ln, bias=env.constf(LN_EPS))
            P.act(scol, var, AF.Exp, scale=-0.5)
            P.ts(negmu, mu, -1.0, None, OP.mult)
            P.ts(xt, xt, negmu, scol, OP.add, OP.mult, tag=f"{pref}_l2n{tt}")
            P.tt(xt, xt, env.GB.ap(), OP.mult, tag=f"{pref}_l2g{tt}")
            P.tt(xt, xt, env.BB.ap(), OP.add, tag=f"{pref}_l2b{tt}")
        if cfg.get("spill", True):
            P.tt(xt, xt, xr, OP.add, tag=f"{pref}_res{tt}")


def _register2():
    EMITTERS["l0_attn_b"] = lambda P, env, seq: emit_attention(
        P, env, seq, dict(wq="L0_attn_b_WqT", wk="L0_attn_b_WkT",
                          wv="L0_attn_b_WvT", wo="L0_attn_b_WoutT",
                          causal=False, tail="ln2",
                          ln2_gamma="L0_attn_b_gamma",
                          ln2_beta="L0_attn_b_beta"), f"ab{seq}")
    EMITTERS["l1_attn_a"] = lambda P, env, seq: emit_attention(
        P, env, seq, dict(wq="L1_attn_a_wqT", wk="L1_attn_a_wkT",
                          wv="L1_attn_a_wvT", wo="L1_attn_a_w_oT",
                          causal=True, tail="plain"), f"aa{seq}")


_register2()


def prep_weights2(inputs, w):
    w["L0_attn_b_WqT"] = np.ascontiguousarray(inputs["L0_attn_b_Wq"].T)
    w["L0_attn_b_WkT"] = np.ascontiguousarray(inputs["L0_attn_b_Wk"].T)
    w["L0_attn_b_WvT"] = np.ascontiguousarray(inputs["L0_attn_b_Wv"].T)
    w["L0_attn_b_WoutT"] = np.ascontiguousarray(inputs["L0_attn_b_Wout"].T)
    w["L0_attn_b_gamma"] = inputs["L0_attn_b_gamma"]
    w["L0_attn_b_beta"] = inputs["L0_attn_b_beta"]
    qkv = inputs["L1_attn_a_w_qkv"].reshape(3, D, D)
    w["L1_attn_a_wqT"] = np.ascontiguousarray(qkv[0].T)
    w["L1_attn_a_wkT"] = np.ascontiguousarray(qkv[1].T)
    w["L1_attn_a_wvT"] = np.ascontiguousarray(qkv[2].T)
    w["L1_attn_a_w_oT"] = np.ascontiguousarray(inputs["L1_attn_a_w_o"].T)
    M, eo = rwkv_consts()
    w["rwkv_consts"] = M
    w["rwkv_eo"] = eo
    u = np.arange(128)
    m6 = ((u[:, None] > u[None, :]) & (u[:, None] // 32 == u[None, :] // 32))
    w["delta_m6"] = m6.astype(np.float32)
    w["L1_attn_b_norm_w4"] = np.ascontiguousarray(
        np.tile(inputs["L1_attn_b_norm_w"], 4))
    w["L1_attn_b_fir_short_f"] = np.ascontiguousarray(
        inputs["L1_attn_b_fir_short_w"].reshape(1024, 3))
    cdg = np.zeros((3, 8, 4, 128, 128), np.float32)
    ii = np.arange(128)
    for ki_, cn in enumerate(("L1_attn_b_conv_v", "L1_attn_b_conv_q",
                              "L1_attn_b_conv_k")):
        cw = inputs[cn].reshape(1024, 4)
        for oc_ in range(8):
            cdg[ki_, oc_, :, ii, ii] = cw[128 * oc_:128 * (oc_ + 1), :]
    w["conv_diag"] = cdg
    lgf = inputs["L1_attn_b_fir_long_w"].reshape(1024, 63)
    dg = np.zeros((8, 63, 128, 128), np.float32)
    ii = np.arange(128)
    for oc_ in range(8):
        dg[oc_, :, ii, ii] = lgf[128 * oc_:128 * (oc_ + 1), :]
    w["fir_long_diag"] = dg
    return w


def emit_moe_gates(P, env, pref, gwT_buf, gbias_row, renorm):
    ps_g = env.ps[4]
    for tt in range(NTT):
        out = ps_g.ap()[:, 8 * tt:8 * (tt + 1)]
        for c in range(NC):
            P.mm(out, env.T1.ap()[:, c, 128 * tt:128 * (tt + 1)],
                 gwT_buf.ap()[:, c, :], start=(c == 0),
                 stop=(c == NC - 1 and gbias_row is None), tag=f"{pref}_g{tt}")
        if gbias_row is not None:
            P.mm(out, env.ones_row.ap(), gbias_row.ap(), start=False,
                 stop=True, tag=f"{pref}_gb{tt}")
    m_tm = env.PCO.ap()[:, :, 0:8]
    gd = env.PCO.ap()[:, :, 8:16]
    gm = env.LACC.ap()[:, :, 0:8]
    gst = env.LACC.ap()[:, :, 8:12]
    glog = ps_g.ap()[:, 0:8 * NTT].rearrange("p (t e) -> p t e", e=8)
    gmax = gst[:, :, 0:1]
    d2 = gst[:, :, 1:2]
    gsum = gst[:, :, 2:3]
    rgsum = gst[:, :, 3:4]
    P.reduce(gmax, glog, OP.max, tag=f"{pref}_gmax")
    P.tt(gd, glog, gmax.broadcast_to((128, NTT, 8)), OP.subtract,
         tag=f"{pref}_gsub")
    P.act(m_tm, gd, AF.Exp, tag=f"{pref}_gexp")
    P.ts(gm, gd, 0.0, None, OP.is_ge, tag=f"{pref}_ismax")
    P.stt(gm, gm, -1e30, gd, OP.mult, OP.add, tag=f"{pref}_dmask")
    P.reduce(d2, gm, OP.max, tag=f"{pref}_d2")
    P.tt(gm, gd, d2.broadcast_to((128, NTT, 8)), OP.is_ge,
         tag=f"{pref}_keep")
    P.tt(m_tm, m_tm, gm, OP.mult, tag=f"{pref}_mk")
    P.reduce(gsum, m_tm, OP.add, tag=f"{pref}_gsum")
    P.recip(rgsum, gsum, tag=f"{pref}_rg")
    if not renorm:
        P.act(gd, gd, AF.Exp, tag=f"{pref}_gexp2")
        P.reduce(gsum, gd, OP.add, tag=f"{pref}_gsumf")
        P.recip(rgsum, gsum, tag=f"{pref}_rgf")
    P.tt(m_tm, m_tm, rgsum.broadcast_to((128, NTT, 8)), OP.mult,
         tag=f"{pref}_mn")
    return m_tm


def emit_l1ffn(P, env, seq, pref):
    st = env.stat.ap()
    sS = env.statS.ap()
    xsp = env.dram_tmp("xspill", [NTT, 128, D])
    for tt in range(NTT):
        xt = env.X[tt].ap()
        P.dma("sync", xsp[tt], xt, tag=f"{pref}_sp{tt}")
        ssum = st[:, 0:1]
        ssq = st[:, 1:2]
        P.reduce(ssum, xt, OP.add, tag=f"{pref}_s{tt}")
        P.act(env.xlnr.ap()[:, tt % 2, :], xt, AF.Square, accum_out=ssq,
              tag=f"{pref}_q{tt}")
        mu = st[:, 2:3]
        var = st[:, 4:5]
        scol = sS[:, 2 * tt + 1:2 * tt + 2]
        negmu = sS[:, 2 * tt:2 * tt + 1]
        P.ts(mu, ssum, 1.0 / D, None, OP.mult)
        P.ts(var, ssq, 1.0 / D, None, OP.mult)
        P.tt(st[:, 3:4], mu, mu, OP.mult)
        P.tt(var, var, st[:, 3:4], OP.subtract)
        P.act(var, var, AF.Ln, bias=env.constf(LN_EPS))
        P.act(scol, var, AF.Exp, scale=-0.5)
        P.ts(negmu, mu, -1.0, None, OP.mult)
        P.ts(xt, xt, negmu, scol, OP.add, OP.mult, tag=f"{pref}_ln{tt}")
    emit_attention(P, env, seq, dict(
        wq="L1_ffn_wq", wk="L1_ffn_wk", wv="L1_ffn_wv", wo="L1_ffn_wo",
        bo="L1_ffn_bo", causal=True, tail="acc", spill=False,
        input_ln_gamma="L1_ffn_ln1_scale", input_ln_beta="L1_ffn_ln1_shift"),
        pref + "a")
    P.dma("gpsimd", env.ROWG.ap(), env.dram_in("L1_ffn_ln2_scale", [D])
          .rearrange("(o d) -> o d", o=1), tag=f"{pref}_g2")
    bcast_row(P, env, env.GB.ap(), env.ROWG.ap(), D, tag=f"{pref}_g2b")
    P.dma("gpsimd", env.ROWB.ap(), env.dram_in("L1_ffn_ln2_shift", [D])
          .rearrange("(o d) -> o d", o=1), tag=f"{pref}_b2")
    bcast_row(P, env, env.BB.ap(), env.ROWB.ap(), D, tag=f"{pref}_b2b")
    emit_ln(P, env, gamma_b=env.GB.ap(), beta_b=env.BB.ap(),
            tag=f"{pref}_ln2")
    gWT = env.sb.get("l1f_gw")
    if gWT is None:
        gWT = load_weight_fm(P, env, "l1f_gw",
                             env.dram_in("L1_ffn_gate_w", [D, 8]), D, 8)
    m_tm = emit_moe_gates(P, env, pref + "g", gWT, None, renorm=True)
    w1d = env.dram_in("L1_ffn_w1", [8, D, 2048])
    w2d = env.dram_in("L1_ffn_w2", [8, D, 2048])
    w3d = env.dram_in("L1_ffn_w3", [8, 2048, D])
    HT = env.T2.ap().rearrange("p a b -> p (a b)")[:, 0:16384].rearrange(
        "p (c t) -> p c t", c=8)
    AB = env.PR.ap()
    qk4 = env.QK.ap().rearrange("p a b -> p (a b)")
    w1vs = [env.W1.ap().rearrange("p c a b -> p c (a b)"),
            qk4[:, 0:2048].rearrange("p (c o) -> p c o", c=NC)]
    w2vs = [env.W2.ap().rearrange("p c a b -> p c (a b)"),
            qk4[:, 2048:4096].rearrange("p (c o) -> p c o", c=NC)]
    for e in range(8):
        for hq in range(8):
            sl = slice(256 * hq, 256 * (hq + 1))
            w1v = w1vs[hq % 2]
            w2v = w2vs[hq % 2]
            P.dma("gpsimd", w1v,
                  w1d[e].rearrange("(c p) o -> p c o", p=128)[:, :, sl],
                  tag=f"{pref}_w1{e}")
            P.dma("gpsimd", w2v,
                  w2d[e].rearrange("(c p) o -> p c o", p=128)[:, :, sl],
                  tag=f"{pref}_w2{e}")
            for j in range(2):
                hc = 2 * hq + j
                for tb in range(4):
                    psA = env.ps[tb % 2]
                    psB = env.ps[2 + tb % 2]
                    for c in range(NC):
                        P.mm(psA.ap(), w1v[:, c, 128 * j:128 * (j + 1)],
                             env.T1.ap()[:, c, 512 * tb:512 * (tb + 1)],
                             start=(c == 0), stop=(c == NC - 1),
                             tag=f"{pref}_h1{e}")
                        P.mm(psB.ap(), w2v[:, c, 128 * j:128 * (j + 1)],
                             env.T1.ap()[:, c, 512 * tb:512 * (tb + 1)],
                             start=(c == 0), stop=(c == NC - 1),
                             tag=f"{pref}_h2{e}")
                    slab = AB[:, tb % 2, :]
                    P.act(slab, psA.ap(), AF.Silu, tag=f"{pref}_si{e}")
                    P.tt(HT[:, hc % 8, 512 * tb:512 * (tb + 1)], slab,
                         psB.ap(), OP.mult, tag=f"{pref}_hm{e}")
            if hq % 4 == 3:
                hh = hq // 4
                P.dma("gpsimd", env.W3.ap(),
                      w3d[e, 1024 * hh:1024 * (hh + 1), :].rearrange(
                          "(c p) o -> p c o", p=128), tag=f"{pref}_w3{e}")
                for tt in range(NTT):
                    mcol = m_tm[:, tt, e:e + 1]
                    for oh in range(2):
                        ps = env.ps[4 + (tt % 2)]
                        for c in range(NC):
                            P.mm(ps.ap(), HT[:, c, 128 * tt:128 * (tt + 1)],
                                 env.W3.ap()[:, c, 512 * oh:512 * (oh + 1)],
                                 start=(c == 0), stop=(c == NC - 1),
                                 tag=f"{pref}_o{e}t{tt}")
                        xs = env.X[tt].ap()[:, 512 * oh:512 * (oh + 1)]
                        P.stt(xs, ps.ap(), mcol, xs, OP.mult, OP.add,
                              tag=f"{pref}_a{e}t{tt}")
    for tt in range(NTT):
        xr = env.XR[tt % 2].ap()
        P.dma("sync", xr, xsp[tt], tag=f"{pref}_rl{tt}")
        P.tt(env.X[tt].ap(), env.X[tt].ap(), xr, OP.add, tag=f"{pref}_rs{tt}")


def _register3():
    EMITTERS["l1_ffn_1"] = lambda P, env, seq: emit_l1ffn(P, env, seq, f"F1{seq}")
    EMITTERS["l1_ffn_2"] = lambda P, env, seq: emit_l1ffn(P, env, seq, f"F2{seq}")


_register3()


def emit_l0attna(P, env, seq, pref):
    emit_ln(P, env, tag=f"{pref}_ln")
    T1 = env.T1.ap()
    MC = env.sb.get("rwM")
    if MC is None:
        MC = env.sbuf("rwM", [128, 5, 128], F32)
        P.dma("sync", MC.ap(), env.dram_in("rwkv_consts", [5, 128, 128])
              .rearrange("m p j -> p m j"), tag="rwMdma")
        eo = env.sbuf("rwEO", [128, 2], F32)
        P.dma("sync", eo.ap(), env.dram_in("rwkv_eo", [128, 2]), tag="rwEOdma")
        env.rwEO = eo
    eo = env.rwEO
    xrb = env.XR[0].ap().bitcast(BF16)
    tfb = xrb[:, 1024:1536]
    P.dma("gpsimd", env.BROW.ap()[:, 0:512],
          env.dram_in("L0_attn_a_time_first", [8, 64]).rearrange(
              "(o h) k -> o (h k)", o=1), tag=f"{pref}_TFr")
    bcast_row(P, env, tfb, env.BROW.ap()[:, 0:512], 512, tag=f"{pref}_TFb")
    bdb = env.XR[0].ap()[:, 0:512]
    rowf = env.LND.ap()
    P.dma("sync", rowf, env.dram_in("L0_attn_a_base_decay", [8, 64])
          .rearrange("(o h) k -> o (h k)", o=1), tag=f"{pref}_BDr")
    bcast_row(P, env, bdb, rowf, 512, tag=f"{pref}_BDb")
    bS = env.stat.ap()[:, 16:32]
    drF = env.AFP.ap()[0:32, 0, :]
    dtW = env.RDB.ap()[0:32, :]
    dtB = env.BROW.ap()[:, 512:1024]
    P.dma("gpsimd", dtW, env.dram_in("L0_attn_a_dt_W", [32, 512]),
          tag=f"{pref}_dtW")
    P.dma("gpsimd", dtB, env.dram_in("L0_attn_a_dt_b", [512]).rearrange(
        "(o k) -> o k", o=1), tag=f"{pref}_dtB")
    w1f = env.W1.ap().rearrange("p c a b -> p (c a b)")
    w2f = env.W2.ap().rearrange("p c a b -> p (c a b)")
    wsel = w1f[:, 0:1280].rearrange("p (c o) -> p c o", c=NC)
    P.dma("gpsimd", wsel, env.dram_in("L0_attn_a_W_sel", [D, 160]).rearrange(
        "(c p) o -> p c o", p=128), tag=f"{pref}_Wsel")
    for tt in range(NTT):
        ps = env.ps[4 + tt % 2]
        for c in range(NC):
            P.mm(ps.ap()[:, 0:160], T1[:, c, 128 * tt:128 * (tt + 1)],
                 wsel[:, c, :], start=(c == 0), stop=(c == NC - 1),
                 tag=f"{pref}_sel{tt}")
        P.reduce(bS[:, tt:tt + 1], ps.ap()[:, 32:96], OP.add,
                 tag=f"{pref}_bv{tt}")
        P.ts(bS[:, tt:tt + 1], bS[:, tt:tt + 1], 1.0 / 64, None, OP.mult,
             tag=f"{pref}_bvs{tt}")
        pst = env.psT[tt % 2]
        P.copy(env.xlnr.ap()[:, tt % 2, 0:32], ps.ap()[:, 0:32],
               tag=f"{pref}_drc{tt}")
        P.transpose(pst.ap()[0:32, 0:128], env.xlnr.ap()[:, tt % 2, 0:32],
                    env.ident.ap(), tag=f"{pref}_drT{tt}")
        P.copy(drF[0:32, 128 * tt:128 * (tt + 1)], pst.ap()[0:32, 0:128],
               tag=f"{pref}_dre{tt}")
    t2f = env.T2.ap().rearrange("p a b -> p (a b)")
    WkB = t2f[:, 0:4096].rearrange("p (c o) -> p c o", c=NC)
    WvB = t2f[:, 4096:8192].rearrange("p (c o) -> p c o", c=NC)
    WrB = t2f[:, 8192:12288].rearrange("p (c o) -> p c o", c=NC)
    WgB = t2f[:, 12288:16384].rearrange("p (c o) -> p c o", c=NC)
    for nm, dst in (("L0_attn_a_Wk", WkB), ("L0_attn_a_Wv", WvB),
                    ("L0_attn_a_Wr", WrB), ("L0_attn_a_Wg", WgB)):
        P.dma("gpsimd", dst, env.dram_in(nm, [D, 512]).rearrange(
            "(c p) o -> p c o", p=128), tag=f"{pref}_{nm}")
    WoA = w1f[:, 0:2048].rearrange("p (c o) -> p c o", c=4)
    WoB2 = w2f[:, 0:2048].rearrange("p (c o) -> p c o", c=4)
    P.dma("gpsimd", WoA, env.dram_in("L0_attn_a_Wo", [512, D]).rearrange(
        "(c p) o -> p c o", p=128)[:, :, 0:512], tag=f"{pref}_WoA")
    P.dma("gpsimd", WoB2, env.dram_in("L0_attn_a_Wo", [512, D]).rearrange(
        "(c p) o -> p c o", p=128)[:, :, 512:1024], tag=f"{pref}_WoB")
    qkf = env.QK.ap().rearrange("p a b -> p (a b)")
    qe = qkf[:, 0:512]
    qo = qkf[:, 512:1024]
    k1e = qkf[:, 1024:1536]
    k1o = qkf[:, 1536:2048]
    k2e = qkf[:, 2048:2560]
    q2 = qkf[:, 2560:3072]
    kuT = qkf[:, 3072:3584]
    eTmp = qkf[:, 3584:4096]
    lF = env.GB.ap().bitcast(F32)
    sgF = env.BB.ap().bitcast(F32)
    w3f = env.W3.ap().rearrange("p c o -> p (c o)")
    KM = w3f[:, 0:3072].rearrange("p (v c t) -> p v c t", v=6, c=4)
    SC = w3f[:, 3072:4096].rearrange("p (h t) -> p h t", h=8)
    kT = w3f[:, 4096:4608]
    vT = w3f[:, 4608:5120]
    rgT = w3f[:, 5120:5632].rearrange("p (c t) -> p c t", c=4)
    EXS = env.sbuf("rwexs", [128, 4, 4], F32)
    EXB = env.sbuf("rwexb", [128, 4, 4], BF16)
    afgf = env.AFG.ap().rearrange("p a c t -> p (a c t)")
    TFT = afgf[:, 0:512].rearrange("p (c t) -> p c t", c=4)
    WS = afgf[:, 1024:1280].rearrange("p (c t) -> p c t", c=4)
    WT = afgf[:, 1280:1536].rearrange("p (c t) -> p c t", c=4)
    YG = afgf[:, 1536:2048].rearrange("p (c t) -> p c t", c=4)
    P.memset(WS, 0.0, tag=f"{pref}_W0")
    for tt in range(NTT):
        tsl = slice(128 * tt, 128 * (tt + 1))
        for kvi, (wb, dst) in enumerate(((WkB, kT), (WvB, vT))):
            ps = env.ps[4 + kvi]
            for c in range(NC):
                P.mm(ps.ap(), T1[:, c, tsl], wb[:, c, :],
                     start=(c == 0), stop=(c == NC - 1), tag=f"{pref}_kv{tt}")
            P.copy(dst, ps.ap(), tag=f"{pref}_kve{tt}")
        psr = env.ps[4]
        psg = env.ps[5]
        for c in range(NC):
            P.mm(psr.ap(), T1[:, c, tsl], WrB[:, c, :],
                 start=(c == 0), stop=(c == NC - 1), tag=f"{pref}_r{tt}")
            P.mm(psg.ap(), T1[:, c, tsl], WgB[:, c, :],
                 start=(c == 0), stop=(c == NC - 1), tag=f"{pref}_g{tt}")
        P.act(env.PR.ap()[:, 0, :], psr.ap(), AF.Sigmoid, tag=f"{pref}_sr{tt}")
        P.act(env.PR.ap()[:, 1, :], psg.ap(), AF.Sigmoid, tag=f"{pref}_sg{tt}")
        P.tt(env.PR.ap()[:, 2, :], env.PR.ap()[:, 0, :], env.PR.ap()[:, 1, :],
             OP.mult, tag=f"{pref}_rg{tt}")
        pst = env.psT[0]
        for c in range(4):
            P.transpose(pst.ap()[:, 128 * c:128 * (c + 1)],
                        env.PR.ap()[:, 2, 128 * c:128 * (c + 1)],
                        env.ident.ap(), tag=f"{pref}_rgT{tt}")
        P.copy(rgT, pst.ap()[:, 0:512], tag=f"{pref}_rge{tt}")
        psd = env.ps[4]
        P.mm(psd.ap(), drF[0:32, tsl], dtW, start=True, stop=False,
             tag=f"{pref}_dp{tt}")
        P.mm(psd.ap(), env.ones_row.ap(), dtB, start=False, stop=True,
             tag=f"{pref}_dpb{tt}")
        P.act(lF, psd.ap(), AF.Abs, tag=f"{pref}_ab{tt}")
        P.act(lF, lF, AF.Exp, scale=-1.0, tag=f"{pref}_ex{tt}")
        P.act(lF, lF, AF.Ln, bias=env.constf(1.0), tag=f"{pref}_ln1p{tt}")
        P.act(sgF, psd.ap(), AF.Relu, tag=f"{pref}_rl{tt}")
        P.tt(lF, lF, sgF, OP.add, tag=f"{pref}_sp{tt}")
        P.tt(lF, lF, bdb, OP.mult, tag=f"{pref}_l{tt}")
        P.ts(lF, lF, -1.0, None, OP.mult, tag=f"{pref}_ng{tt}")
        specs = [(0, qe, None, eo.ap()[:, 0:1], False),
                 (0, qo, None, eo.ap()[:, 1:2], False),
                 (0, k1e, -1.0, eo.ap()[:, 0:1], True),
                 (0, k1o, -1.0, eo.ap()[:, 1:2], True),
                 (1, k2e, None, eo.ap()[:, 0:1], True),
                 (3, q2, None, None, False)]
        pse = [env.ps[4], env.ps[5]]
        eTmps = [eTmp, env.PR.ap()[:, 0, :]]

        def spec_mm(i, tt=tt):
            mi = specs[i][0]
            P.mm(pse[i % 2].ap()[:, 0:512], MC.ap()[:, mi, :], lF,
                 tag=f"{pref}_f{tt}_{i}")
        spec_mm(0)
        for i, (mi, dst, sc, msk, useb) in enumerate(specs):
            if i + 1 < len(specs):
                spec_mm(i + 1)
            et_ = eTmps[i % 2]
            P.act(et_, pse[i % 2].ap()[:, 0:512], AF.Exp, scale=sc,
                  tag=f"{pref}_fe{tt}_{i}")
            P.tt(dst, et_, kT, OP.mult, tag=f"{pref}_fk{tt}_{i}")
            if msk is not None:
                sc2 = bS[:, tt:tt + 1] if useb else None
                P.ts(dst, dst, msk, sc2, OP.mult,
                     OP.mult if useb else OP.bypass, tag=f"{pref}_fm{tt}_{i}")
            pst = env.psT[i % 2]
            for c in range(4):
                P.transpose(pst.ap()[:, 128 * c:128 * (c + 1)],
                            dst[:, 128 * c:128 * (c + 1)],
                            env.ident.ap(), tag=f"{pref}_fT{tt}_{i}")
            P.copy(KM[:, i, :, :],
                   pst.ap()[:, 0:512].rearrange("p (c t) -> p c t", c=4),
                   tag=f"{pref}_fTe{tt}_{i}")
        ps = env.ps[4]
        P.mm(ps.ap()[:, 0:512], MC.ap()[:, 2, :], lF, tag=f"{pref}_fu{tt}")
        P.act(eTmp, ps.ap()[:, 0:512], AF.Exp, tag=f"{pref}_fue{tt}")
        P.tt(kuT, eTmp, kT, OP.mult, tag=f"{pref}_fuk{tt}")
        P.ts(kuT, kuT, bS[:, tt:tt + 1], None, OP.mult, tag=f"{pref}_fub{tt}")
        ps = env.ps[5]
        P.mm(ps.ap()[:, 0:512], MC.ap()[:, 4, :], lF, tag=f"{pref}_s5{tt}")
        P.copy(sgF, ps.ap()[:, 0:512], tag=f"{pref}_s5c{tt}")
        for c in range(4):
            pst = env.psT[c % 2]
            psf = pst.ap().bitcast(F32)
            P.transpose(psf[:, 0:128], sgF[:, 128 * c:128 * (c + 1)],
                        env.identf.ap(), tag=f"{pref}_s5T{tt}_{c}")
            P.copy(EXS.ap()[:, c, :], psf[:, 0:128].rearrange(
                "p (a b) -> p a b", a=4)[:, :, 0:1].rearrange(
                "p a o -> p (a o)"), tag=f"{pref}_s5e{tt}_{c}")
        P.act(EXB.ap().rearrange("p a b -> p (a b)"),
              EXS.ap().rearrange("p a b -> p (a b)"), AF.Exp,
              tag=f"{pref}_s5x{tt}")
        P.tt(eTmp, kT, vT, OP.mult, tag=f"{pref}_tf{tt}")
        P.tt(eTmp, eTmp, tfb, OP.mult, tag=f"{pref}_tf2{tt}")
        pst = env.psT[0]
        for c in range(4):
            P.transpose(pst.ap()[:, 128 * c:128 * (c + 1)],
                        eTmp[:, 128 * c:128 * (c + 1)],
                        env.ident.ap(), tag=f"{pref}_tfT{tt}")
        P.copy(TFT, pst.ap()[:, 0:512].rearrange("p (c t) -> p c t", c=4),
               tag=f"{pref}_tfe{tt}")
        for h in range(8):
            cc = h // 2
            pb = 64 * (h % 2)
            ps = env.ps[4 + h % 2]
            P.mm(ps.ap()[:, 0:128], KM[pb:pb + 64, 2, cc, :],
                 KM[pb:pb + 64, 0, cc, :], start=True, stop=False,
                 tag=f"{pref}_sa{tt}_{h}")
            P.mm(ps.ap()[:, 0:128], KM[pb:pb + 64, 3, cc, :],
                 KM[pb:pb + 64, 1, cc, :], start=False, stop=False,
                 tag=f"{pref}_sb{tt}_{h}")
            P.mm(ps.ap()[:, 0:128], KM[pb:pb + 64, 4, cc, :],
                 KM[pb:pb + 64, 1, cc, :], start=False, stop=True,
                 tag=f"{pref}_sc{tt}_{h}")
            P.copy(SC[:, h, :], ps.ap()[:, 0:128], tag=f"{pref}_se{tt}_{h}")
            P.affine_select(SC[:, h, :], SC[:, h, :], [[1, 128]],
                            OP.is_ge, 0.0, base=0, channel_multiplier=-1,
                            tag=f"{pref}_sm{tt}_{h}")
        for b in range(4):
            sb = slice(32 * b, 32 * (b + 1))
            for h in range(8):
                p_ = h // 2
                h2 = h % 2
                pb = 64 * h2
                psy = env.ps[p_]
                P.mm(psy.ap()[pb:pb + 64, 32 * b:32 * (b + 1)],
                     vT.rearrange("p (h k) -> p h k", h=8)[sb, h, :],
                     SC[:, h, :][32 * b:32 * (b + 1), 32 * b:32 * (b + 1)],
                     start=True, stop=False, tp=(32 * b, pb),
                     tag=f"{pref}_yi{tt}_{b}_{h}")
            psu = env.ps[4 + b % 2]
            for h in range(8):
                h2 = h % 2
                p_ = h // 2
                P.mm(psu.ap()[64 * h2:64 * h2 + 64, 64 * p_:64 * p_ + 64],
                     kuT.rearrange("p (h k) -> p h k", h=8)[sb, h, :],
                     vT.rearrange("p (h k) -> p h k", h=8)[sb, h, :],
                     start=True, stop=True, tp=(32 * b, 64 * h2),
                     tag=f"{pref}_u{tt}_{b}_{h}")
            for h in range(8):
                p_ = h // 2
                h2 = h % 2
                pb = 64 * h2
                psy = env.ps[p_]
                P.mm(psy.ap()[pb:pb + 64, 32 * b:32 * (b + 1)],
                     WS[pb:pb + 64, p_, :],
                     KM[pb:pb + 64, 5, h // 2, 32 * b:32 * (b + 1)],
                     start=False, stop=True, tp=(pb, pb),
                     tag=f"{pref}_yw{tt}_{b}_{h}")
            P.tt(WT, WS, EXB.ap()[:, :, b:b + 1].broadcast_to((128, 4, 64)),
                 OP.mult, tag=f"{pref}_wd{tt}_{b}")
            P.tt(WS, WT, psu.ap()[:, 0:256].rearrange("p (a k) -> p a k", a=4),
                 OP.add, tag=f"{pref}_wu{tt}_{b}")
        for p_ in range(4):
            P.tt(YG[:, p_, :], env.ps[p_].ap()[:, 0:128], TFT[:, p_, :],
                 OP.add, tag=f"{pref}_ya{tt}_{p_}")
            P.tt(YG[:, p_, :], YG[:, p_, :], rgT[:, p_, :], OP.mult,
                 tag=f"{pref}_yg{tt}_{p_}")
        for oh in range(2):
            ps = env.ps[4 + oh]
            wo = WoA if oh == 0 else WoB2
            for p_ in range(4):
                P.mm(ps.ap(), YG[:, p_, :], wo[:, p_, :],
                     start=(p_ == 0), stop=(p_ == 3), tag=f"{pref}_wo{tt}")
            xs = env.X[tt].ap()[:, 512 * oh:512 * (oh + 1)]
            P.tt(xs, xs, ps.ap(), OP.add, tag=f"{pref}_xo{tt}")


def _register4():
    EMITTERS["l0_attn_a"] = lambda P, env, seq: emit_l0attna(P, env, seq,
                                                            f"R{seq}")


_register4()


def rwkv_consts():
    M = np.zeros((5, 128, 128), np.float32)
    u = np.arange(128)
    bu = u // 16
    sbu = u // 32
    U, T_ = np.meshgrid(u, u, indexing="ij")
    sameb = (bu[:, None] == bu[None, :])
    samesb = (sbu[:, None] == sbu[None, :])
    M[0] = sameb * ((U <= T_) - 0.5)
    even_s = (bu[None, :] % 2 == 0)
    M[1] = (sameb * (U > T_) + 0.5 * ((bu[:, None] == bu[None, :] + 1)
                                      & samesb)) * even_s
    M[2] = samesb * (U > T_)
    M[3] = samesb * (U <= T_)
    M[4] = samesb * 1.0
    eo = np.zeros((128, 2), np.float32)
    eo[:, 0] = (bu % 2 == 0)
    eo[:, 1] = (bu % 2 == 1)
    return M, eo


def emit_l1attnb(P, env, seq, pref):
    emit_ln(P, env, tag=f"{pref}_ln")
    T1 = env.T1.ap()
    xsp = env.dram_tmp("xspill", [NTT, 128, D])
    for tt in range(NTT):
        P.dma("sync", xsp[tt], env.X[tt].ap(), tag=f"{pref}_sp{tt}")
    M6 = env.sb.get("dbM6")
    if M6 is None:
        M6 = env.sbuf("dbM6", [128, 128], F32)
        P.dma("sync", M6.ap(), env.dram_in("delta_m6", [128, 128]),
              tag="dbM6dma")
    lamc = env.sbuf("dbLam", [128, 4], F32)
    FLB = env.sbuf("dbFlb", [128, 16], F32)
    RTB = env.sbuf("dbRt", [128, 4], F32)
    NWB = env.AFG.ap().rearrange("p a c t -> p (a c t)")[:, 0:1024]
    betaS = env.sbuf("dbBeta", [128, NTT, 4], F32)
    statT = env.AFP.ap()[0:64, 0, :]
    LACC = env.LACC
    PCO = env.PCO
    W1S = env.sbuf("dbW1s", [64, 128], BF16)
    W2S = env.sbuf("dbW2s", [128, 16], BF16)
    row = env.LND.ap()
    P.memset(row[:, 0:36], 0.0, tag=f"{pref}_rz")
    P.dma("sync", row[:, 0:4], env.dram_in("L1_attn_b_forget_param", [4])
          .rearrange("(o a) -> o a", o=1), tag=f"{pref}_fg")
    P.act(row[:, 4:8], row[:, 0:4], AF.Sigmoid, tag=f"{pref}_fgs")
    P.ts(row[:, 4:8], row[:, 4:8], 0.45, 0.55, OP.mult, OP.add,
         tag=f"{pref}_fgl")
    P.dma("sync", row[:, 16:32], env.dram_in("L1_attn_b_floor_param", [4, 4])
          .rearrange("(o a) b -> o (a b)", o=1), tag=f"{pref}_fl")
    P.dma("sync", row[:, 32:36], env.dram_in("L1_attn_b_log_temp", [4])
          .rearrange("(o a) -> o a", o=1), tag=f"{pref}_lt")
    ps = env.ps[5]
    P.mm(ps.ap()[:, 0:36], env.ones_row_f.ap(), row[:, 0:36],
         tag=f"{pref}_cb")
    P.copy(lamc.ap(), ps.ap()[:, 4:8], tag=f"{pref}_lc")
    P.act(FLB.ap(), ps.ap()[:, 16:32], AF.Sigmoid, tag=f"{pref}_fls")
    P.ts(FLB.ap(), FLB.ap(), 0.075, None, OP.mult, tag=f"{pref}_flm")
    P.act(RTB.ap(), ps.ap()[:, 32:36], AF.Exp, scale=-1.0, tag=f"{pref}_rt")
    P.dma("gpsimd", env.BROW.ap(), env.dram_in("L1_attn_b_norm_w4", [1024])
          .rearrange("(o a) -> o a", o=1), tag=f"{pref}_nw")
    bcast_row(P, env, NWB, env.BROW.ap(), 1024, tag=f"{pref}_nwb")

    w3f = env.W3.ap().rearrange("p c o -> p (c o)")
    WB = w3f[:, 0:8192].rearrange("p (c o) -> p c o", c=NC)
    Wb4 = env.sbuf("dbWb", [128, NC, 4], BF16)
    vfm = [env.X[8 + j].ap().bitcast(BF16) for j in range(NC)]
    qkd = env.dram_tmp("qkproj", [2, NTT, 128, D], dtype=BF16)
    ext = env.QK.ap().rearrange("p a b -> p (a b)")
    CVD = env.T2.ap().rearrange("p a b -> p (a b)")[:, 0:4096].rearrange(
        "p (z o) -> p z o", o=128)
    cvdd = env.dram_in("conv_diag", [3, NC, 4, 128, 128])
    P.dma("gpsimd", Wb4.ap(), env.dram_in("L1_attn_b_Wb", [D, 4]).rearrange(
        "(c p) o -> p c o", p=128), tag=f"{pref}_Wb")
    for ki, (wname, kind) in enumerate((("L1_attn_b_Wv", "v"),
                                        ("L1_attn_b_Wq", "q"),
                                        ("L1_attn_b_Wk", "k"))):
        P.dma("gpsimd", WB, env.dram_in(wname, [D, D]).rearrange(
            "(c p) o -> p c o", p=128), tag=f"{pref}_{wname}")
        P.dma("gpsimd", CVD,
              cvdd[ki].rearrange("c a p o -> p (c a) o"),
              tag=f"{pref}_cvd{kind}")
        for tt in range(NTT):
            w = 131 if tt > 0 else 128

            def proj(oc, tt=tt, kind=kind, w=w):
                ps = env.ps[oc % 2]
                for c in range(NC):
                    P.mm(ps.ap()[:, 0:w],
                         WB[:, c, 128 * oc:128 * (oc + 1)],
                         T1[:, c, max(0, 128 * tt - 3):128 * (tt + 1)],
                         start=(c == 0), stop=(c == NC - 1),
                         tag=f"{pref}_{kind}p{tt}")

            def convp(oc, tt=tt, kind=kind):
                ps = env.ps[oc % 2]
                et = ext[:, 512 * (oc % 4):512 * (oc % 4) + 131]
                if tt == 0:
                    P.memset(et[:, 0:3], 0.0, tag=f"{pref}_{kind}z{tt}")
                    P.copy(et[:, 3:131], ps.ap()[:, 0:128], eng="scalar",
                           tag=f"{pref}_{kind}e{tt}")
                else:
                    P.copy(et[:, 0:131], ps.ap()[:, 0:131], eng="scalar",
                           tag=f"{pref}_{kind}e{tt}")
                ps2 = env.ps[2 + oc % 2]
                for tau in range(4):
                    P.mm(ps2.ap()[:, 0:128], CVD[:, 4 * oc + tau, :],
                         et[:, tau:tau + 128], start=(tau == 0),
                         stop=(tau == 3), tag=f"{pref}_{kind}c{tau}_{tt}")
                dst = (vfm[oc][:, 128 * tt:128 * (tt + 1)] if kind == "v"
                       else ext[:, 2048 + 128 * oc:2048 + 128 * (oc + 1)])
                P.act(dst, ps2.ap()[:, 0:128], AF.Silu,
                      tag=f"{pref}_{kind}s{tt}")
            proj(0)
            for oc in range(NC):
                if oc + 1 < NC:
                    proj(oc + 1)
                convp(oc)
            if kind == "v":
                psb = env.ps[4]
                for c in range(NC):
                    P.mm(psb.ap()[:, 0:4], T1[:, c, 128 * tt:128 * (tt + 1)],
                         Wb4.ap()[:, c, :], start=(c == 0),
                         stop=(c == NC - 1), tag=f"{pref}_b{tt}")
                P.act(betaS.ap()[:, tt, :], psb.ap()[:, 0:4], AF.Sigmoid,
                      tag=f"{pref}_bs{tt}")
            else:
                qi = 0 if kind == "q" else 1
                P.dma("sync", qkd[qi, tt], ext[:, 2048:3072],
                      tag=f"{pref}_{kind}o{tt}")

    lf = env.lnf32.ap()
    Lm = lf[:, 0:128]
    Tm = lf[:, 128:256]
    Tt = lf[:, 256:384]
    Mm = lf[:, 384:512]
    Mt = lf[:, 512:640]
    Sv = [env.X[0].ap()[:, 0:512], env.X[0].ap()[:, 512:1024],
          env.X[1].ap()[:, 0:512], env.X[1].ap()[:, 512:1024]]
    uT = env.X[2].ap()
    x4b = env.X[4].ap().bitcast(BF16)
    qF = x4b[:, 0:1024].rearrange("p (h c t) -> p h c t", h=4, c=2)
    kF = x4b[:, 1024:2048].rearrange("p (h c t) -> p h c t", h=4, c=2)
    x3b = env.X[3].ap().bitcast(BF16)
    wF = x3b[:, 0:1024].rearrange("p (h c t) -> p h c t", h=4, c=2)
    x5b = env.X[5].ap().bitcast(BF16)
    kTm = x5b[:, 0:1024]
    vTm = x5b[:, 1024:2048]
    x6b = env.X[6].ap().bitcast(BF16)
    kbTm = x6b[:, 0:1024]
    uSn = x6b[:, 1024:2048]
    x7b = env.X[7].ap().bitcast(BF16)
    Ssn = [x7b[:, 0:1024], x7b[:, 1024:2048]]
    atl = env.xlnr.ap().rearrange("p a b -> p (a b)")
    wtmS = atl[:, 512:768]
    P.memset(env.X[0].ap(), 0.0, tag=f"{pref}_S0")
    P.memset(env.X[1].ap(), 0.0, tag=f"{pref}_S1")
    prv = env.PR.ap().rearrange("p a b -> p (a b)")
    DTM = env.T2.ap().rearrange("p a b -> p (a b)")[:, 0:16384].rearrange(
        "p (t d) -> p t d", d=1024)
    for tt in range(NTT):
        P.dma("sync", ext[:, 0:1024], qkd[0, tt], tag=f"{pref}_qr{tt}")
        P.dma("sync", ext[:, 1024:2048], qkd[1, tt], tag=f"{pref}_kr{tt}")
        for src_off, isk in ((0, False), (1024, True)):
            pst = env.psT[0]
            for c in range(NC):
                P.transpose(pst.ap()[:, 128 * c:128 * (c + 1)],
                            ext[:, src_off + 128 * c:src_off + 128 * (c + 1)],
                            env.ident.ap(), tag=f"{pref}_nT{tt}")
            tmv = prv[:, 0:1024]
            P.copy(tmv, pst.ap()[:, 0:1024], tag=f"{pref}_nTe{tt}")
            sq = ext[:, 2048:3072]
            P.act(sq, tmv, AF.Square, tag=f"{pref}_nsq{tt}")
            ssq = env.stat.ap()[:, 32:36]
            P.reduce(ssq, sq.rearrange("p (h d) -> p h d", h=4), OP.add,
                     tag=f"{pref}_nss{tt}")
            P.act(ssq, ssq, AF.Ln, bias=env.constf(1e-6), tag=f"{pref}_nl{tt}")
            P.act(ssq, ssq, AF.Exp, scale=-0.5, tag=f"{pref}_ne{tt}")
            P.tt(tmv.rearrange("p (h d) -> p h d", h=4),
                 tmv.rearrange("p (h d) -> p h d", h=4),
                 ssq[:, :, None].broadcast_to((128, 4, 256)), OP.mult,
                 tag=f"{pref}_nm{tt}")
            if isk:
                P.copy(kTm, tmv, tag=f"{pref}_ktm{tt}")
                P.tt(kbTm.rearrange("p (h d) -> p h d", h=4),
                     tmv.rearrange("p (h d) -> p h d", h=4),
                     betaS.ap()[:, tt, :][:, :, None].broadcast_to(
                         (128, 4, 256)), OP.mult, tag=f"{pref}_kbm{tt}")
            pst = env.psT[1]
            for c in range(NC):
                P.transpose(pst.ap()[:, 128 * c:128 * (c + 1)],
                            tmv[:, 128 * c:128 * (c + 1)],
                            env.ident.ap(), tag=f"{pref}_nB{tt}")
            dstF = kF if isk else qF
            P.copy(dstF.rearrange("p h c t -> p (h c t)"),
                   pst.ap()[:, 0:1024], tag=f"{pref}_nBe{tt}")
        pst = env.psT[0]
        for c in range(NC):
            P.transpose(pst.ap()[:, 128 * c:128 * (c + 1)],
                        vfm[c][:, 128 * tt:128 * (tt + 1)],
                        env.ident.ap(), tag=f"{pref}_vT{tt}")
        P.copy(vTm, pst.ap()[:, 0:1024], tag=f"{pref}_vTe{tt}")
        P.tt(vTm.rearrange("p (h d) -> p h d", h=4),
             vTm.rearrange("p (h d) -> p h d", h=4),
             betaS.ap()[:, tt, :][:, :, None].broadcast_to((128, 4, 256)),
             OP.mult, tag=f"{pref}_vbm{tt}")
        for h in range(4):
            psa = env.ps[h % 2]
            for c in range(2):
                P.mm(psa.ap()[:, 0:128], kF[:, h, c, :], kF[:, h, c, :],
                     start=(c == 0), stop=(c == 1), tag=f"{pref}_kk{tt}_{h}")
            P.ts(Lm, psa.ap()[:, 0:128], betaS.ap()[:, tt, h:h + 1], -1.0,
                 OP.mult, OP.mult, tag=f"{pref}_L{tt}_{h}")
            P.tt(Lm, Lm, M6.ap(), OP.mult, tag=f"{pref}_Lm{tt}_{h}")
            P.tt(Tm, Lm, env.identf.ap(), OP.add, tag=f"{pref}_T0{tt}_{h}")
            psf = env.psT[0].ap().bitcast(F32)
            P.transpose(psf[:, 0:128], Tm, env.identf.ap(),
                        tag=f"{pref}_Tt{tt}_{h}")
            P.copy(Tt, psf[:, 0:128], tag=f"{pref}_Tte{tt}_{h}")
            P.transpose(psf[:, 128:256], Lm, env.identf.ap(),
                        tag=f"{pref}_Lt{tt}_{h}")
            P.copy(Mt, psf[:, 128:256], tag=f"{pref}_Lte{tt}_{h}")
            psb = env.ps[2 + h % 2]
            P.mm(psb.ap()[:, 0:128], Mt, Lm, tag=f"{pref}_M2{tt}_{h}")
            P.copy(Mm, psb.ap()[:, 0:128], tag=f"{pref}_M2e{tt}_{h}")
            for it in range(4):
                P.transpose(psf[:, 128:256], Mm, env.identf.ap(),
                            tag=f"{pref}_Mt{tt}_{h}_{it}")
                P.copy(Mt, psf[:, 128:256], tag=f"{pref}_Mte{tt}_{h}_{it}")
                P.mm(psb.ap()[:, 0:128], Tt, Mm, tag=f"{pref}_TM{tt}_{h}_{it}")
                P.tt(Tm, Tm, psb.ap()[:, 0:128], OP.add,
                     tag=f"{pref}_Ta{tt}_{h}_{it}")
                P.transpose(psf[:, 0:128], Tm, env.identf.ap(),
                            tag=f"{pref}_Tt2{tt}_{h}_{it}")
                P.copy(Tt, psf[:, 0:128], tag=f"{pref}_Tte2{tt}_{h}_{it}")
                if it < 3:
                    P.mm(psb.ap()[:, 0:128], Mt, Mm, tag=f"{pref}_MM{tt}_{h}")
                    P.copy(Mm, psb.ap()[:, 0:128], tag=f"{pref}_MMe{tt}_{h}")
            psu = env.ps[4]
            P.mm(psu.ap()[:, 0:256], Tt.bitcast(BF16)[:, 0:256], None,
                 tag="") if False else None
            TtB = atl[:, 768:896]
            P.copy(TtB, Tt, tag=f"{pref}_TtB{tt}_{h}")
            P.mm(psu.ap()[:, 0:256], TtB, vTm[:, 256 * h:256 * (h + 1)],
                 tag=f"{pref}_u{tt}_{h}")
            P.copy(uT[:, 256 * h:256 * (h + 1)], psu.ap()[:, 0:256],
                   tag=f"{pref}_ue{tt}_{h}")
            psw = env.ps[5]
            P.mm(psw.ap()[:, 0:256], TtB, kbTm[:, 256 * h:256 * (h + 1)],
                 tag=f"{pref}_w{tt}_{h}")
            P.copy(wtmS, psw.ap()[:, 0:256], tag=f"{pref}_we{tt}_{h}")
            pst = env.psT[1]
            for c in range(2):
                P.transpose(pst.ap()[:, 128 * c:128 * (c + 1)],
                            wtmS[:, 128 * c:128 * (c + 1)],
                            env.ident.ap(), tag=f"{pref}_wT{tt}_{h}")
            P.copy(wF[:, h, :, :].rearrange("p c t -> p (c t)"),
                   pst.ap()[:, 0:256], tag=f"{pref}_wTe{tt}_{h}")
            psat = env.ps[h % 2]
            for c in range(2):
                P.mm(psat.ap()[:, 0:128], kF[:, h, c, :], qF[:, h, c, :],
                     start=(c == 0), stop=(c == 1), tag=f"{pref}_at{tt}_{h}")
            ath = atl[:, 128 * h:128 * (h + 1)]
            P.copy(ath, psat.ap()[:, 0:128], tag=f"{pref}_ate{tt}_{h}")
            P.affine_select(ath, ath, [[1, 128]], OP.is_ge, 0.0, base=0,
                            channel_multiplier=-1, tag=f"{pref}_atm{tt}_{h}")
        for i in range(4):
            rows = slice(32 * i, 32 * (i + 1))
            for c in range(2):
                P.copy(Ssn[c].rearrange("p (h d) -> p h d", h=4),
                       bass.AP.__new__(bass.AP) if False else
                       None, tag="") if False else None
            for h in range(4):
                for c in range(2):
                    P.copy(Ssn[c][:, 256 * h:256 * (h + 1)],
                           Sv[h][:, 256 * c:256 * (c + 1)],
                           tag=f"{pref}_ss{tt}_{i}_{h}")
            for h in range(4):
                psu = env.ps[h % 2]
                for c in range(2):
                    P.mm(psu.ap()[32 * i:32 * (i + 1), 0:256],
                         wF[:, h, c, 32 * i:32 * (i + 1)], Ssn[c][:, 256 * h:256 * (h + 1)],
                         start=(c == 0), stop=(c == 1), tp=(0, 32 * i),
                         tag=f"{pref}_up{tt}_{i}_{h}")
                usl = uT[rows, 256 * h:256 * (h + 1)]
                P.tt(usl, usl, psu.ap()[32 * i:32 * (i + 1), 0:256],
                     OP.subtract, tag=f"{pref}_ui{tt}_{i}_{h}")
                P.copy(uSn[rows, 256 * h:256 * (h + 1)], usl,
                       tag=f"{pref}_us{tt}_{i}_{h}")
                pso = env.ps[2 + h % 2]
                for c in range(2):
                    P.mm(pso.ap()[32 * i:32 * (i + 1), 0:256],
                         qF[:, h, c, 32 * i:32 * (i + 1)],
                         Ssn[c][:, 256 * h:256 * (h + 1)],
                         start=(c == 0), stop=False, tp=(0, 32 * i),
                         tag=f"{pref}_oq{tt}_{i}_{h}")
                P.mm(pso.ap()[32 * i:32 * (i + 1), 0:256],
                     atl[rows, 128 * h + 32 * i:128 * h + 32 * (i + 1)],
                     uSn[rows, 256 * h:256 * (h + 1)],
                     start=False, stop=True, tp=(32 * i, 32 * i),
                     tag=f"{pref}_oa{tt}_{i}_{h}")
                P.copy(DTM[rows, tt, 256 * h:256 * (h + 1)],
                       pso.ap()[32 * i:32 * (i + 1), 0:256],
                       tag=f"{pref}_oe{tt}_{i}_{h}")
                pss = env.ps[4 + h % 2]
                for c in range(2):
                    P.mm(pss.ap()[:, 0:256],
                         kTm[rows, 256 * h + 128 * c:256 * h + 128 * (c + 1)],
                         uSn[rows, 256 * h:256 * (h + 1)],
                         start=True, stop=True, tp=(32 * i, 0),
                         tag=f"{pref}_su{tt}_{i}_{h}_{c}")
                    P.stt(Sv[h][:, 256 * c:256 * (c + 1)],
                          Sv[h][:, 256 * c:256 * (c + 1)],
                          lamc.ap()[:, h:h + 1], pss.ap()[:, 0:256],
                          OP.mult, OP.add, tag=f"{pref}_sup{tt}_{i}_{h}_{c}")

    sh_w = env.sbuf("dbFs", [128, NC, 3], F32)
    P.dma("sync", sh_w.ap(), env.dram_in("L1_attn_b_fir_short_f", [D, 3])
          .rearrange("(c p) o -> p c o", p=128), tag=f"{pref}_fsw")
    ldram = env.dram_tmp("longtm", [NTT, 128, 1024], dtype=BF16)
    DTM2 = env.T2.ap().rearrange("p a b -> p (a b)")[:, 0:16384].rearrange(
        "p (t d) -> p t d", d=1024)
    shTM = [env.X[j].ap().bitcast(BF16) for j in range(8)]

    def sh_tile(tt):
        return shTM[tt // 2][:, 1024 * (tt % 2):1024 * (tt % 2) + 1024]
    SH = env.GB.ap().bitcast(F32)
    sqs = env.xlnr.ap().rearrange("p a b -> p (a b)")[:, 0:1024]
    stt_tm = env.sbuf("dbStm", [128, 64], F32)
    DG = env.W2.ap().rearrange("p c a b -> p (c a b)").rearrange(
        "p (u t o) -> p u t o", u=2, t=8)
    ZB = env.sbuf("dbZB", [128, 128], BF16)
    P.memset(ZB.ap(), 0.0, tag=f"{pref}_zb")
    dgd = env.dram_in("fir_long_diag", [NC, 63, 128, 128])
    extb = env.QK.ap().bitcast(BF16).rearrange("p a b -> p (a b)")
    lgfm = extb[:, 0:2048]
    strips = prv[:, 1024:1536]

    def fir(acc, wcol, taps, oc, tb):
        base = 512 * tb
        for tau in range(taps):
            sh = taps - 1 - tau
            lo = max(0, sh - base)
            src = vfm[oc][:, base - sh + lo:base - sh + 512]
            if tau == 0:
                P.memset(acc[:, 0:512], 0.0, tag=f"{pref}_fz{oc}_{tb}")
            P.stt(acc[:, lo:512], src, wcol[:, oc, tau:tau + 1],
                  acc[:, lo:512], OP.mult, OP.add, tag=f"{pref}_ft{oc}_{tb}")

    def dostats(P, env, src_tm, ts_idx, tt):
        P.act(sqs, src_tm, AF.Square, tag=f"{pref}_sq{ts_idx}_{tt}")
        g = src_tm.rearrange("p (h d) -> p h d", h=4)
        gq = sqs.rearrange("p (h d) -> p h d", h=4)
        st = env.stat.ap()
        msum = st[:, 40:44]
        mssq = st[:, 44:48]
        mmax = st[:, 48:52]
        P.reduce(msum, g, OP.add, tag=f"{pref}_st1{ts_idx}_{tt}")
        P.reduce(mssq, gq, OP.add, tag=f"{pref}_st2{ts_idx}_{tt}")
        P.reduce(mmax, g, OP.max, tag=f"{pref}_st3{ts_idx}_{tt}")
        base = 16 * ts_idx
        o = stt_tm.ap().rearrange("p (ts h st) -> p ts h st", ts=4, h=4)
        P.ts(o[:, ts_idx, :, 0], msum, 1.0 / 256, None, OP.mult,
             tag=f"{pref}_sm{ts_idx}_{tt}")
        P.ts(st[:, 52:56], mssq, 1.0 / 256, None, OP.mult,
             tag=f"{pref}_sv1{ts_idx}_{tt}")
        P.tt(st[:, 56:60], o[:, ts_idx, :, 0], o[:, ts_idx, :, 0], OP.mult,
             tag=f"{pref}_sv2{ts_idx}_{tt}")
        P.tt(o[:, ts_idx, :, 1], st[:, 52:56], st[:, 56:60], OP.subtract,
             tag=f"{pref}_sv3{ts_idx}_{tt}")
        P.copy(o[:, ts_idx, :, 2], mmax, tag=f"{pref}_sx{ts_idx}_{tt}")
        P.act(o[:, ts_idx, :, 3], mssq, AF.Ln, tag=f"{pref}_sl{ts_idx}_{tt}")
        P.act(o[:, ts_idx, :, 3], o[:, ts_idx, :, 3], AF.Exp, scale=0.5,
              tag=f"{pref}_se{ts_idx}_{tt}")

    groups = [(8 * i, min(8 * i + 8, 63)) for i in range(8)]
    for oc in range(NC):
        for gi, (g0, g1) in enumerate(groups):
            buf = DG[:, (oc * 8 + gi) % 2]
            P.dma("gpsimd", buf[:, 0:g1 - g0, :],
                  dgd[oc, g0:g1].rearrange("t p o -> p t o"),
                  tag=f"{pref}_dg{oc}_{gi}")
            for tb in range(4):
                for j, tau in enumerate(range(g0, g1)):
                    sh = 62 - tau
                    base = 512 * tb - sh
                    if tb == 0:
                        if tau == 0:
                            P.mm(env.ps[0].ap(), ZB.ap(),
                                 vfm[oc][:, 0:512], start=True, stop=False,
                                 tag=f"{pref}_fl{oc}_z")
                        lo = max(0, -base)
                        P.mm(env.ps[0].ap()[:, lo:512], buf[:, j, :],
                             vfm[oc][:, base + lo:base + 512],
                             start=False, stop=(tau == 62),
                             tag=f"{pref}_fl{oc}_{tb}")
                    else:
                        P.mm(env.ps[tb].ap(), buf[:, j, :],
                             vfm[oc][:, base:base + 512],
                             start=(tau == 0), stop=(tau == 62),
                             tag=f"{pref}_fl{oc}_{tb}")
        for tb in range(4):
            P.copy(lgfm[:, 512 * tb:512 * (tb + 1)], env.ps[tb].ap(),
                   eng="scalar", tag=f"{pref}_fle{oc}_{tb}")
            fir(SH, sh_w.ap(), 3, oc, tb)
            shb = sqs[:, 0:512]
            P.copy(shb, SH[:, 0:512], tag=f"{pref}_fse{oc}_{tb}")
            for j in range(4):
                pst = env.psT[1]
                P.transpose(pst.ap()[:, 0:128], shb[:, 128 * j:128 * (j + 1)],
                            env.ident.ap(), tag=f"{pref}_fTs{oc}_{tb}")
                P.copy(sh_tile(4 * tb + j)[:, 128 * oc:128 * (oc + 1)],
                       pst.ap()[:, 0:128], tag=f"{pref}_fTse{oc}_{tb}")
        for tt in range(NTT):
            pst = env.psT[0]
            P.transpose(pst.ap()[:, 128 * (tt % 4):128 * (tt % 4) + 128],
                        lgfm[:, 128 * tt:128 * (tt + 1)],
                        env.ident.ap(), tag=f"{pref}_fTl{oc}_{tt}")
            stp = strips[:, 128 * (tt % 4):128 * (tt % 4) + 128]
            P.copy(stp, pst.ap()[:, 128 * (tt % 4):128 * (tt % 4) + 128],
                   tag=f"{pref}_fTle{oc}_{tt}")
            P.dma("scalar", ldram[tt][:, 128 * oc:128 * (oc + 1)], stp,
                  tag=f"{pref}_ldo{oc}_{tt}")
    for tt in range(NTT):
        lrb = extb[:, 2048 + 1024 * (tt % 2):3072 + 1024 * (tt % 2)]
        P.dma("scalar", lrb, ldram[tt], tag=f"{pref}_lst{tt}")
        dostats(P, env, sh_tile(tt), 0, tt)
        dostats(P, env, lrb, 1, tt)
        dostats(P, env, DTM2[:, tt, :], 2, tt)
        pst = env.psT[0]
        for c in range(NC):
            P.transpose(pst.ap()[:, 128 * c:128 * (c + 1)],
                        vfm[c][:, 128 * tt:128 * (tt + 1)],
                        env.ident.ap(), tag=f"{pref}_vT2{tt}")
        vtm2 = prv[:, 0:1024]
        P.copy(vtm2, pst.ap()[:, 0:1024], tag=f"{pref}_vT2e{tt}")
        dostats(P, env, vtm2, 3, tt)
        stb = prv[:, 1024:1088]
        P.copy(stb, stt_tm.ap(), tag=f"{pref}_stb{tt}")
        pst = env.psT[1]
        P.transpose(pst.ap()[0:64, 0:128], stb, env.ident.ap(),
                    tag=f"{pref}_stT{tt}")
        P.copy(statT[:, 128 * tt:128 * (tt + 1)],
               pst.ap()[0:64, 0:128], tag=f"{pref}_stTe{tt}")

    w1d = env.dram_in("L1_attn_b_W1", [1088, 2048])
    w2d = env.dram_in("L1_attn_b_W2", [2048, 16])
    b1d = env.dram_in("L1_attn_b_b1", [2048])
    b2d = env.dram_in("L1_attn_b_b2", [16])
    W1Ms = [w3f[:, 0:1024].rearrange("p (c o) -> p c o", c=NC),
            w3f[:, 1024:2048].rearrange("p (c o) -> p c o", c=NC)]
    P.dma("gpsimd", env.BROW.ap()[:, 128:144],
          b2d.rearrange("(o a) -> o a", o=1), tag=f"{pref}_b2")
    w1sb = env.W1.ap().rearrange("p c a b -> p (c a b)")
    xlf = env.xlnr.ap().rearrange("p a b -> p (a b)")
    w2bufs = [W2S.ap(), xlf[:, 1536:1552]]
    ones512 = env.RDB.ap()[0:1, :]
    P.memset(ones512, 1.0, tag=f"{pref}_o512")
    for hs in range(16):
        sl = slice(128 * hs, 128 * (hs + 1))
        W1M = W1Ms[hs % 2]
        P.dma("gpsimd", W1M, w1d[0:1024, sl].rearrange(
            "(c p) o -> p c o", p=128), tag=f"{pref}_w1{hs}")
        w1s = w1sb[0:64, 1024 * (hs % 2):1024 * (hs % 2) + 128]
        P.dma("gpsimd", w1s, w1d[1024:1088, sl], tag=f"{pref}_w1s{hs}")
        w2s = w2bufs[hs % 2]
        P.dma("gpsimd", w2s, w2d[sl, :], tag=f"{pref}_w2{hs}")
        b1row = env.BROW.ap()[:, 256 + 128 * (hs % 2):384 + 128 * (hs % 2)]
        P.dma("gpsimd", b1row, b1d[sl].rearrange("(o a) -> o a", o=1),
              tag=f"{pref}_b1{hs}")
        for tb in range(4):
            ps = env.ps[tb % 2]
            for c in range(NC):
                P.mm(ps.ap(), W1M[:, c, :],
                     T1[:, c, 512 * tb:512 * (tb + 1)],
                     start=(c == 0), stop=False, tag=f"{pref}_h1{hs}_{tb}")
            P.mm(ps.ap(), w1s, statT[:, 512 * tb:512 * (tb + 1)],
                 start=False, stop=False, tag=f"{pref}_h2{hs}_{tb}")
            P.mm(ps.ap(), b1row, ones512,
                 start=False, stop=True, tag=f"{pref}_h3{hs}_{tb}")
            hm = xlf[:, 512 * (tb % 2):512 * (tb % 2) + 512]
            P.act(hm, ps.ap(), AF.Gelu, tag=f"{pref}_ge{hs}_{tb}")
            for q4 in range(4):
                tt = 4 * tb + q4
                ps2 = env.ps[2 + tt % 2]
                P.mm(ps2.ap()[:, 0:16], hm[:, 128 * q4:128 * (q4 + 1)],
                     w2s, start=True, stop=(hs > 0),
                     tag=f"{pref}_l2m{hs}_{tt}")
                if hs == 0:
                    P.mm(ps2.ap()[:, 0:16], env.ones_row.ap(),
                         env.BROW.ap()[:, 128:144], start=False, stop=True,
                         tag=f"{pref}_l2b{hs}_{tt}")
                if hs == 0:
                    P.copy(LACC.ap()[:, tt, :], ps2.ap()[:, 0:16],
                           tag=f"{pref}_la{hs}_{tt}")
                else:
                    P.tt(LACC.ap()[:, tt, :], LACC.ap()[:, tt, :],
                         ps2.ap()[:, 0:16], OP.add, tag=f"{pref}_la{hs}_{tt}")
    for tt in range(NTT):
        z = env.stat.ap()[:, 32:48].rearrange("p (h s) -> p h s", h=4)
        la = LACC.ap()[:, tt, :].rearrange("p (h s) -> p h s", h=4)
        P.tt(z, la, RTB.ap()[:, :, None].broadcast_to((128, 4, 4)), OP.mult,
             tag=f"{pref}_gz{tt}")
        mx = env.stat.ap()[:, 48:52]
        P.reduce(mx, z, OP.max, tag=f"{pref}_gm{tt}")
        P.tt(z, z, mx[:, :, None].broadcast_to((128, 4, 4)), OP.subtract,
             tag=f"{pref}_gs{tt}")
        P.act(z.rearrange("p h s -> p (h s)"), z.rearrange("p h s -> p (h s)"),
              AF.Exp, tag=f"{pref}_gx{tt}")
        sm = env.stat.ap()[:, 52:56]
        P.reduce(sm, z, OP.add, tag=f"{pref}_gsum{tt}")
        P.recip(sm, sm, tag=f"{pref}_gr{tt}")
        P.tt(z, z, sm[:, :, None].broadcast_to((128, 4, 4)), OP.mult,
             tag=f"{pref}_gp{tt}")
        P.tt(z.rearrange("p h s -> p (h s)"),
             z.rearrange("p h s -> p (h s)"), FLB.ap(), OP.max,
             tag=f"{pref}_gf{tt}")
        P.reduce(sm, z, OP.add, tag=f"{pref}_gs2{tt}")
        P.recip(sm, sm, tag=f"{pref}_gr2{tt}")
        P.tt(z, z, sm[:, :, None].broadcast_to((128, 4, 4)), OP.mult,
             tag=f"{pref}_gp2{tt}")
        P.copy(PCO.ap()[:, tt, :], z.rearrange("p h s -> p (h s)"),
               tag=f"{pref}_gpc{tt}")

    P.dma("gpsimd", WB, env.dram_in("L1_attn_b_Wo", [D, D]).rearrange(
        "(c p) o -> p c o", p=128), tag=f"{pref}_Wo")
    youtd = env.dram_tmp("youtd", [NTT, 128, D])
    DTM3 = env.T2.ap().rearrange("p a b -> p (a b)")[:, 0:16384].rearrange(
        "p (t d) -> p t d", d=1024)
    oacc = env.GB.ap().bitcast(F32)
    otmp = env.BB.ap().bitcast(F32)
    obf = env.xlnr.ap().rearrange("p a b -> p (a b)")
    stage = env.lnf32.ap()
    pend = None
    for tt in range(NTT):
        llt = ext[:, 1024 * (tt % 2):1024 * (tt % 2) + 1024]
        P.dma("scalar", llt, ldram[tt], tag=f"{pref}_lri{tt}")
        pst = env.psT[0]
        for c in range(NC):
            P.transpose(pst.ap()[:, 128 * c:128 * (c + 1)],
                        vfm[c][:, 128 * tt:128 * (tt + 1)],
                        env.ident.ap(), tag=f"{pref}_vT3{tt}")
        vtm3 = prv[:, 0:1024]
        P.copy(vtm3, pst.ap()[:, 0:1024], tag=f"{pref}_vT3e{tt}")
        if pend is not None:
            pend()
        xr = env.XR[0].ap()
        P.dma("scalar", xr, xsp[tt], tag=f"{pref}_rl{tt}")
        pc = PCO.ap()[:, tt, :].rearrange("p (h s) -> p h s", h=4)
        ob = obf[:, 0:1024]
        ssq = env.stat.ap()[:, 32:36]
        for hf in range(2):
            csl = slice(512 * hf, 512 * (hf + 1))
            hsl = slice(2 * hf, 2 * hf + 2)
            srcs = [(sh_tile(tt), 0), (llt, 1), (DTM3[:, tt, :], 2),
                    (vtm3, 3)]
            for src, si in srcs:
                g = src[:, csl].rearrange("p (h d) -> p h d", h=2)
                pcb = pc[:, hsl, si:si + 1].broadcast_to((128, 2, 256))
                if si == 0:
                    P.tt(oacc.rearrange("p (h d) -> p h d", h=2), g, pcb,
                         OP.mult, tag=f"{pref}_cb{si}_{tt}")
                else:
                    P.tt(otmp.rearrange("p (h d) -> p h d", h=2), g, pcb,
                         OP.mult, tag=f"{pref}_cm{si}_{tt}")
                    P.tt(oacc, oacc, otmp, OP.add, tag=f"{pref}_ca{si}_{tt}")
            P.act(otmp, oacc, AF.Square, tag=f"{pref}_rsq{tt}")
            P.reduce(ssq[:, 0:2], otmp.rearrange("p (h d) -> p h d", h=2),
                     OP.add, tag=f"{pref}_rss{tt}")
            P.ts(ssq[:, 0:2], ssq[:, 0:2], 1.0 / 256, None, OP.mult,
                 tag=f"{pref}_rsm{tt}")
            P.act(ssq[:, 0:2], ssq[:, 0:2], AF.Ln, bias=env.constf(LN_EPS),
                  tag=f"{pref}_rsl{tt}")
            P.act(ssq[:, 0:2], ssq[:, 0:2], AF.Exp, scale=-0.5,
                  tag=f"{pref}_rse{tt}")
            P.tt(oacc.rearrange("p (h d) -> p h d", h=2),
                 oacc.rearrange("p (h d) -> p h d", h=2),
                 ssq[:, 0:2][:, :, None].broadcast_to((128, 2, 256)), OP.mult,
                 tag=f"{pref}_rsn{tt}")
            P.tt(ob[:, csl], oacc, NWB[:, csl], OP.mult,
                 tag=f"{pref}_rnw{tt}")
        def pend(tt=tt, ob=ob, xr=xr):
            pst = env.psT[1]
            for c in range(NC):
                P.transpose(pst.ap()[:, 128 * c:128 * (c + 1)],
                            ob[:, 128 * c:128 * (c + 1)], env.ident.ap(),
                            tag=f"{pref}_oT{tt}")
            ofm = ext[:, 2048:3072]
            P.copy(ofm, pst.ap()[:, 0:1024], tag=f"{pref}_oTe{tt}")
            for oh in range(2):
                ps = env.ps[4 + oh]
                for c in range(NC):
                    P.mm(ps.ap(),
                         ofm.rearrange("p (c t) -> p c t", c=NC)[:, c, :],
                         WB[:, c, 512 * oh:512 * (oh + 1)],
                         start=(c == 0), stop=(c == NC - 1),
                         tag=f"{pref}_om{tt}")
                P.tt(stage[:, 512 * oh:512 * (oh + 1)], ps.ap(),
                     xr[:, 512 * oh:512 * (oh + 1)],
                     OP.add, tag=f"{pref}_oadd{tt}")
            P.dma("sync", youtd[tt], stage, tag=f"{pref}_yo{tt}")
    pend()
    for tt in range(NTT):
        P.dma("sync", env.X[tt].ap(), youtd[tt], tag=f"{pref}_xr{tt}")


def _register5():
    EMITTERS["l1_attn_b"] = lambda P, env, seq: emit_l1attnb(P, env, seq,
                                                             f"B{seq}")


_register5()
